# version 21
# baseline (speedup 1.0000x reference)
import numpy as np
import concourse.bass as bass
import concourse.mybir as mybir
from concourse.bass_utils import run_bass_kernel_spmd
from concourse.bass_core_rust import IndirectOffsetOnAxis
from contextlib import ExitStack

F32 = mybir.dt.float32
BF16 = mybir.dt.bfloat16
I32 = mybir.dt.int32
U32 = mybir.dt.uint32
AF = mybir.ActivationFunctionType
ALU = mybir.AluOpType
AX = mybir.AxisListType

ENGINES = ("tensor", "vector", "scalar", "gpsimd", "sync")
LN20 = float(np.log(20.0))



def _ap_rect(ap):
    dims = [(int(s), int(c)) for s, c in ap.ap]
    off = int(ap.offset)
    esz = np.dtype(mybir.dt.np(ap.dtype)).itemsize
    space = str(ap.space)
    if "DRAM" in space:
        ext = sum(abs(s) * (c - 1) for s, c in dims)
        return (0, 128, off * esz, (off + ext + 1) * esz)
    p0 = ap.base_partition()
    pn = dims[0][1] if dims else 1
    pstep = dims[0][0] if dims else 0
    foff = off - p0 * pstep
    ext = sum(abs(s) * (c - 1) for s, c in dims[1:])
    return (p0, p0 + pn, foff * esz, (foff + ext + 1) * esz)


def _overlap(r1, r2):
    return (r1[0] < r2[1] and r2[0] < r1[1] and
            r1[2] < r2[3] and r2[2] < r1[3])


def _covers(rn, ro):
    return (rn[0] <= ro[0] and rn[1] >= ro[1] and
            rn[2] <= ro[2] and rn[3] >= ro[3])


class Op:
    __slots__ = ("idx", "eng", "emit", "compl", "marked", "sem_val", "tag",
                 "dom", "prewait")

    def __init__(self, idx, eng, emit, compl, tag):
        self.idx, self.eng, self.emit = idx, eng, emit
        self.compl = compl
        self.marked = False
        self.sem_val = None
        self.tag = tag
        self.dom = None
        self.prewait = None


class Prog:
    NLANE = 12

    def __init__(self, nc):
        self.nc = nc
        self.ops = []
        self.hstate = {}
        self.edges = []
        self.lane_ctr = {}
        self.lane_use = {}

    def op(self, eng, emit, reads=(), writes=(), compl=None, tag=""):
        idx = len(self.ops)
        o = Op(idx, eng, emit, compl or ("eng", eng), tag)
        if o.compl[0] == "dma":
            q = o.compl[1]
            k = self.lane_ctr.get(q, 0)
            self.lane_ctr[q] = k + 1
            dom = ("dma", q, k % self.NLANE)
            o.dom = dom
            u = self.lane_use.get(dom, 0) + 1
            self.lane_use[dom] = u
            o.sem_val = 16 * u
            if u > 1:
                o.prewait = (dom, 16 * (u - 1))
        self.ops.append(o)
        deps = set()
        rrects = [(a.tensor.name, _ap_rect(a)) for a in reads]
        wrects = [(a.tensor.name, _ap_rect(a)) for a in writes]
        for key, rect in rrects:
            stt = self.hstate.get(key)
            if stt:
                for widx, wrect in stt["w"]:
                    if _overlap(rect, wrect):
                        deps.add(widx)
        for key, rect in wrects:
            stt = self.hstate.get(key)
            if stt:
                for widx, wrect in stt["w"]:
                    if _overlap(rect, wrect):
                        deps.add(widx)
                for ridx, rrect in stt["r"]:
                    if _overlap(rect, rrect):
                        deps.add(ridx)
        deps.discard(idx)
        self.edges.append(deps)
        for key, rect in wrects:
            stt = self.hstate.setdefault(key, {"w": [], "r": []})
            stt["w"] = [(w, r) for (w, r) in stt["w"]
                        if not _covers(rect, r)] + [(idx, rect)]
            stt["r"] = [(w, r) for (w, r) in stt["r"] if not _covers(rect, r)]
        for key, rect in rrects:
            stt = self.hstate.setdefault(key, {"w": [], "r": []})
            stt["r"].append((idx, rect))
            if len(stt["r"]) > 768:
                stt["r"] = stt["r"][-768:]
        return o

    def barrier(self):
        tails = {}
        for o in self.ops:
            tails[self._dom(o)] = o.idx
        tl = set(tails.values())
        for eng in ENGINES:
            o = self.op(eng, lambda e: e.nop(), tag="barrier")
            self.edges[o.idx] = set(tl)
        bidx = [o.idx for o in self.ops[-5:]]
        for key, stt in self.hstate.items():
            stt["w"] = [(bidx[0], (0, 128, 0, 1 << 40))]
            stt["r"] = []

    def _dom(self, o):
        if o.dom is not None:
            return o.dom
        return o.compl if o.compl[0] != "eng" else ("eng", o.eng)

    def emit(self):
        nc = self.nc
        for deps in self.edges:
            for d in deps:
                self.ops[d].marked = True
        for o in self.ops:
            if o.compl[0] in ("dma", "cc"):
                o.marked = True
        lane_use = self.lane_use
        dom_counts = {}
        doms = {}
        for o in self.ops:
            dom = self._dom(o)
            doms.setdefault(dom, []).append(o)
            if o.marked and o.sem_val is None:
                inc = 1
                dom_counts[dom] = dom_counts.get(dom, 0) + inc
                o.sem_val = dom_counts[dom]
                if dom[0] == "cc":
                    o.sem_val = 1
        for dom, u in lane_use.items():
            dom_counts[dom] = 16 * u
        self.sems = {}
        self._semctx = ExitStack()
        for dom in doms:
            self.sems[dom] = self._semctx.enter_context(
                nc.semaphore('s_' + '_'.join(str(x) for x in dom)))
        eng_clock = {e: {} for e in ENGINES}
        op_clock = {}
        waits_for = {}
        for o in self.ops:
            cl = dict(eng_clock[o.eng])
            w = []
            for d in sorted(self.edges[o.idx]):
                p = self.ops[d]
                pdom = self._dom(p)
                assert p.marked
                need = p.sem_val
                if cl.get(pdom, -1) >= need:
                    continue
                w.append((pdom, need))
                cl[pdom] = need
                for kd, kv in op_clock.get(d, {}).items():
                    if cl.get(kd, -1) < kv:
                        cl[kd] = kv
            waits_for[o.idx] = w
            op_clock[o.idx] = cl
            eng_clock[o.eng] = cl
        with nc.Block() as block:
            for eng_name in ENGINES:
                stream = [o for o in self.ops if o.eng == eng_name]

                def make_body(stream, eng_name):
                    def body(eng):
                        for o in stream:
                            if o.prewait is not None:
                                eng.wait_ge(self.sems[o.prewait[0]],
                                            o.prewait[1])
                            for dom, val in waits_for[o.idx]:
                                eng.wait_ge(self.sems[dom], val)
                            try:
                                inst = o.emit(eng)
                            except Exception as ex:
                                raise RuntimeError(
                                    f"emit failed op#{o.idx} tag={o.tag} "
                                    f"eng={o.eng}") from ex
                            if o.marked:
                                dom = self._dom(o)
                                inst.then_inc(self.sems[dom],
                                              16 if dom[0] == "dma" else 1)
                        if eng_name == "sync":
                            for dom, cnt in dom_counts.items():
                                if dom[0] == "dma":
                                    eng.wait_ge(self.sems[dom], cnt)
                                elif dom[0] == "cc":
                                    eng.wait_ge(self.sems[dom], 1)
                    return body

                getattr(block, eng_name)(make_body(stream, eng_name))



class CFG:
    def __init__(self, L=4096, E=8, CAP=3072):
        self.L, self.DM, self.DI, self.NST = L, 1024, 2048, 16
        self.E, self.H, self.CAP = E, 4096, CAP
        self.NDT = self.DI // 128
        self.NDM = self.DM // 128
        self.NH = self.H // 128
        R = 10
        base = 26 * R
        chunks = [(0, min(L, base + 1))]
        t = chunks[0][1]
        while t < L:
            n = min(base, L - t)
            chunks.append((t, n))
            t += n
        self.chunks = chunks
        segs = [(0, 1, True)]
        t = 1
        while t < L:
            n = min(R, L - t)
            segs.append((t, n, ((t + n - 1) % R) == 0))
            t += n
        self.segs = segs
        self.NTILE = L // 128



def build_program(cfg, n_cores=8):
    nc = bass.Bass("TRN2", target_bir_lowering=False)
    P = Prog(nc)
    L, DM, DI, NST = cfg.L, cfg.DM, cfg.DI, cfg.NST
    E, H, CAP = cfg.E, cfg.H, cfg.CAP
    NDT, NDM, NH, NTILE = cfg.NDT, cfg.NDM, cfg.NH, cfg.NTILE
    MAXC = max(n for _, n in cfg.chunks)
    R11 = 11

    def din(name, shape, dtype=F32):
        return nc.dram_tensor(name, shape, dtype, kind="ExternalInput").ap()

    x_t = din("x_t", [DM, L])
    norm_w_t = din("norm_w_t", [128, NDM])
    w_inT = din("w_inT", [DM, 2 * DI])
    conv_w_t = din("conv_w_t", [128, NDT, 4])
    conv_b_t = din("conv_b_t", [128, NDT])
    w_dtT = din("w_dtT", [DI, DI])
    b_dt_t = din("b_dt_t", [128, NDT])
    w_BT = din("w_BT", [DI, NST])
    w_CT = din("w_CT", [DI, NST])
    A_log_t = din("A_log_t", [NDT, 128, NST])
    D_t = din("D_t", [128, NDT])
    w_outT = din("w_outT", [DI, DM])
    res_scale = din("res_scale", [1, 1])
    w_router_i = din("w_router", [DM, E])
    w1_my = din("w1", [DM, H])
    b1_t = din("b1_t", [128, NH])
    w2_my = din("w2", [H, DM])
    b2_t = din("b2_t", [128, NDM])
    rank1hot = din("rank1hot", [8, 1])
    prefmask = din("prefmask", [8, 1])

    out = nc.dram_tensor("out", [L, DM], F32, kind="ExternalOutput").ap()

    def dscr(name, shape, dtype=F32):
        return nc.dram_tensor(name, shape, dtype, kind="Internal").ap()

    winT_bf = dscr("winT_bf", [2 * NDT, 128, NDM * 128], BF16)
    wdtT_bf = dscr("wdtT_bf", [NDT, 128, NDT * 128], BF16)
    woutT_bf = dscr("woutT_bf", [NDM, 128, NDT * 128], BF16)
    w1_bf = dscr("w1_bf", [NH, 128, NDM * 128], BF16)
    w2_bf = dscr("w2_bf", [NDM, 128, NH * 128], BF16)
    brow_d = dscr("brow_d", [NST, MAXC])
    crow_d = dscr("crow_d", [NST, MAXC])
    xf_d = dscr("xf_d", [L, DM])
    logit_d = dscr("logit_d", [E, L])
    slot_d = dscr("slot_d", [4, 2 * L])
    slab_d = dscr("slab_d", [E * CAP, DM], BF16)
    shard_d = dscr("shard_d", [CAP, DM], BF16)
    eout_my = dscr("eout_my", [CAP, DM])
    eout_d = dscr("eout_d", [E * CAP, DM])
    cc_src = dscr("cc_src", [8, 8])
    cc_dst = dscr("cc_dst", [8, 8])
    meta_d = dscr("meta_d", [1, 64])

    def vop(emit, reads, writes, tag=""):
        return P.op("vector", emit, reads, writes, tag=tag)

    def sop(emit, reads, writes, tag=""):
        return P.op("scalar", emit, list(reads) + [const_pool[:]], writes,
                    tag=tag)

    def top(emit, reads, writes, tag=""):
        return P.op("tensor", emit, reads, writes, tag=tag)

    def gop(emit, reads, writes, tag=""):
        return P.op("gpsimd", emit, reads, writes, tag=tag)

    def dma(queue, dst, src, tag=""):
        return P.op(queue, lambda e, d=dst, s=src: e.dma_start(d, s),
                    [src], [dst], compl=("dma", queue), tag=tag)

    def dma4(dst, src, w, tag=""):
        q = w // 4
        for i in range(4):
            dma(("sync", "scalar")[i % 2], dst[:, i * q:(i + 1) * q],
                src[:, i * q:(i + 1) * q], tag)

    def idma_scatter(dst, idx_ap, src, bound, tag=""):
        return P.op("gpsimd",
                    lambda e: e.indirect_dma_start(
                        dst, IndirectOffsetOnAxis(idx_ap, 0), src, None,
                        bounds_check=bound, oob_is_err=False),
                    [src, idx_ap], [dst], compl=("dma", "gpsimd"), tag=tag)

    def idma_gather(dst, src, idx_ap, tag=""):
        return P.op("gpsimd",
                    lambda e: e.indirect_dma_start(
                        dst, None, src, IndirectOffsetOnAxis(idx_ap, 0)),
                    [src, idx_ap], [dst], compl=("dma", "gpsimd"), tag=tag)

    persist = ExitStack()
    SBp = lambda name, shape, dtype=F32: persist.enter_context(
        nc.sbuf_tensor(name, shape, dtype))
    PSp = lambda name, shape, dtype=F32: persist.enter_context(
        nc.psum_tensor(name, shape, dtype))

    ones_f = SBp("ones_f", [128, 128])
    ident_bf = SBp("ident_bf", [128, 128], BF16)
    ident_f = SBp("ident_f", [128, 128])
    normw_c = SBp("normw_c", [128, NDM])
    convw_c = SBp("convw_c", [128, NDT, 4])
    convb_c = SBp("convb_c", [128, NDT])
    bdt_c = SBp("bdt_c", [128, NDT])
    D_c = SBp("D_c", [128, NDT])
    A16_c = SBp("A16_c", [128, NDT, NST])
    sscale = SBp("sscale", [128, 1])
    r1hot = SBp("r1hot", [8, 1])
    pmask = SBp("pmask", [8, 1])
    wbt_s = SBp("wbt_s", [128, NDT, NST], BF16)
    wct_s = SBp("wct_s", [128, NDT, NST], BF16)
    wrt_s = SBp("wrt_s", [128, NDM, E])
    b1c = SBp("b1c", [128, NH])
    b2c = SBp("b2c", [128, NDM])
    g_t = SBp("g_t", [128, NTILE, 2])
    io128 = SBp("io128", [128, 128], I32)
    pidx = SBp("pidx", [128, 1], I32)
    pidxf = SBp("pidxf", [128, 1])
    zrow = SBp("zrow", [128, DM], BF16)

    CONST_VALS = [0.0, 1.0, 1e-6, 1e-5, 1e-30, LN20, -0.0505 / 0.099]
    const_pool = SBp("const_pool", [128, len(CONST_VALS)])
    for i, v in enumerate(CONST_VALS):
        nc.const_aps.register(F32, v, const_pool[:, i:i + 1])

    pMM = PSp("pMM", [128, 512])
    pMM2 = PSp("pMM2", [128, 512])
    pMM3 = PSp("pMM3", [128, 512])
    pStat = PSp("pStat", [1, 512])
    pStat2 = PSp("pStat2", [1, 512])
    pBC = PSp("pBC", [128, 512])
    pBC2 = PSp("pBC2", [128, 512])
    pT = PSp("pT", [128, 128])

    for i, v in enumerate(CONST_VALS):
        gop(lambda e, i=i, v=v: e.memset(const_pool[:, i:i + 1], v),
            [], [const_pool[:, i:i + 1]], "cpool")
    gop(lambda e: e.iota(io128[:], [[1, 128]], channel_multiplier=0),
        [], [io128[:]], "iota128")
    gop(lambda e: e.iota(pidx[:], [[1, 1]], channel_multiplier=1),
        [], [pidx[:]], "iotap")
    gop(lambda e: e.memset(ones_f[:], 1.0), [], [ones_f[:]], "ones")
    gop(lambda e: e.memset(zrow[:], 0.0), [], [zrow[:]], "zrow")
    vop(lambda e: e.tensor_copy(ident_f[:], io128[:]), [io128[:]], [ident_f[:]],
        "io_f")
    vop(lambda e: e.tensor_copy(pidxf[:], pidx[:]), [pidx[:]], [pidxf[:]],
        "pi_f")
    vop(lambda e: e.tensor_scalar(ident_f[:], ident_f[:], pidxf[:, 0:1], None,
                                  ALU.is_equal),
        [ident_f[:], pidxf[:]], [ident_f[:]], "ident")
    vop(lambda e: e.tensor_copy(ident_bf[:], ident_f[:]),
        [ident_f[:]], [ident_bf[:]], "identb")

    dma("sync", normw_c[:], norm_w_t)
    dma("sync", convw_c[:], conv_w_t)
    dma("sync", convb_c[:], conv_b_t)
    dma("sync", bdt_c[:], b_dt_t)
    dma("sync", D_c[:], D_t)
    dma("sync", r1hot[:], rank1hot)
    dma("sync", pmask[:], prefmask)
    dma("sync", b1c[:], b1_t)
    dma("sync", b2c[:], b2_t)
    dma("sync", wrt_s[:], w_router_i.rearrange("(c p) e -> p c e", p=128))

    mam = ExitStack()
    SB = lambda name, shape, dtype=F32: mam.enter_context(
        nc.sbuf_tensor(name, shape, dtype))

    atmp = SB("atmp", [128, NDT * NST])
    dma("sync", atmp[:].rearrange("p (a b) -> p a b", b=NST),
        A_log_t.rearrange("a p b -> p a b"))
    sop(lambda e: e.activation(atmp[:], atmp[:], AF.Tanh, scale=0.1),
        [atmp[:]], [atmp[:]], "A_tanh")
    sop(lambda e: e.activation(atmp[:], atmp[:], AF.Exp, scale=10.0),
        [atmp[:]], [atmp[:]], "A_exp")
    vop(lambda e: e.tensor_scalar(A16_c[:].rearrange("p a b -> p (a b)"),
                                  atmp[:], -1.0 / 16.0, None, ALU.mult),
        [atmp[:]], [A16_c[:]], "A_sc")

    stmp = SB("stmp", [1, 1])
    dma("sync", stmp[:], res_scale)
    sop(lambda e: e.activation(stmp[:], stmp[:], AF.Sigmoid),
        [stmp[:]], [stmp[:]], "sres")
    top(lambda e: e.matmul(pBC[:, 0:1], ones_f[0:1, :], stmp[:],
                           start=True, stop=True),
        [ones_f[:], stmp[:]], [pBC[:, 0:1]], "sres_bc")
    vop(lambda e: e.tensor_scalar(sscale[:], pBC[:, 0:1], 0.5, None, ALU.mult),
        [pBC[:, 0:1]], [sscale[:]], "sres_h")

    btmp = SB("btmp", [128, NDT * NST])
    for src, dst_s in ((w_BT, wbt_s), (w_CT, wct_s)):
        dma("sync", btmp[:].rearrange("p (a b) -> p a b", b=NST),
            src.rearrange("(a p) b -> p a b", p=128))
        vop(lambda e, dd=dst_s: e.tensor_copy(
            dd[:].rearrange("p a b -> p (a b)"), btmp[:]),
            [btmp[:]], [dst_s[:]], "wbc_bf")

    cast_st = SB("cast_st", [128, 1024])
    cast_ot = SB("cast_ot", [128, 1024], BF16)

    def cast_weight(src_rows, dst_slabs, n_c, n_m, queue, ceng, tag):
        srcv = src_rows.rearrange("(c p) (m f) -> m p c f", p=128, f=128)
        CB = 8
        for m in range(n_m):
            for c0 in range(0, n_c, CB):
                c1 = min(n_c, c0 + CB)
                ww = (c1 - c0) * 128
                stg = cast_st[:, 0:ww]
                outg = cast_ot[:, 0:ww]
                dma(queue, stg, srcv[m:m + 1, :, c0:c1, :], tag)
                ceng(lambda e, s=stg, o=outg: e.tensor_copy(o, s),
                     [stg], [outg], tag + "c")
                dma(queue, dst_slabs[m, :, c0 * 128:c1 * 128], outg,
                    tag + "o")

    cast_weight(w_inT, winT_bf, NDM, 2 * NDT, "sync", vop, "win")
    cast_weight(w_dtT, wdtT_bf, NDT, NDT, "sync", vop, "wdt")
    cast_weight(w_outT, woutT_bf, NDT, NDM, "sync", vop, "wou")

    xb_s = SB("xb_s", [128, NDM, MAXC])
    xnxc_bf = SB("xnxc_bf", [128, NDT, MAXC], BF16)
    xc_s = SB("xc_s", [128, NDT, MAXC])
    dt_s = SB("dt_s", [128, NDT, MAXC])
    dtx_s = SB("dtx_s", [128, NDT, MAXC])
    gsilu_bf = SB("gsilu_bf", [128, NDT, MAXC], BF16)
    bbc_s = SB("bbc_s", [128, NST, MAXC])
    cbc_s = SB("cbc_s", [128, NST, MAXC])
    SEGW = NDT * NST * R11
    a_buf = SB("a_buf", [128, 2 * SEGW])
    u_buf = SB("u_buf", [128, SEGW])
    h_buf = SB("h_buf", [128, SEGW])
    ybf_s = SB("ybf_s", [128, NDT, MAXC], BF16)
    conv_sc = SB("conv_sc", [128, MAXC + 3])
    acc0_sc = SB("acc0_sc", [128, MAXC])
    acc1_sc = SB("acc1_sc", [128, MAXC])
    sq_sc = SB("sq_sc", [128, MAXC])
    ysc_s = SB("ysc_s", [128, NST * NDT * 10])
    xtail_s = SB("xtail_s", [128, NDT, 3])
    wm_st = SB("wm_st", [128, 3, NDT * 128], BF16)
    brow_s = SB("brow_s", [NST, MAXC])
    crow_s = SB("crow_s", [NST, MAXC])
    lrow_s = SB("lrow_s", [E, MAXC])
    row_sc = SB("row_sc", [1, MAXC])
    row_sc2 = SB("row_sc2", [1, MAXC])
    sqcol = SB("sqcol", [128, 1])
    nsc_s = SB("nsc_s", [128, 1])
    bscr2 = SB("bscr2", [128, MAXC])
    fac_s = SB("fac_s", [128, 1])
    xfrow_st = SB("xfrow_st", [128, 128])

    MAGIC = 1597463007

    def newton_rsqrt(buf, scr, np_, w, tag):
        bi = buf.bitcast(I32)
        si = scr.bitcast(I32)
        t2 = bscr2[0:np_, 0:w]
        vop(lambda e: e.tensor_scalar(si, bi, 1, None,
                                      ALU.arith_shift_right),
            [buf], [scr], tag + "_sh")
        vop(lambda e: e.tensor_scalar(si, si, -1, MAGIC, ALU.mult, ALU.add),
            [scr], [scr], tag + "_sd")
        for it in range(2):
            vop(lambda e: e.scalar_tensor_tensor(t2, scr, 1.0,
                                                 scr, ALU.bypass, ALU.mult),
                [scr], [t2], tag + "_y2")
            vop(lambda e: e.tensor_tensor(t2, t2, buf, ALU.mult),
                [t2, buf], [t2], tag + "_sy")
            vop(lambda e: e.tensor_scalar(t2, t2, -0.5, 1.5,
                                          ALU.mult, ALU.add),
                [t2], [t2], tag + "_ht")
            vop(lambda e: e.tensor_tensor(scr, scr, t2, ALU.mult),
                [scr, t2], [scr], tag + "_yn")
        vop(lambda e: e.tensor_copy(buf, scr), [scr], [buf], tag + "_cp")

    vop(lambda e: e.memset(u_buf[:], 0.0), [], [u_buf[:]], "zc")
    vop(lambda e: e.memset(ysc_s[:], 0.0), [], [ysc_s[:]], "zy")
    vop(lambda e: e.memset(h_buf[:], 0.0), [], [h_buf[:]], "zc2")
    vop(lambda e: e.memset(a_buf[:], 0.0), [], [a_buf[:]], "za")

    for r0 in range(0, E * CAP, 128):
        dma("gpsimd", slab_d[r0:r0 + 128, :], zrow[:], "zslab")

    seg_ptr = [0]
    seg_par = [0]

    def emit_chunk(ci, t0, tn):
        for d in range(NDM):
            dma("sync", xb_s[:, d, 0:tn],
                x_t[d * 128:(d + 1) * 128, t0:t0 + tn], "x_in")
        for d in range(NDM):
            sop(lambda e, d=d: e.activation(sq_sc[:, 0:tn], xb_s[:, d, 0:tn],
                                            AF.Square),
                [xb_s[:, d, 0:tn]], [sq_sc[:, 0:tn]], "rms_sq")
            top(lambda e, d=d: e.matmul(pStat[:, 0:tn], ones_f[:, 0:1],
                                        sq_sc[:, 0:tn], start=(d == 0),
                                        stop=(d == NDM - 1)),
                [ones_f[:], sq_sc[:, 0:tn]], [pStat[:, 0:tn]], "rms_mm")
        vop(lambda e: e.tensor_scalar(row_sc[:, 0:tn], pStat[0:1, 0:tn],
                                      1.0 / DM, 1e-6, ALU.mult, ALU.add),
            [pStat[:, 0:tn]], [row_sc[:, 0:tn]], "rms_v")
        newton_rsqrt(row_sc[0:1, 0:tn], row_sc2[0:1, 0:tn], 1, tn, "rms_n")
        top(lambda e: e.matmul(pBC[:, 0:tn], ones_f[0:1, :], row_sc[:, 0:tn],
                               start=True, stop=True),
            [ones_f[:], row_sc[:, 0:tn]], [pBC[:, 0:tn]], "rms_bc")
        for d in range(NDM):
            vop(lambda e, d=d: e.scalar_tensor_tensor(
                xnxc_bf[:, d, 0:tn], xb_s[:, d, 0:tn], normw_c[:, d:d + 1],
                pBC[:, 0:tn], ALU.mult, ALU.mult),
                [xb_s[:, d, 0:tn], normw_c[:], pBC[:, 0:tn]],
                [xnxc_bf[:, d, 0:tn]], "xn")

        for mi in range(2 * NDT):
            m = mi + NDT if mi < NDT else mi - NDT
            par = mi % 3
            wslab = wm_st[:, par, 0:NDM * 128]
            dma4(wslab, winT_bf[m, :, :], NDM * 128, "wl")
            pj = (pMM, pMM2, pMM3)[m % 3]
            for c in range(NDM):
                top(lambda e, c=c, pj=pj, ws=wslab: e.matmul(
                    pj[:, 0:tn], ws[:, c * 128:(c + 1) * 128],
                    xnxc_bf[:, c, 0:tn], start=(c == 0), stop=(c == NDM - 1)),
                    [wslab, xnxc_bf[:, c, 0:tn]], [pj[:, 0:tn]], "pj_mm")
            if m >= NDT:
                g = m - NDT
                sop(lambda e, g=g, pj=pj: e.activation(
                    gsilu_bf[:, g, 0:tn], pj[:, 0:tn], AF.Silu),
                    [pj[:, 0:tn]], [gsilu_bf[:, g, 0:tn]], "gsilu")
            else:
                d = m
                sop(lambda e, pj=pj: e.activation(conv_sc[:, 3:3 + tn],
                                                  pj[:, 0:tn], AF.Copy),
                    [pj[:, 0:tn]], [conv_sc[:, 3:3 + tn]], "cv_cp")
                if ci == 0:
                    vop(lambda e: e.memset(conv_sc[:, 0:3], 0.0),
                        [], [conv_sc[:, 0:3]], "cv_z")
                else:
                    vop(lambda e, d=d: e.tensor_copy(conv_sc[:, 0:3],
                                                     xtail_s[:, d, :]),
                        [xtail_s[:, d, :]], [conv_sc[:, 0:3]], "cv_t")
                vop(lambda e, d=d: e.tensor_copy(xtail_s[:, d, :],
                                                 conv_sc[:, tn:tn + 3]),
                    [conv_sc[:, 0:tn + 3]], [xtail_s[:, d, :]], "cv_st")
                vop(lambda e, d=d: e.tensor_scalar(
                    acc0_sc[:, 0:tn], conv_sc[:, 0:tn], convw_c[:, d, 0:1],
                    convb_c[:, d:d + 1], ALU.mult, ALU.add),
                    [conv_sc[:, 0:tn + 3], convw_c[:], convb_c[:]],
                    [acc0_sc[:, 0:tn]], "cv0")
                lastacc = acc0_sc
                for k in range(1, 4):
                    src, dst = ((acc0_sc, acc1_sc) if k % 2 else
                                (acc1_sc, acc0_sc))
                    vop(lambda e, d=d, k=k, s=src, ds=dst:
                        e.scalar_tensor_tensor(
                            ds[:, 0:tn], conv_sc[:, k:k + tn],
                            convw_c[:, d, k:k + 1], s[:, 0:tn],
                            ALU.mult, ALU.add),
                        [conv_sc[:, 0:tn + 3], convw_c[:], src[:, 0:tn]],
                        [dst[:, 0:tn]], "cvk")
                    lastacc = dst
                sop(lambda e, d=d, la=lastacc: e.activation(
                    xc_s[:, d, 0:tn], la[:, 0:tn], AF.Silu),
                    [lastacc[:, 0:tn]], [xc_s[:, d, 0:tn]], "xc_silu")

        for d in range(NDT):
            vop(lambda e, d=d: e.tensor_copy(xnxc_bf[:, d, 0:tn],
                                             xc_s[:, d, 0:tn]),
                [xc_s[:, d, 0:tn]], [xnxc_bf[:, d, 0:tn]], "xcbf")

        for wt, pp, rows, rowd, bc in ((wbt_s, pBC, brow_s, brow_d, bbc_s),
                                       (wct_s, pBC2, crow_s, crow_d, cbc_s)):
            for c in range(NDT):
                top(lambda e, c=c, wt=wt, pp=pp: e.matmul(
                    pp[0:NST, 0:tn], wt[:, c, :], xnxc_bf[:, c, 0:tn],
                    start=(c == 0), stop=(c == NDT - 1)),
                    [wt[:], xnxc_bf[:, c, 0:tn]], [pp[:, 0:tn]], "bc_mm")
            sop(lambda e, pp=pp, rows=rows: e.activation(
                rows[:, 0:tn], pp[0:NST, 0:tn], AF.Copy),
                [pp[:, 0:tn]], [rows[:, 0:tn]], "bc_ev")
            dma("scalar", rowd[:, 0:tn], rows[:, 0:tn], "bc_o")
            dma("scalar", bc[:, :, 0:tn],
                rowd[:, 0:tn].unsqueeze(0).broadcast_to([128, NST, tn]),
                "bc_r")

        for m in range(NDT):
            par = m % 3
            wslab = wm_st[:, par, 0:NDT * 128]
            dma4(wslab, wdtT_bf[m, :, :], NDT * 128, "wl2")
            pj = (pMM, pMM2, pMM3)[m % 3]
            for c in range(NDT):
                top(lambda e, c=c, pj=pj, ws=wslab: e.matmul(
                    pj[:, 0:tn], ws[:, c * 128:(c + 1) * 128],
                    xnxc_bf[:, c, 0:tn], start=(c == 0), stop=(c == NDT - 1)),
                    [wslab, xnxc_bf[:, c, 0:tn]], [pj[:, 0:tn]], "dt_mm")
            sop(lambda e, m=m, pj=pj: e.activation(
                dt_s[:, m, 0:tn], pj[:, 0:tn], AF.Exp,
                bias=bdt_c[:, m:m + 1]),
                [pj[:, 0:tn], bdt_c[:]], [dt_s[:, m, 0:tn]], "sp_e")
            sop(lambda e, m=m: e.activation(dt_s[:, m, 0:tn],
                                            dt_s[:, m, 0:tn], AF.Ln,
                                            bias=1.0),
                [dt_s[:, m, 0:tn]], [dt_s[:, m, 0:tn]], "sp_l")
        for m in range(NDT):
            sop(lambda e, m=m: e.activation(dt_s[:, m, 0:tn],
                                            dt_s[:, m, 0:tn],
                                            AF.Tanh, scale=1.0 / 0.099,
                                            bias=-0.0505 / 0.099),
                [dt_s[:, m, 0:tn]], [dt_s[:, m, 0:tn]], "sp_t")
            vop(lambda e, m=m: e.tensor_scalar(
                dt_s[:, m, 0:tn], dt_s[:, m, 0:tn], 0.099, 0.0505,
                ALU.mult, ALU.add),
                [dt_s[:, m, 0:tn]], [dt_s[:, m, 0:tn]], "dt_a")
            vop(lambda e, m=m: e.tensor_mul(dtx_s[:, m, 0:tn],
                                            dt_s[:, m, 0:tn],
                                            xc_s[:, m, 0:tn]),
                [dt_s[:, m, 0:tn], xc_s[:, m, 0:tn]],
                [dtx_s[:, m, 0:tn]], "dtx")
        for d in range(NDT):
            vop(lambda e, d=d: e.tensor_scalar(
                xc_s[:, d, 0:tn], xc_s[:, d, 0:tn], D_c[:, d:d + 1], None,
                ALU.mult),
                [xc_s[:, d, 0:tn], D_c[:]], [xc_s[:, d, 0:tn]], "dxc")

        def a_slice(par):
            return a_buf[:, par * SEGW:(par + 1) * SEGW]

        def a_prep(s0, sl, par):
            toff = s0 - t0
            ab = a_slice(par).rearrange("p (d n s) -> p d n s", n=NST, s=R11)
            dt4 = dt_s[:, :, toff:toff + sl].unsqueeze(2).broadcast_to(
                [128, NDT, NST, sl])
            a164 = A16_c[:].unsqueeze(3).broadcast_to([128, NDT, NST, sl])
            vop(lambda e: e.tensor_tensor(ab[:, :, :, 1:1 + sl], dt4, a164,
                                          ALU.mult),
                [dt_s[:], A16_c[:]], [a_slice(par)], "a_w")
            sop(lambda e: e.activation(ab[:, :, :, 1:1 + sl],
                                       ab[:, :, :, 1:1 + sl], AF.Tanh,
                                       scale=1.0 / 16.0),
                [a_slice(par)], [a_slice(par)], "a_t")
            sop(lambda e: e.activation(ab[:, :, :, 1:1 + sl],
                                       ab[:, :, :, 1:1 + sl], AF.Exp,
                                       scale=16.0),
                [a_slice(par)], [a_slice(par)], "a_e")

        csegs = []
        while seg_ptr[0] < len(cfg.segs):
            s0, sl, rs = cfg.segs[seg_ptr[0]]
            if s0 >= t0 + tn:
                break
            assert s0 >= t0 and s0 + sl <= t0 + tn
            csegs.append((s0, sl, rs))
            seg_ptr[0] += 1

        def y_finish(par, toff, sl):
            DT10 = NDT * 10
            for lev, half in ((0, 8), (1, 4), (2, 2), (3, 1)):
                w = half * DT10
                vop(lambda e, w=w: e.tensor_add(
                    ysc_s[:, 0:w], ysc_s[:, 0:w], ysc_s[:, w:2 * w]),
                    [ysc_s[:, 0:2 * w]], [ysc_s[:, 0:w]], "ytr")
            ydt = bscr2[:, 0:NDT * sl].rearrange("p (d t) -> p d t", t=sl)
            ysum = ysc_s[:, 0:DT10].rearrange(
                "p (d t) -> p d t", t=10)[:, :, 0:sl]
            dxc4 = xc_s[:, :, toff:toff + sl]
            vop(lambda e: e.tensor_tensor(ydt, ysum, dxc4, ALU.add),
                [ysc_s[:, 0:DT10], xc_s[:]], [bscr2[:, 0:NDT * sl]], "yD")
            g4 = gsilu_bf[:, :, toff:toff + sl]
            yb4 = ybf_s[:, :, toff:toff + sl]
            vop(lambda e: e.tensor_tensor(yb4, ydt, g4, ALU.mult),
                [bscr2[:, 0:NDT * sl], gsilu_bf[:]], [ybf_s[:]], "yg")

        pending_y = None
        for si, (s0, sl, rs) in enumerate(csegs):
            par = seg_par[0]
            seg_par[0] = 1 - par
            uh = u_buf if par == 0 else h_buf
            uhn = h_buf if par == 0 else u_buf
            if si == 0:
                a_prep(s0, sl, par)
            toff = s0 - t0
            ub = uh[:].rearrange("p (d n s) -> p d n s", n=NST, s=R11)
            ubn = uhn[:].rearrange("p (d n s) -> p d n s", n=NST, s=R11)
            dtx4 = dtx_s[:, :, toff:toff + sl].unsqueeze(2).broadcast_to(
                [128, NDT, NST, sl])
            bb4 = bbc_s[:, :, toff:toff + sl].unsqueeze(1).broadcast_to(
                [128, NDT, NST, sl])
            vop(lambda e, dtx4=dtx4, bb4=bb4, ub=ub, sl=sl: e.tensor_tensor(
                ub[:, :, :, 1:1 + sl], dtx4, bb4, ALU.mult),
                [dtx_s[:], bbc_s[:]], [uh[:]], "u_w")
            if si + 1 < len(csegs):
                n0, nl, _ = csegs[si + 1]
                a_prep(n0, nl, 1 - par)
            vop(lambda e, par=par, uh=uh: e.tensor_tensor_scan(
                uh[:], a_slice(par), uh[:], 0.0, ALU.mult, ALU.add),
                [a_slice(par), uh[:]], [uh[:]], "scan")
            if pending_y is not None:
                y_finish(*pending_y)
                pending_y = None
            hcols = ub[:, :, :, sl:sl + 1]
            if rs:
                vop(lambda e, hcols=hcols: e.scalar_tensor_tensor(
                    sq_sc[:, 0:NDT * NST].rearrange(
                        "p (d n) -> p d n", n=NST).unsqueeze(3),
                    hcols, 1.0, hcols, ALU.bypass, ALU.mult,
                    accum_out=sqcol[:]),
                    [uh[:]], [sq_sc[:, 0:NDT * NST], sqcol[:]], "r_sq")
                top(lambda e: e.matmul(pBC2[:, 0:1], ones_f[:], sqcol[:],
                                       start=True, stop=True),
                    [ones_f[:], sqcol[:]], [pBC2[:, 0:1]], "r_nm")
                sop(lambda e: e.activation(fac_s[:], pBC2[:, 0:1],
                                           AF.Ln, scale=1.0 / 400.0),
                    [pBC2[:, 0:1]], [fac_s[:]], "r_ln")
                sop(lambda e: e.activation(fac_s[:], fac_s[:],
                                           AF.Exp, scale=-0.5),
                    [fac_s[:]], [fac_s[:]], "r_ex")
                vop(lambda e: e.tensor_scalar_min(fac_s[:], fac_s[:], 1.0),
                    [fac_s[:]], [fac_s[:]], "r_mn")
                vop(lambda e, hcols=hcols: e.tensor_scalar(
                    hcols, hcols, fac_s[:, 0:1], None, ALU.mult),
                    [uh[:], fac_s[:]], [uh[:]], "r_ap")
            vop(lambda e, hcols=hcols, ubn=ubn: e.tensor_copy(
                ubn[:, :, :, 0:1], hcols),
                [uh[:]], [uhn[:]], "carry")
            h_ndt = ub[:, :, :, 1:1 + sl].transpose([0, 2, 1, 3])
            c_ndt = cbc_s[:, :, toff:toff + sl].unsqueeze(1).broadcast_to(
                [128, NDT, NST, sl]).transpose([0, 2, 1, 3])
            yh = ysc_s[:].rearrange("p (n d t) -> p n d t", d=NDT, t=10)
            vop(lambda e, h_ndt=h_ndt, c_ndt=c_ndt, yh=yh, sl=sl:
                e.tensor_tensor(yh[:, :, :, 0:sl], h_ndt, c_ndt, ALU.mult),
                [uh[:], cbc_s[:]], [ysc_s[:]], "yhat")
            pending_y = (par, toff, sl)
        if pending_y is not None:
            y_finish(*pending_y)
            pending_y = None

        for m in range(NDM):
            par = m % 3
            wslab = wm_st[:, par, 0:NDT * 128]
            dma4(wslab, woutT_bf[m, :, :], NDT * 128, "wl3")
            pj = (pMM, pMM2, pMM3)[m % 3]
            for c in range(NDT):
                top(lambda e, c=c, pj=pj, ws=wslab: e.matmul(
                    pj[:, 0:tn], ws[:, c * 128:(c + 1) * 128],
                    ybf_s[:, c, 0:tn], start=(c == 0), stop=(c == NDT - 1)),
                    [wslab, ybf_s[:, c, 0:tn]], [pj[:, 0:tn]], "wo_mm")
            vop(lambda e, m=m: e.tensor_scalar(
                xb_s[:, m, 0:tn], xb_s[:, m, 0:tn], 2.0, None, ALU.mult),
                [xb_s[:, m, 0:tn]], [xb_s[:, m, 0:tn]], "x2x")
            vop(lambda e, m=m, pj=pj: e.scalar_tensor_tensor(
                xb_s[:, m, 0:tn], pj[:, 0:tn], sscale[:, 0:1],
                xb_s[:, m, 0:tn], ALU.mult, ALU.add),
                [pj[:, 0:tn], sscale[:], xb_s[:, m, 0:tn]],
                [xb_s[:, m, 0:tn]], "resid")
        for d in range(NDM):
            top(lambda e, d=d: e.matmul(pStat[:, 0:tn], ones_f[:, 0:1],
                                        xb_s[:, d, 0:tn], start=(d == 0),
                                        stop=(d == NDM - 1)),
                [ones_f[:], xb_s[:, d, 0:tn]], [pStat[:, 0:tn]], "l1_s")
            sop(lambda e, d=d: e.activation(sq_sc[:, 0:tn], xb_s[:, d, 0:tn],
                                            AF.Square),
                [xb_s[:, d, 0:tn]], [sq_sc[:, 0:tn]], "l1_q")
            top(lambda e, d=d: e.matmul(pStat2[:, 0:tn], ones_f[:, 0:1],
                                        sq_sc[:, 0:tn], start=(d == 0),
                                        stop=(d == NDM - 1)),
                [ones_f[:], sq_sc[:, 0:tn]], [pStat2[:, 0:tn]], "l1_q2")
        sop(lambda e: e.activation(row_sc[:, 0:tn], pStat[0:1, 0:tn], AF.Copy,
                                   scale=1.0 / DM),
            [pStat[:, 0:tn]], [row_sc[:, 0:tn]], "l1_m")
        sop(lambda e: e.activation(row_sc2[:, 0:tn], row_sc[:, 0:tn],
                                   AF.Square),
            [row_sc[:, 0:tn]], [row_sc2[:, 0:tn]], "l1_m2")
        top(lambda e: e.matmul(pBC[:, 0:tn], ones_f[0:1, :], row_sc[:, 0:tn],
                               start=True, stop=True),
            [ones_f[:], row_sc[:, 0:tn]], [pBC[:, 0:tn]], "l1_bm")
        vop(lambda e: e.scalar_tensor_tensor(
            row_sc[:, 0:tn], pStat2[0:1, 0:tn], 1.0 / DM, row_sc2[:, 0:tn],
            ALU.mult, ALU.subtract),
            [pStat2[:, 0:tn], row_sc2[:, 0:tn]], [row_sc[:, 0:tn]], "l1_v")
        vop(lambda e: e.tensor_scalar(row_sc[:, 0:tn], row_sc[:, 0:tn],
                                      1.0, 1e-5, ALU.mult, ALU.add),
            [row_sc[:, 0:tn]], [row_sc[:, 0:tn]], "l1_e")
        newton_rsqrt(row_sc[0:1, 0:tn], row_sc2[0:1, 0:tn], 1, tn, "l1_n")
        top(lambda e: e.matmul(pBC2[:, 0:tn], ones_f[0:1, :],
                               row_sc[:, 0:tn], start=True, stop=True),
            [ones_f[:], row_sc[:, 0:tn]], [pBC2[:, 0:tn]], "l1_br")
        for d in range(NDM):
            vop(lambda e, d=d: e.tensor_sub(xb_s[:, d, 0:tn],
                                            xb_s[:, d, 0:tn], pBC[:, 0:tn]),
                [xb_s[:, d, 0:tn], pBC[:, 0:tn]], [xb_s[:, d, 0:tn]], "xf_c")
            vop(lambda e, d=d: e.tensor_mul(xb_s[:, d, 0:tn],
                                            xb_s[:, d, 0:tn], pBC2[:, 0:tn]),
                [xb_s[:, d, 0:tn], pBC2[:, 0:tn]], [xb_s[:, d, 0:tn]],
                "xf_r")
        for c in range(NDM):
            top(lambda e, c=c: e.matmul(
                pBC[0:E, 0:tn], wrt_s[:, c, :], xb_s[:, c, 0:tn],
                start=(c == 0), stop=(c == NDM - 1)),
                [wrt_s[:], xb_s[:, c, 0:tn]], [pBC[:, 0:tn]], "rt_mm")
        vop(lambda e: e.tensor_copy(lrow_s[:, 0:tn], pBC[0:E, 0:tn]),
            [pBC[:, 0:tn]], [lrow_s[:, 0:tn]], "rt_ev")
        dma("scalar", logit_d[:, t0:t0 + tn], lrow_s[:, 0:tn], "rt_o")
        nsub = (tn + 127) // 128
        for s in range(nsub):
            w = min(128, tn - s * 128)
            for d in range(NDM):
                top(lambda e, s=s, d=d, w=w: e.transpose(
                    pT[0:w, :], xb_s[:, d, s * 128:s * 128 + w], ident_f[:]),
                    [xb_s[:, d, 0:tn], ident_f[:]], [pT[:, :]], "xf_tp")
                vop(lambda e, d=d, w=w: e.tensor_copy(
                    xfrow_st[0:w, :], pT[0:w, :]),
                    [pT[:, :]], [xfrow_st[:, :]], "xf_ev")
                dma("scalar",
                    xf_d[t0 + s * 128:t0 + s * 128 + w,
                         d * 128:(d + 1) * 128],
                    xfrow_st[0:w, :], "xf_o")

    moe_cast_done = [0]

    for ci, (t0, tn) in enumerate(cfg.chunks):
        emit_chunk(ci, t0, tn)
        if ci == 1:
            cast_weight(w1_my, w1_bf, NDM, NH, "scalar", vop, "w1")
            moe_cast_done[0] |= 1
        elif ci == 3:
            cast_weight(w2_my, w2_bf, NH, NDM, "scalar", vop, "w2")
            moe_cast_done[0] |= 2
    if not moe_cast_done[0] & 1:
        cast_weight(w1_my, w1_bf, NDM, NH, "scalar", vop, "w1")
    if not moe_cast_done[0] & 2:
        cast_weight(w2_my, w2_bf, NH, NDM, "scalar", vop, "w2")

    mam.close()
    P.barrier()

    moe = ExitStack()
    SM = lambda name, shape, dtype=F32: moe.enter_context(
        nc.sbuf_tensor(name, shape, dtype))

    lgrow = SM("lgrow", [E, L])
    lg_t = SM("lg_t", [128, NTILE, E])
    mx_t = SM("mx_t", [128, NTILE, 8])
    ix_t = SM("ix_t", [128, NTILE, 8], U32)
    e_t = SM("e_t", [128, NTILE, 2])
    dma("sync", lgrow[:], logit_d, "lg_i")
    for i in range(NTILE):
        top(lambda e, i=i: e.transpose(pT[:, 0:E],
                                       lgrow[:, i * 128:(i + 1) * 128],
                                       ident_f[0:E, 0:E]),
            [lgrow[:], ident_f[:]], [pT[:, :]], "lg_tp")
        vop(lambda e, i=i: e.tensor_copy(lg_t[:, i, :], pT[:, 0:E]),
            [pT[:, :]], [lg_t[:, i, :]], "lg_ev")
    for i in range(NTILE):
        vop(lambda e, i=i: e.max(out=mx_t[:, i, :], in_=lg_t[:, i, :]),
            [lg_t[:, i, :]], [mx_t[:, i, :]], "top8")
        vop(lambda e, i=i: e.max_index(out=ix_t[:, i, :],
                                       in_max=mx_t[:, i, :],
                                       in_values=lg_t[:, i, :]),
            [mx_t[:, i, :], lg_t[:, i, :]], [ix_t[:, i, :]], "topi")
        vop(lambda e, i=i: e.tensor_copy(e_t[:, i, :], ix_t[:, i, 0:2]),
            [ix_t[:, i, :]], [e_t[:, i, :]], "topc")
        vop(lambda e, i=i: e.tensor_sub(g_t[:, i, 0:1], mx_t[:, i, 0:1],
                                        mx_t[:, i, 1:2]),
            [mx_t[:, i, :]], [g_t[:, i, 0:1]], "gd")
    sop(lambda e: e.activation(
        g_t[:].rearrange("p a b -> p a b")[:, :, 0:1],
        g_t[:].rearrange("p a b -> p a b")[:, :, 0:1], AF.Sigmoid),
        [g_t[:]], [g_t[:]], "gsig")
    vop(lambda e: e.tensor_scalar(
        g_t[:].rearrange("p a b -> p a b")[:, :, 1:2],
        g_t[:].rearrange("p a b -> p a b")[:, :, 0:1], -1.0, 1.0,
        ALU.mult, ALU.add),
        [g_t[:]], [g_t[:]], "g1m")

    dma("gpsimd", slot_d[0, :].rearrange("(a p b) -> p a b", p=128, b=2),
        e_t[:], "ei_o")
    SC = 512
    NSC = (2 * L + SC - 1) // SC
    cs_d = dscr("cs_d", [8, 2 * L])
    oh_s = SM("oh_s", [8, SC])
    cs_s = SM("cs_s", [8, SC])
    zz_s = SM("zz_s", [8, SC])
    eidx_s = SM("eidx_s", [8, SC])
    tmp8 = SM("tmp8", [8, SC])
    carry8 = SM("carry8", [8, 1])
    ecol = SM("ecol", [8, 1], I32)
    ecolf = SM("ecolf", [8, 1])
    gop(lambda e: e.iota(ecol[:], [[1, 1]], channel_multiplier=1),
        [], [ecol[:]], "e_io")
    vop(lambda e: e.tensor_copy(ecolf[:], ecol[:]), [ecol[:]], [ecolf[:]],
        "e_iof")
    gop(lambda e: e.memset(zz_s[:], 0.0), [], [zz_s[:]], "zz")
    vop(lambda e: e.memset(carry8[:], 0.0), [], [carry8[:]], "zc8")

    def slot_chunk_oh_cs(q, wq):
        dma("sync", eidx_s[:, 0:wq],
            slot_d[0:1, q * SC:q * SC + wq].broadcast_to([8, wq]), "ei_i")
        vop(lambda e, wq=wq: e.tensor_scalar(
            oh_s[:, 0:wq], eidx_s[:, 0:wq], ecolf[:, 0:1], None,
            ALU.is_equal),
            [eidx_s[:, 0:wq], ecolf[:]], [oh_s[:, 0:wq]], "oh")
        vop(lambda e, wq=wq: e.tensor_tensor_scan(
            cs_s[:, 0:wq], oh_s[:, 0:wq], zz_s[:, 0:wq], carry8[:, 0:1],
            ALU.add, ALU.add),
            [oh_s[:, 0:wq], zz_s[:, 0:wq], carry8[:]], [cs_s[:, 0:wq]],
            "cscan")
        vop(lambda e, wq=wq: e.tensor_copy(carry8[:], cs_s[:, wq - 1:wq]),
            [cs_s[:, 0:wq]], [carry8[:]], "csc")

    for q in range(NSC):
        wq = min(SC, 2 * L - q * SC)
        slot_chunk_oh_cs(q, wq)
        dma("gpsimd", cs_d[:, q * SC:q * SC + wq], cs_s[:, 0:wq], "cs_o")
    cnt_s = SM("cnt_s", [8, 8])
    cntall = SM("cntall", [8, 8])
    offm = SM("offm", [8, 8])
    offcol = SM("offcol", [8, 1])
    dma("gpsimd", meta_d[0, 0:8].rearrange("(a b) -> a b", a=8),
        carry8[:], "cn_o")
    dma("sync", cnt_s[:], meta_d[0:1, 0:8].broadcast_to([8, 8]), "cn_i")
    vop(lambda e: e.tensor_scalar(cnt_s[:], cnt_s[:], r1hot[:, 0:1], None,
                                  ALU.mult),
        [cnt_s[:], r1hot[:]], [cnt_s[:]], "cn_m")
    dma("gpsimd", cc_src, cnt_s[:], "cc_o")
    P.op("gpsimd", lambda e: e.collective_compute(
        "AllReduce", ALU.add, replica_groups=[list(range(n_cores))],
        ins=[cc_src.opt()], outs=[cc_dst.opt()]),
        [cc_src], [cc_dst], compl=("cc", "gpsimd"), tag="ccar")
    dma("gpsimd", cntall[:], cc_dst, "cc_i")
    vop(lambda e: e.tensor_scalar(offm[:], cntall[:], pmask[:, 0:1], None,
                                  ALU.mult),
        [cntall[:], pmask[:]], [offm[:]], "of_m")
    top(lambda e: e.matmul(pBC[0:8, 0:1], offm[:], ones_f[0:8, 0:1],
                           start=True, stop=True),
        [offm[:], ones_f[:]], [pBC[:, 0:1]], "of_s")
    vop(lambda e: e.tensor_copy(offcol[:], pBC[0:8, 0:1]),
        [pBC[:, 0:1]], [offcol[:]], "of_e")
    selrow = SM("selrow", [1, SC])
    keep_r = SM("keep_r", [1, SC])
    lpos_r = SM("lpos_r", [1, SC])
    gidx_r = SM("gidx_r", [1, SC])
    sidx_r = SM("sidx_r", [1, SC])
    for q in range(NSC):
        wq = min(SC, 2 * L - q * SC)
        dma("sync", eidx_s[:, 0:wq],
            slot_d[0:1, q * SC:q * SC + wq].broadcast_to([8, wq]), "ei_i2")
        vop(lambda e, wq=wq: e.tensor_scalar(
            oh_s[:, 0:wq], eidx_s[:, 0:wq], ecolf[:, 0:1], None,
            ALU.is_equal),
            [eidx_s[:, 0:wq], ecolf[:]], [oh_s[:, 0:wq]], "oh2")
        dma("sync", cs_s[:, 0:wq], cs_d[:, q * SC:q * SC + wq], "cs_i")
        vop(lambda e, wq=wq: e.tensor_scalar(
            tmp8[:, 0:wq], cs_s[:, 0:wq], offcol[:, 0:1], None, ALU.add),
            [cs_s[:, 0:wq], offcol[:]], [tmp8[:, 0:wq]], "sl_a")
        vop(lambda e, wq=wq: e.tensor_tensor(
            tmp8[:, 0:wq], tmp8[:, 0:wq], oh_s[:, 0:wq], ALU.mult),
            [tmp8[:, 0:wq], oh_s[:, 0:wq]], [tmp8[:, 0:wq]], "sl_m")
        for j in range((wq + 511) // 512):
            w = min(512, wq - j * 512)
            top(lambda e, j=j, w=w: e.matmul(
                pStat[0:1, 0:w], ones_f[0:8, 0:1],
                tmp8[:, j * 512:j * 512 + w], start=True, stop=True),
                [ones_f[:], tmp8[:, 0:wq]], [pStat[:, :]], "sl_r")
            vop(lambda e, j=j, w=w: e.tensor_copy(
                selrow[:, j * 512:j * 512 + w], pStat[0:1, 0:w]),
                [pStat[:, :]], [selrow[:]], "sl_e")
        vop(lambda e, wq=wq: e.tensor_scalar(
            keep_r[:, 0:wq], selrow[:, 0:wq], CAP + 0.5, None, ALU.is_lt),
            [selrow[:, 0:wq]], [keep_r[:, 0:wq]], "kp")
        vop(lambda e, wq=wq: e.tensor_scalar(
            lpos_r[:, 0:wq], selrow[:, 0:wq], -1.0, None, ALU.add),
            [selrow[:, 0:wq]], [lpos_r[:, 0:wq]], "lp")
        vop(lambda e, wq=wq: e.tensor_tensor(
            lpos_r[:, 0:wq], lpos_r[:, 0:wq], keep_r[:, 0:wq], ALU.mult),
            [lpos_r[:, 0:wq], keep_r[:, 0:wq]], [lpos_r[:, 0:wq]], "lpk")
        vop(lambda e, wq=wq: e.scalar_tensor_tensor(
            gidx_r[:, 0:wq], eidx_s[0:1, 0:wq], float(CAP),
            lpos_r[:, 0:wq], ALU.mult, ALU.add),
            [eidx_s[:, 0:wq], lpos_r[:, 0:wq]], [gidx_r[:, 0:wq]], "gx")
        vop(lambda e, wq=wq: e.tensor_scalar(
            sidx_r[:, 0:wq], keep_r[:, 0:wq], -1e7, 1e7, ALU.mult, ALU.add),
            [keep_r[:, 0:wq]], [sidx_r[:, 0:wq]], "sx1")
        vop(lambda e, wq=wq: e.tensor_tensor(
            sidx_r[:, 0:wq], sidx_r[:, 0:wq], gidx_r[:, 0:wq], ALU.add),
            [sidx_r[:, 0:wq], gidx_r[:, 0:wq]], [sidx_r[:, 0:wq]], "sx2")
        dma("gpsimd", slot_d[1, q * SC:q * SC + wq], gidx_r[0:1, 0:wq], "gx_o")
        dma("gpsimd", slot_d[2, q * SC:q * SC + wq], sidx_r[0:1, 0:wq], "sx_o")
        dma("gpsimd", slot_d[3, q * SC:q * SC + wq], keep_r[0:1, 0:wq], "kp_o")
    keep_t = SM("keep_t", [128, NTILE, 2])
    dma("sync", keep_t[:],
        slot_d[3, :].rearrange("(a p b) -> p a b", p=128, b=2), "kp_i")
    vop(lambda e: e.tensor_tensor(g_t[:].rearrange("p a b -> p (a b)"),
                                  g_t[:].rearrange("p a b -> p (a b)"),
                                  keep_t[:].rearrange("p a b -> p (a b)"),
                                  ALU.mult),
        [g_t[:], keep_t[:]], [g_t[:]], "gk")
    sidx_tf = SM("sidx_tf", [128, NTILE, 2])
    sidx_t = SM("sidx_t", [128, NTILE, 2], I32)
    dma("sync", sidx_tf[:],
        slot_d[2, :].rearrange("(a p b) -> p a b", p=128, b=2), "sx_i")
    vop(lambda e: e.tensor_copy(sidx_t[:].rearrange("p a b -> p (a b)"),
                                sidx_tf[:].rearrange("p a b -> p (a b)")),
        [sidx_tf[:]], [sidx_t[:]], "sx_c")
    rowb = SM("rowb", [128, DM])
    rowbf = SM("rowbf", [128, DM], BF16)
    bnd_reg = moe.enter_context(nc.gpsimd.register("bnd_reg"))
    gop(lambda e: e.reg_mov(bnd_reg, E * CAP - 1), [], [], "bmov")
    for i in range(NTILE):
        dma("sync", rowb[:], xf_d[i * 128:(i + 1) * 128, :], "xf_i")
        vop(lambda e: e.tensor_copy(rowbf[:], rowb[:]),
            [rowb[:]], [rowbf[:]], "sc_b")
        for k in range(2):
            idma_scatter(slab_d, sidx_t[:, i, k:k + 1], rowbf[:],
                         bnd_reg, "scat")

    P.op("gpsimd", lambda e: e.collective_compute(
        "ReduceScatter", ALU.add, replica_groups=[list(range(n_cores))],
        ins=[slab_d.opt()], outs=[shard_d.opt()]),
        [slab_d], [shard_d], compl=("cc", "rs"), tag="ccrs")

    ffn = ExitStack()
    SF = lambda name, shape, dtype=F32: ffn.enter_context(
        nc.sbuf_tensor(name, shape, dtype))
    xfT_e = SF("xfT_e", [128, NDM, CAP], BF16)
    hrc_s = SF("hrc_s", [128, NH, 512], BF16)
    ob_sc = SF("ob_sc", [128, NDM, 512])
    rowbB = SF("rowbB", [128, DM], BF16)
    orow_st = SF("orow_st", [128, DM])
    w1sl = SF("w1sl", [128, 2, 8, NDM * 128], BF16)
    wm2b = SF("wm2b", [128, 2, NH * 128], BF16)
    NRC = (CAP + 511) // 512
    for r in range(CAP // 128):
        dma("sync", rowbB[:], shard_d[r * 128:(r + 1) * 128, :], "sb_i")
        for d in range(NDM):
            top(lambda e, d=d: e.transpose(
                pT.bitcast(BF16)[:, 0:128],
                rowbB[:, d * 128:(d + 1) * 128], ident_bf[:]),
                [rowbB[:], ident_bf[:]], [pT[:, :]], "sb_t")
            vop(lambda e, d=d, r=r: e.tensor_copy(
                xfT_e[:, d, r * 128:(r + 1) * 128],
                pT.bitcast(BF16)[:, 0:128]),
                [pT[:, :]], [xfT_e[:]], "sb_e")
    for rc in range(NRC):
        w = min(512, CAP - rc * 512)
        for mg in range(NH // 8):
            gpar = mg % 2
            for mm in range(8):
                m = mg * 8 + mm
                dma4(w1sl[:, gpar, mm, :], w1_bf[m, :, :], NDM * 128, "w1l")
            for mm in range(8):
                m = mg * 8 + mm
                wslab = w1sl[:, gpar, mm, :]
                pj = (pMM, pMM2, pMM3)[m % 3]
                for c in range(NDM):
                    top(lambda e, c=c, pj=pj, ws=wslab, rc=rc, w=w:
                        e.matmul(
                            pj[:, 0:w], ws[:, c * 128:(c + 1) * 128],
                            xfT_e[:, c, rc * 512:rc * 512 + w],
                            start=(c == 0), stop=(c == NDM - 1)),
                        [wslab, xfT_e[:]], [pj[:, 0:w]], "w1m")
                sop(lambda e, m=m, pj=pj, w=w: e.activation(
                    hrc_s[:, m, 0:w], pj[:, 0:w], AF.Relu,
                    bias=b1c[:, m:m + 1]),
                    [pj[:, 0:w], b1c[:]], [hrc_s[:, m, 0:w]], "rel")
        for m in range(NDM):
            par = m % 2
            wslab = wm2b[:, par, 0:NH * 128]
            dma4(wslab, w2_bf[m, :, :], NH * 128, "w2l")
            pj = pMM if m % 2 == 0 else pMM2
            for c in range(NH):
                top(lambda e, c=c, pj=pj, ws=wslab, w=w: e.matmul(
                    pj[:, 0:w], ws[:, c * 128:(c + 1) * 128],
                    hrc_s[:, c, 0:w], start=(c == 0), stop=(c == NH - 1)),
                    [wslab, hrc_s[:]], [pj[:, 0:w]], "w2m")
            sop(lambda e, m=m, pj=pj, w=w: e.activation(
                ob_sc[:, m, 0:w], pj[:, 0:w], AF.Identity,
                bias=b2c[:, m:m + 1]),
                [pj[:, 0:w], b2c[:]], [ob_sc[:, m, 0:w]], "b2a")
        nsub = (w + 127) // 128
        for s2 in range(nsub):
            ww = min(128, w - s2 * 128)
            for d in range(NDM):
                top(lambda e, d=d, s2=s2, ww=ww: e.transpose(
                    pT[0:ww, :], ob_sc[:, d, s2 * 128:s2 * 128 + ww],
                    ident_f[:]),
                    [ob_sc[:], ident_f[:]], [pT[:, :]], "ob_t")
                vop(lambda e, d=d, ww=ww: e.tensor_copy(
                    orow_st[0:ww, d * 128:(d + 1) * 128], pT[0:ww, :]),
                    [pT[:, :]], [orow_st[:, :]], "ob_e")
            r0 = rc * 512 + s2 * 128
            dma("gpsimd", eout_my[r0:r0 + ww, :], orow_st[0:ww, :], "ob_o")

    P.op("gpsimd", lambda e: e.collective_compute(
        "AllGather", ALU.bypass, replica_groups=[list(range(n_cores))],
        ins=[eout_my.opt()], outs=[eout_d.opt()]),
        [eout_my], [eout_d], compl=("cc", "ag"), tag="ccag")
    ffn.close()

    gidx_tf = SM("gidx_tf", [128, NTILE, 2])
    gidx_t = SM("gidx_t", [128, NTILE, 2], I32)
    dma("sync", gidx_tf[:],
        slot_d[1, :].rearrange("(a p b) -> p a b", p=128, b=2), "gx_i")
    vop(lambda e: e.tensor_copy(gidx_t[:].rearrange("p a b -> p (a b)"),
                                gidx_tf[:].rearrange("p a b -> p (a b)")),
        [gidx_tf[:]], [gidx_t[:]], "gx_c")
    r0b = SM("r0b", [128, DM])
    r1b = SM("r1b", [128, DM])
    xfb = SM("xfb", [128, DM])
    ycb = SM("ycb", [128, DM])
    osq = SM("osq", [128, DM])
    msum = SM("msum", [128, 1])
    vsum = SM("vsum", [128, 1])
    m2c = SM("m2c", [128, 1])
    for i in range(NTILE):
        idma_gather(r0b[:], eout_d, gidx_t[:, i, 0:1], "ga0")
        idma_gather(r1b[:], eout_d, gidx_t[:, i, 1:2], "ga1")
        dma("sync", xfb[:], xf_d[i * 128:(i + 1) * 128, :], "xf_i2")
        vop(lambda e, i=i: e.tensor_scalar(
            ycb[:], r0b[:], g_t[:].rearrange("p a b -> p a b")[:, i, 0:1],
            None, ALU.mult),
            [r0b[:], g_t[:]], [ycb[:]], "cb0")
        vop(lambda e, i=i: e.scalar_tensor_tensor(
            ycb[:], r1b[:], g_t[:].rearrange("p a b -> p a b")[:, i, 1:2],
            ycb[:], ALU.mult, ALU.add),
            [r1b[:], g_t[:], ycb[:]], [ycb[:]], "cb1")
        vop(lambda e: e.tensor_add(ycb[:], ycb[:], xfb[:]),
            [ycb[:], xfb[:]], [ycb[:]], "cb2")
        vop(lambda e: e.tensor_reduce(msum[:], ycb[:], AX.X, ALU.add),
            [ycb[:]], [msum[:]], "l2s")
        vop(lambda e: e.scalar_tensor_tensor(osq[:], ycb[:], 1.0, ycb[:],
                                             ALU.bypass, ALU.mult,
                                             accum_out=vsum[:]),
            [ycb[:]], [osq[:], vsum[:]], "l2q")
        vop(lambda e: e.tensor_scalar(msum[:], msum[:], 1.0 / DM, None,
                                      ALU.mult),
            [msum[:]], [msum[:]], "l2m")
        vop(lambda e: e.scalar_tensor_tensor(m2c[:], msum[:], 1.0, msum[:],
                                             ALU.bypass, ALU.mult),
            [msum[:]], [m2c[:]], "l2m2")
        vop(lambda e: e.scalar_tensor_tensor(vsum[:], vsum[:], 1.0 / DM,
                                             m2c[:], ALU.mult, ALU.subtract),
            [vsum[:], m2c[:]], [vsum[:]], "l2v")
        sop(lambda e: e.activation(vsum[:], vsum[:], AF.Ln, bias=1e-5),
            [vsum[:]], [vsum[:]], "l2l")
        sop(lambda e: e.activation(vsum[:], vsum[:], AF.Exp, scale=-0.5),
            [vsum[:]], [vsum[:]], "l2r")
        vop(lambda e: e.tensor_scalar(ycb[:], ycb[:], msum[:, 0:1], None,
                                      ALU.subtract),
            [ycb[:], msum[:]], [ycb[:]], "l2c")
        vop(lambda e: e.tensor_scalar(ycb[:], ycb[:], vsum[:, 0:1], None,
                                      ALU.mult),
            [ycb[:], vsum[:]], [ycb[:]], "l2f")
        dma("gpsimd", out[i * 128:(i + 1) * 128, :], ycb[:], "f_o")

    P.emit()
    moe.close()
    persist.close()
    return nc



def make_in_map(cfg, core, xb, w):
    E = cfg.E
    r1 = np.zeros((8, 1), np.float32)
    r1[core, 0] = 1.0
    pm = np.zeros((8, 1), np.float32)
    pm[:core, 0] = 1.0
    DI, NST, NDT, NDM, NH = cfg.DI, cfg.NST, cfg.NDT, cfg.NDM, cfg.NH
    return {
        "x_t": np.ascontiguousarray(xb.T),
        "norm_w_t": np.ascontiguousarray(
            w["norm_w"].reshape(NDM, 128).T),
        "w_inT": np.ascontiguousarray(w["W_in"].T),
        "conv_w_t": np.ascontiguousarray(
            w["conv_w"].reshape(NDT, 128, 4).transpose(1, 0, 2)),
        "conv_b_t": np.ascontiguousarray(w["conv_b"].reshape(NDT, 128).T),
        "w_dtT": np.ascontiguousarray(w["W_dt"].T),
        "b_dt_t": np.ascontiguousarray(w["b_dt"].reshape(NDT, 128).T),
        "w_BT": np.ascontiguousarray(w["W_B"].T),
        "w_CT": np.ascontiguousarray(w["W_C"].T),
        "A_log_t": np.ascontiguousarray(w["A_log"].reshape(NDT, 128, NST)),
        "D_t": np.ascontiguousarray(w["D_param"].reshape(NDT, 128).T),
        "w_outT": np.ascontiguousarray(w["W_out"].T),
        "res_scale": np.ascontiguousarray(
            w["residual_scale"].reshape(1, 1)),
        "w_router": np.ascontiguousarray(w["w_router"]),
        "w1": np.ascontiguousarray(w["w1"][core]),
        "b1_t": np.ascontiguousarray(w["b1"][core].reshape(NH, 128).T),
        "w2": np.ascontiguousarray(w["w2"][core]),
        "b2_t": np.ascontiguousarray(w["b2"][core].reshape(NDM, 128).T),
        "rank1hot": r1,
        "prefmask": pm,
    }


def kernel(**inputs):
    cfg = CFG()
    w = {
        "norm_w": inputs["L0_attn_norm_w"], "W_in": inputs["L0_attn_W_in"],
        "conv_w": inputs["L0_attn_conv_w"][:, 0, :],
        "conv_b": inputs["L0_attn_conv_b"], "W_dt": inputs["L0_attn_W_dt"],
        "b_dt": inputs["L0_attn_b_dt"], "W_B": inputs["L0_attn_W_B"],
        "W_C": inputs["L0_attn_W_C"], "A_log": inputs["L0_attn_A_log"],
        "D_param": inputs["L0_attn_D_param"], "W_out": inputs["L0_attn_W_out"],
        "residual_scale": np.asarray(inputs["L0_attn_residual_scale"]),
        "w_router": inputs["L0_ffn_w_router"], "w1": inputs["L0_ffn_w1"],
        "b1": inputs["L0_ffn_b1"], "w2": inputs["L0_ffn_w2"],
        "b2": inputs["L0_ffn_b2"],
    }
    x = inputs["L0_attn_x"]
    n_cores = x.shape[0]
    nc = build_program(cfg, n_cores=n_cores)
    in_maps = [make_in_map(cfg, c, x[c], w) for c in range(n_cores)]
    res = run_bass_kernel_spmd(nc, in_maps, core_ids=list(range(n_cores)))
    outs = [res.results[c]["out"] for c in range(n_cores)]
    return np.stack(outs, axis=0)


# version 23
# speedup vs baseline: 1.0601x; 1.0601x over previous
import numpy as np
import concourse.bass as bass
import concourse.mybir as mybir
from concourse.bass_utils import run_bass_kernel_spmd
from concourse.bass_core_rust import IndirectOffsetOnAxis
from contextlib import ExitStack

F32 = mybir.dt.float32
BF16 = mybir.dt.bfloat16
I32 = mybir.dt.int32
U32 = mybir.dt.uint32
AF = mybir.ActivationFunctionType
ALU = mybir.AluOpType
AX = mybir.AxisListType

ENGINES = ("tensor", "vector", "scalar", "gpsimd", "sync")
LN20 = float(np.log(20.0))



def _ap_rect(ap):
    dims = [(int(s), int(c)) for s, c in ap.ap]
    off = int(ap.offset)
    esz = np.dtype(mybir.dt.np(ap.dtype)).itemsize
    space = str(ap.space)
    if "DRAM" in space:
        ext = sum(abs(s) * (c - 1) for s, c in dims)
        return (0, 128, off * esz, (off + ext + 1) * esz)
    p0 = ap.base_partition()
    pn = dims[0][1] if dims else 1
    pstep = dims[0][0] if dims else 0
    foff = off - p0 * pstep
    ext = sum(abs(s) * (c - 1) for s, c in dims[1:])
    return (p0, p0 + pn, foff * esz, (foff + ext + 1) * esz)


def _overlap(r1, r2):
    return (r1[0] < r2[1] and r2[0] < r1[1] and
            r1[2] < r2[3] and r2[2] < r1[3])


def _covers(rn, ro):
    return (rn[0] <= ro[0] and rn[1] >= ro[1] and
            rn[2] <= ro[2] and rn[3] >= ro[3])


class Op:
    __slots__ = ("idx", "eng", "emit", "compl", "marked", "sem_val", "tag",
                 "dom", "prewait")

    def __init__(self, idx, eng, emit, compl, tag):
        self.idx, self.eng, self.emit = idx, eng, emit
        self.compl = compl
        self.marked = False
        self.sem_val = None
        self.tag = tag
        self.dom = None
        self.prewait = None


class Prog:
    NLANE = 12

    def __init__(self, nc):
        self.nc = nc
        self.ops = []
        self.hstate = {}
        self.edges = []
        self.lane_ctr = {}
        self.lane_use = {}

    def op(self, eng, emit, reads=(), writes=(), compl=None, tag=""):
        idx = len(self.ops)
        o = Op(idx, eng, emit, compl or ("eng", eng), tag)
        if o.compl[0] == "dma":
            q = o.compl[1]
            k = self.lane_ctr.get(q, 0)
            self.lane_ctr[q] = k + 1
            dom = ("dma", q, k % self.NLANE)
            o.dom = dom
            u = self.lane_use.get(dom, 0) + 1
            self.lane_use[dom] = u
            o.sem_val = 16 * u
            if u > 1:
                o.prewait = (dom, 16 * (u - 1))
        self.ops.append(o)
        deps = set()
        rrects = [(a.tensor.name, _ap_rect(a)) for a in reads]
        wrects = [(a.tensor.name, _ap_rect(a)) for a in writes]
        for key, rect in rrects:
            stt = self.hstate.get(key)
            if stt:
                for widx, wrect in stt["w"]:
                    if _overlap(rect, wrect):
                        deps.add(widx)
        for key, rect in wrects:
            stt = self.hstate.get(key)
            if stt:
                for widx, wrect in stt["w"]:
                    if _overlap(rect, wrect):
                        deps.add(widx)
                for ridx, rrect in stt["r"]:
                    if _overlap(rect, rrect):
                        deps.add(ridx)
        deps.discard(idx)
        self.edges.append(deps)
        for key, rect in wrects:
            stt = self.hstate.setdefault(key, {"w": [], "r": []})
            stt["w"] = [(w, r) for (w, r) in stt["w"]
                        if not _covers(rect, r)] + [(idx, rect)]
            stt["r"] = [(w, r) for (w, r) in stt["r"] if not _covers(rect, r)]
        for key, rect in rrects:
            stt = self.hstate.setdefault(key, {"w": [], "r": []})
            stt["r"].append((idx, rect))
            if len(stt["r"]) > 768:
                stt["r"] = stt["r"][-768:]
        return o

    def barrier(self):
        tails = {}
        for o in self.ops:
            tails[self._dom(o)] = o.idx
        tl = set(tails.values())
        for eng in ENGINES:
            o = self.op(eng, lambda e: e.nop(), tag="barrier")
            self.edges[o.idx] = set(tl)
        bidx = [o.idx for o in self.ops[-5:]]
        for key, stt in self.hstate.items():
            stt["w"] = [(bidx[0], (0, 128, 0, 1 << 40))]
            stt["r"] = []

    def _dom(self, o):
        if o.dom is not None:
            return o.dom
        return o.compl if o.compl[0] != "eng" else ("eng", o.eng)

    def emit(self):
        nc = self.nc
        for deps in self.edges:
            for d in deps:
                self.ops[d].marked = True
        for o in self.ops:
            if o.compl[0] in ("dma", "cc"):
                o.marked = True
        lane_use = self.lane_use
        dom_counts = {}
        doms = {}
        for o in self.ops:
            dom = self._dom(o)
            doms.setdefault(dom, []).append(o)
            if o.marked and o.sem_val is None:
                inc = 1
                dom_counts[dom] = dom_counts.get(dom, 0) + inc
                o.sem_val = dom_counts[dom]
                if dom[0] == "cc":
                    o.sem_val = 1
        for dom, u in lane_use.items():
            dom_counts[dom] = 16 * u
        self.sems = {}
        self._semctx = ExitStack()
        for dom in doms:
            self.sems[dom] = self._semctx.enter_context(
                nc.semaphore('s_' + '_'.join(str(x) for x in dom)))
        eng_clock = {e: {} for e in ENGINES}
        op_clock = {}
        waits_for = {}
        for o in self.ops:
            cl = dict(eng_clock[o.eng])
            w = []
            for d in sorted(self.edges[o.idx]):
                p = self.ops[d]
                pdom = self._dom(p)
                assert p.marked
                need = p.sem_val
                if cl.get(pdom, -1) >= need:
                    continue
                w.append((pdom, need))
                cl[pdom] = need
                for kd, kv in op_clock.get(d, {}).items():
                    if cl.get(kd, -1) < kv:
                        cl[kd] = kv
            waits_for[o.idx] = w
            op_clock[o.idx] = cl
            eng_clock[o.eng] = cl
        with nc.Block() as block:
            for eng_name in ENGINES:
                stream = [o for o in self.ops if o.eng == eng_name]

                def make_body(stream, eng_name):
                    def body(eng):
                        for o in stream:
                            if o.prewait is not None:
                                eng.wait_ge(self.sems[o.prewait[0]],
                                            o.prewait[1])
                            for dom, val in waits_for[o.idx]:
                                eng.wait_ge(self.sems[dom], val)
                            try:
                                inst = o.emit(eng)
                            except Exception as ex:
                                raise RuntimeError(
                                    f"emit failed op#{o.idx} tag={o.tag} "
                                    f"eng={o.eng}") from ex
                            if o.marked:
                                dom = self._dom(o)
                                inst.then_inc(self.sems[dom],
                                              16 if dom[0] == "dma" else 1)
                        if eng_name == "sync":
                            for dom, cnt in dom_counts.items():
                                if dom[0] == "dma":
                                    eng.wait_ge(self.sems[dom], cnt)
                                elif dom[0] == "cc":
                                    eng.wait_ge(self.sems[dom], 1)
                    return body

                getattr(block, eng_name)(make_body(stream, eng_name))



class CFG:
    def __init__(self, L=4096, E=8, CAP=3072):
        self.L, self.DM, self.DI, self.NST = L, 1024, 2048, 16
        self.E, self.H, self.CAP = E, 4096, CAP
        self.NDT = self.DI // 128
        self.NDM = self.DM // 128
        self.NH = self.H // 128
        R = 10
        base = 26 * R
        chunks = [(0, min(L, base + 1))]
        t = chunks[0][1]
        while t < L:
            n = min(base, L - t)
            chunks.append((t, n))
            t += n
        self.chunks = chunks
        segs = [(0, 1, True)]
        t = 1
        while t < L:
            n = min(R, L - t)
            segs.append((t, n, ((t + n - 1) % R) == 0))
            t += n
        self.segs = segs
        self.NTILE = L // 128



def build_program(cfg, n_cores=8):
    nc = bass.Bass("TRN2", target_bir_lowering=False)
    P = Prog(nc)
    L, DM, DI, NST = cfg.L, cfg.DM, cfg.DI, cfg.NST
    E, H, CAP = cfg.E, cfg.H, cfg.CAP
    NDT, NDM, NH, NTILE = cfg.NDT, cfg.NDM, cfg.NH, cfg.NTILE
    MAXC = max(n for _, n in cfg.chunks)
    R11 = 11

    def din(name, shape, dtype=F32):
        return nc.dram_tensor(name, shape, dtype, kind="ExternalInput").ap()

    x_t = din("x_t", [DM, L])
    norm_w_t = din("norm_w_t", [128, NDM])
    w_inT = din("w_inT", [DM, 2 * DI])
    conv_w_t = din("conv_w_t", [128, NDT, 4])
    conv_b_t = din("conv_b_t", [128, NDT])
    w_dtT = din("w_dtT", [DI, DI])
    b_dt_t = din("b_dt_t", [128, NDT])
    w_BT = din("w_BT", [DI, NST])
    w_CT = din("w_CT", [DI, NST])
    A_log_t = din("A_log_t", [NDT, 128, NST])
    D_t = din("D_t", [128, NDT])
    w_outT = din("w_outT", [DI, DM])
    res_scale = din("res_scale", [1, 1])
    w_router_i = din("w_router", [DM, E])
    w1_my = din("w1", [DM, H])
    b1_t = din("b1_t", [128, NH])
    w2_my = din("w2", [H, DM])
    b2_t = din("b2_t", [128, NDM])
    rank1hot = din("rank1hot", [8, 1])
    prefmask = din("prefmask", [8, 1])

    out = nc.dram_tensor("out", [L, DM], F32, kind="ExternalOutput").ap()

    def dscr(name, shape, dtype=F32):
        return nc.dram_tensor(name, shape, dtype, kind="Internal").ap()

    winT_bf = dscr("winT_bf", [2 * NDT, 128, NDM * 128], BF16)
    wdtT_bf = dscr("wdtT_bf", [NDT, 128, NDT * 128], BF16)
    woutT_bf = dscr("woutT_bf", [NDM, 128, NDT * 128], BF16)
    w1_bf = dscr("w1_bf", [NH, 128, NDM * 128], BF16)
    w2_bf = dscr("w2_bf", [NDM, 128, NH * 128], BF16)
    brow_d = dscr("brow_d", [NST, MAXC])
    crow_d = dscr("crow_d", [NST, MAXC])
    xf_d = dscr("xf_d", [L, DM])
    logit_d = dscr("logit_d", [E, L])
    slot_d = dscr("slot_d", [4, 2 * L])
    slab_d = dscr("slab_d", [E * CAP, DM], BF16)
    shard_d = dscr("shard_d", [CAP, DM], BF16)
    eout_my = dscr("eout_my", [CAP, DM])
    eout_d = dscr("eout_d", [E * CAP, DM])
    cc_src = dscr("cc_src", [8, 8])
    cc_dst = dscr("cc_dst", [8, 8])
    meta_d = dscr("meta_d", [1, 64])

    def vop(emit, reads, writes, tag=""):
        return P.op("vector", emit, reads, writes, tag=tag)

    def sop(emit, reads, writes, tag=""):
        return P.op("scalar", emit, list(reads) + [const_pool[:]], writes,
                    tag=tag)

    def top(emit, reads, writes, tag=""):
        return P.op("tensor", emit, reads, writes, tag=tag)

    def gop(emit, reads, writes, tag=""):
        return P.op("gpsimd", emit, reads, writes, tag=tag)

    def dma(queue, dst, src, tag=""):
        return P.op(queue, lambda e, d=dst, s=src: e.dma_start(d, s),
                    [src], [dst], compl=("dma", queue), tag=tag)

    def dma4(dst, src, w, tag="", xreads=()):
        q = w // 4
        for i in range(4):
            P.op(("sync", "scalar")[i % 2],
                 lambda e, d=dst[:, i * q:(i + 1) * q],
                 s=src[:, i * q:(i + 1) * q]: e.dma_start(d, s),
                 [src[:, i * q:(i + 1) * q]] + list(xreads),
                 [dst[:, i * q:(i + 1) * q]],
                 compl=("dma", ("sync", "scalar")[i % 2]), tag=tag)

    def idma_scatter(dst, idx_ap, src, bound, tag=""):
        return P.op("gpsimd",
                    lambda e: e.indirect_dma_start(
                        dst, IndirectOffsetOnAxis(idx_ap, 0), src, None,
                        bounds_check=bound, oob_is_err=False),
                    [src, idx_ap], [dst], compl=("dma", "gpsimd"), tag=tag)

    def idma_gather(dst, src, idx_ap, tag=""):
        return P.op("gpsimd",
                    lambda e: e.indirect_dma_start(
                        dst, None, src, IndirectOffsetOnAxis(idx_ap, 0)),
                    [src, idx_ap], [dst], compl=("dma", "gpsimd"), tag=tag)

    persist = ExitStack()
    SBp = lambda name, shape, dtype=F32: persist.enter_context(
        nc.sbuf_tensor(name, shape, dtype))
    PSp = lambda name, shape, dtype=F32: persist.enter_context(
        nc.psum_tensor(name, shape, dtype))

    ones_f = SBp("ones_f", [128, 128])
    ident_bf = SBp("ident_bf", [128, 128], BF16)
    ident_f = SBp("ident_f", [128, 128])
    normw_c = SBp("normw_c", [128, NDM])
    convw_c = SBp("convw_c", [128, NDT, 4])
    convb_c = SBp("convb_c", [128, NDT])
    bdt_c = SBp("bdt_c", [128, NDT])
    D_c = SBp("D_c", [128, NDT])
    A16_c = SBp("A16_c", [128, NDT, NST])
    sscale = SBp("sscale", [128, 1])
    r1hot = SBp("r1hot", [8, 1])
    pmask = SBp("pmask", [8, 1])
    wbt_s = SBp("wbt_s", [128, NDT, NST], BF16)
    wct_s = SBp("wct_s", [128, NDT, NST], BF16)
    wrt_s = SBp("wrt_s", [128, NDM, E])
    b1c = SBp("b1c", [128, NH])
    b2c = SBp("b2c", [128, NDM])
    g_t = SBp("g_t", [128, NTILE, 2])
    io128 = SBp("io128", [128, 128], I32)
    pidx = SBp("pidx", [128, 1], I32)
    pidxf = SBp("pidxf", [128, 1])
    zrow = SBp("zrow", [128, DM], BF16)

    CONST_VALS = [0.0, 1.0, 1e-6, 1e-5, 1e-30, LN20, -0.0505 / 0.099]
    const_pool = SBp("const_pool", [128, len(CONST_VALS)])
    for i, v in enumerate(CONST_VALS):
        nc.const_aps.register(F32, v, const_pool[:, i:i + 1])

    pMM = PSp("pMM", [128, 512])
    pMM2 = PSp("pMM2", [128, 512])
    pMM3 = PSp("pMM3", [128, 512])
    pStat = PSp("pStat", [1, 512])
    pStat2 = PSp("pStat2", [1, 512])
    pBC = PSp("pBC", [128, 512])
    pBC2 = PSp("pBC2", [128, 512])
    pT = PSp("pT", [128, 128])

    for i, v in enumerate(CONST_VALS):
        gop(lambda e, i=i, v=v: e.memset(const_pool[:, i:i + 1], v),
            [], [const_pool[:, i:i + 1]], "cpool")
    gop(lambda e: e.iota(io128[:], [[1, 128]], channel_multiplier=0),
        [], [io128[:]], "iota128")
    gop(lambda e: e.iota(pidx[:], [[1, 1]], channel_multiplier=1),
        [], [pidx[:]], "iotap")
    gop(lambda e: e.memset(ones_f[:], 1.0), [], [ones_f[:]], "ones")
    gop(lambda e: e.memset(zrow[:], 0.0), [], [zrow[:]], "zrow")
    vop(lambda e: e.tensor_copy(ident_f[:], io128[:]), [io128[:]], [ident_f[:]],
        "io_f")
    vop(lambda e: e.tensor_copy(pidxf[:], pidx[:]), [pidx[:]], [pidxf[:]],
        "pi_f")
    vop(lambda e: e.tensor_scalar(ident_f[:], ident_f[:], pidxf[:, 0:1], None,
                                  ALU.is_equal),
        [ident_f[:], pidxf[:]], [ident_f[:]], "ident")
    vop(lambda e: e.tensor_copy(ident_bf[:], ident_f[:]),
        [ident_f[:]], [ident_bf[:]], "identb")

    dma("sync", normw_c[:], norm_w_t)
    dma("sync", convw_c[:], conv_w_t)
    dma("sync", convb_c[:], conv_b_t)
    dma("sync", bdt_c[:], b_dt_t)
    dma("sync", D_c[:], D_t)
    dma("sync", r1hot[:], rank1hot)
    dma("sync", pmask[:], prefmask)
    dma("sync", b1c[:], b1_t)
    dma("sync", b2c[:], b2_t)
    dma("sync", wrt_s[:], w_router_i.rearrange("(c p) e -> p c e", p=128))

    mam = ExitStack()
    SB = lambda name, shape, dtype=F32: mam.enter_context(
        nc.sbuf_tensor(name, shape, dtype))

    atmp = SB("atmp", [128, NDT * NST])
    dma("sync", atmp[:].rearrange("p (a b) -> p a b", b=NST),
        A_log_t.rearrange("a p b -> p a b"))
    sop(lambda e: e.activation(atmp[:], atmp[:], AF.Tanh, scale=0.1),
        [atmp[:]], [atmp[:]], "A_tanh")
    sop(lambda e: e.activation(atmp[:], atmp[:], AF.Exp, scale=10.0),
        [atmp[:]], [atmp[:]], "A_exp")
    vop(lambda e: e.tensor_scalar(A16_c[:].rearrange("p a b -> p (a b)"),
                                  atmp[:], -1.0 / 16.0, None, ALU.mult),
        [atmp[:]], [A16_c[:]], "A_sc")

    stmp = SB("stmp", [1, 1])
    dma("sync", stmp[:], res_scale)
    sop(lambda e: e.activation(stmp[:], stmp[:], AF.Sigmoid),
        [stmp[:]], [stmp[:]], "sres")
    top(lambda e: e.matmul(pBC[:, 0:1], ones_f[0:1, :], stmp[:],
                           start=True, stop=True),
        [ones_f[:], stmp[:]], [pBC[:, 0:1]], "sres_bc")
    vop(lambda e: e.tensor_scalar(sscale[:], pBC[:, 0:1], 0.5, None, ALU.mult),
        [pBC[:, 0:1]], [sscale[:]], "sres_h")

    btmp = SB("btmp", [128, NDT * NST])
    for src, dst_s in ((w_BT, wbt_s), (w_CT, wct_s)):
        dma("sync", btmp[:].rearrange("p (a b) -> p a b", b=NST),
            src.rearrange("(a p) b -> p a b", p=128))
        vop(lambda e, dd=dst_s: e.tensor_copy(
            dd[:].rearrange("p a b -> p (a b)"), btmp[:]),
            [btmp[:]], [dst_s[:]], "wbc_bf")

    cast_st = SB("cast_st", [128, 1024])
    cast_ot = SB("cast_ot", [128, 1024], BF16)

    def cast_weight(src_rows, dst_slabs, n_c, n_m, queue, ceng, tag):
        srcv = src_rows.rearrange("(c p) (m f) -> m p c f", p=128, f=128)
        CB = 8
        for m in range(n_m):
            for c0 in range(0, n_c, CB):
                c1 = min(n_c, c0 + CB)
                ww = (c1 - c0) * 128
                stg = cast_st[:, 0:ww]
                outg = cast_ot[:, 0:ww]
                dma(queue, stg, srcv[m:m + 1, :, c0:c1, :], tag)
                ceng(lambda e, s=stg, o=outg: e.tensor_copy(o, s),
                     [stg], [outg], tag + "c")
                dma(queue, dst_slabs[m, :, c0 * 128:c1 * 128], outg,
                    tag + "o")

    cast_weight(w_inT, winT_bf, NDM, 2 * NDT, "sync", vop, "win")
    cast_weight(w_dtT, wdtT_bf, NDT, NDT, "sync", vop, "wdt")
    cast_weight(w_outT, woutT_bf, NDT, NDM, "sync", vop, "wou")

    xb_s = SB("xb_s", [128, NDM, MAXC])
    xnxc_bf = SB("xnxc_bf", [128, NDT, MAXC], BF16)
    xc_s = SB("xc_s", [128, NDT, MAXC])
    dt_s = SB("dt_s", [128, NDT, MAXC])
    dtx_s = SB("dtx_s", [128, NDT, MAXC])
    gsilu_bf = SB("gsilu_bf", [128, NDT, MAXC], BF16)
    bbc_s = SB("bbc_s", [128, NST, MAXC])
    cbc_s = SB("cbc_s", [128, NST, MAXC])
    SEGW = NDT * NST * R11
    a_buf = SB("a_buf", [128, 2 * SEGW])
    u_buf = SB("u_buf", [128, SEGW])
    h_buf = SB("h_buf", [128, SEGW])
    ybf_s = SB("ybf_s", [128, NDT, MAXC], BF16)
    conv_sc = SB("conv_sc", [128, MAXC + 3])
    acc0_sc = SB("acc0_sc", [128, MAXC])
    acc1_sc = SB("acc1_sc", [128, MAXC])
    sq_sc = SB("sq_sc", [128, MAXC])
    ysc_s = SB("ysc_s", [128, NST * NDT * 10])
    xtail_s = SB("xtail_s", [128, NDT, 3])
    wm_st = SB("wm_st", [128, 3, NDT * 128], BF16)
    brow_s = SB("brow_s", [NST, MAXC])
    crow_s = SB("crow_s", [NST, MAXC])
    lrow_s = SB("lrow_s", [E, MAXC])
    row_sc = SB("row_sc", [1, MAXC])
    row_sc2 = SB("row_sc2", [1, MAXC])
    sqcol = SB("sqcol", [128, 1])
    nsc_s = SB("nsc_s", [128, 1])
    bscr2 = SB("bscr2", [128, MAXC])
    fac_s = SB("fac_s", [128, 1])
    xfrow_st = SB("xfrow_st", [128, 128])

    MAGIC = 1597463007

    def newton_rsqrt(buf, scr, np_, w, tag):
        bi = buf.bitcast(I32)
        si = scr.bitcast(I32)
        t2 = bscr2[0:np_, 0:w]
        vop(lambda e: e.tensor_scalar(si, bi, 1, None,
                                      ALU.arith_shift_right),
            [buf], [scr], tag + "_sh")
        vop(lambda e: e.tensor_scalar(si, si, -1, MAGIC, ALU.mult, ALU.add),
            [scr], [scr], tag + "_sd")
        for it in range(2):
            vop(lambda e: e.scalar_tensor_tensor(t2, scr, 1.0,
                                                 scr, ALU.bypass, ALU.mult),
                [scr], [t2], tag + "_y2")
            vop(lambda e: e.tensor_tensor(t2, t2, buf, ALU.mult),
                [t2, buf], [t2], tag + "_sy")
            vop(lambda e: e.tensor_scalar(t2, t2, -0.5, 1.5,
                                          ALU.mult, ALU.add),
                [t2], [t2], tag + "_ht")
            vop(lambda e: e.tensor_tensor(scr, scr, t2, ALU.mult),
                [scr, t2], [scr], tag + "_yn")
        vop(lambda e: e.tensor_copy(buf, scr), [scr], [buf], tag + "_cp")

    vop(lambda e: e.memset(u_buf[:], 0.0), [], [u_buf[:]], "zc")
    vop(lambda e: e.memset(ysc_s[:], 0.0), [], [ysc_s[:]], "zy")
    vop(lambda e: e.memset(h_buf[:], 0.0), [], [h_buf[:]], "zc2")
    vop(lambda e: e.memset(a_buf[:], 0.0), [], [a_buf[:]], "za")

    for r0 in range(0, E * CAP, 128):
        dma("gpsimd", slab_d[r0:r0 + 128, :], zrow[:], "zslab")

    seg_ptr = [0]
    seg_par = [0]

    def emit_chunk(ci, t0, tn):
        for d in range(NDM):
            dma("sync", xb_s[:, d, 0:tn],
                x_t[d * 128:(d + 1) * 128, t0:t0 + tn], "x_in")
        for d in range(NDM):
            sop(lambda e, d=d: e.activation(sq_sc[:, 0:tn], xb_s[:, d, 0:tn],
                                            AF.Square),
                [xb_s[:, d, 0:tn]], [sq_sc[:, 0:tn]], "rms_sq")
            top(lambda e, d=d: e.matmul(pStat[:, 0:tn], ones_f[:, 0:1],
                                        sq_sc[:, 0:tn], start=(d == 0),
                                        stop=(d == NDM - 1)),
                [ones_f[:], sq_sc[:, 0:tn]], [pStat[:, 0:tn]], "rms_mm")
        vop(lambda e: e.tensor_scalar(row_sc[:, 0:tn], pStat[0:1, 0:tn],
                                      1.0 / DM, 1e-6, ALU.mult, ALU.add),
            [pStat[:, 0:tn]], [row_sc[:, 0:tn]], "rms_v")
        newton_rsqrt(row_sc[0:1, 0:tn], row_sc2[0:1, 0:tn], 1, tn, "rms_n")
        top(lambda e: e.matmul(pBC[:, 0:tn], ones_f[0:1, :], row_sc[:, 0:tn],
                               start=True, stop=True),
            [ones_f[:], row_sc[:, 0:tn]], [pBC[:, 0:tn]], "rms_bc")
        for d in range(NDM):
            vop(lambda e, d=d: e.scalar_tensor_tensor(
                xnxc_bf[:, d, 0:tn], xb_s[:, d, 0:tn], normw_c[:, d:d + 1],
                pBC[:, 0:tn], ALU.mult, ALU.mult),
                [xb_s[:, d, 0:tn], normw_c[:], pBC[:, 0:tn]],
                [xnxc_bf[:, d, 0:tn]], "xn")

        for m in range(2 * NDT):
            par = m % 3
            wslab = wm_st[:, par, 0:NDM * 128]
            dma("sync" if m % 2 == 0 else "scalar", wslab,
                winT_bf[m, :, :], "wl")
            pj = (pMM, pMM2, pMM3)[m % 3]
            for c in range(NDM):
                top(lambda e, c=c, pj=pj, ws=wslab: e.matmul(
                    pj[:, 0:tn], ws[:, c * 128:(c + 1) * 128],
                    xnxc_bf[:, c, 0:tn], start=(c == 0), stop=(c == NDM - 1)),
                    [wslab, xnxc_bf[:, c, 0:tn]], [pj[:, 0:tn]], "pj_mm")
            if m >= NDT:
                g = m - NDT
                sop(lambda e, g=g, pj=pj: e.activation(
                    gsilu_bf[:, g, 0:tn], pj[:, 0:tn], AF.Silu),
                    [pj[:, 0:tn]], [gsilu_bf[:, g, 0:tn]], "gsilu")
            else:
                d = m
                vop(lambda e, pj=pj: e.tensor_copy(conv_sc[:, 3:3 + tn],
                                                   pj[:, 0:tn]),
                    [pj[:, 0:tn]], [conv_sc[:, 3:3 + tn]], "cv_cp")
                if ci == 0:
                    vop(lambda e: e.memset(conv_sc[:, 0:3], 0.0),
                        [], [conv_sc[:, 0:3]], "cv_z")
                else:
                    vop(lambda e, d=d: e.tensor_copy(conv_sc[:, 0:3],
                                                     xtail_s[:, d, :]),
                        [xtail_s[:, d, :]], [conv_sc[:, 0:3]], "cv_t")
                vop(lambda e, d=d: e.tensor_copy(xtail_s[:, d, :],
                                                 conv_sc[:, tn:tn + 3]),
                    [conv_sc[:, 0:tn + 3]], [xtail_s[:, d, :]], "cv_st")
                vop(lambda e, d=d: e.tensor_scalar(
                    acc0_sc[:, 0:tn], conv_sc[:, 0:tn], convw_c[:, d, 0:1],
                    convb_c[:, d:d + 1], ALU.mult, ALU.add),
                    [conv_sc[:, 0:tn + 3], convw_c[:], convb_c[:]],
                    [acc0_sc[:, 0:tn]], "cv0")
                lastacc = acc0_sc
                for k in range(1, 4):
                    src, dst = ((acc0_sc, acc1_sc) if k % 2 else
                                (acc1_sc, acc0_sc))
                    vop(lambda e, d=d, k=k, s=src, ds=dst:
                        e.scalar_tensor_tensor(
                            ds[:, 0:tn], conv_sc[:, k:k + tn],
                            convw_c[:, d, k:k + 1], s[:, 0:tn],
                            ALU.mult, ALU.add),
                        [conv_sc[:, 0:tn + 3], convw_c[:], src[:, 0:tn]],
                        [dst[:, 0:tn]], "cvk")
                    lastacc = dst
                sop(lambda e, d=d, la=lastacc: e.activation(
                    xc_s[:, d, 0:tn], la[:, 0:tn], AF.Silu),
                    [lastacc[:, 0:tn]], [xc_s[:, d, 0:tn]], "xc_silu")

        for d in range(NDT):
            vop(lambda e, d=d: e.tensor_copy(xnxc_bf[:, d, 0:tn],
                                             xc_s[:, d, 0:tn]),
                [xc_s[:, d, 0:tn]], [xnxc_bf[:, d, 0:tn]], "xcbf")

        for wt, pp, rows, rowd, bc in ((wbt_s, pBC, brow_s, brow_d, bbc_s),
                                       (wct_s, pBC2, crow_s, crow_d, cbc_s)):
            for c in range(NDT):
                top(lambda e, c=c, wt=wt, pp=pp: e.matmul(
                    pp[0:NST, 0:tn], wt[:, c, :], xnxc_bf[:, c, 0:tn],
                    start=(c == 0), stop=(c == NDT - 1)),
                    [wt[:], xnxc_bf[:, c, 0:tn]], [pp[:, 0:tn]], "bc_mm")
            vop(lambda e, pp=pp, rows=rows: e.tensor_copy(
                rows[:, 0:tn], pp[0:NST, 0:tn]),
                [pp[:, 0:tn]], [rows[:, 0:tn]], "bc_ev")
            dma("scalar", rowd[:, 0:tn], rows[:, 0:tn], "bc_o")
            dma("scalar", bc[:, :, 0:tn],
                rowd[:, 0:tn].unsqueeze(0).broadcast_to([128, NST, tn]),
                "bc_r")

        for m in range(NDT):
            par = m % 3
            wslab = wm_st[:, par, 0:NDT * 128]
            dma("sync" if m % 2 == 0 else "scalar", wslab,
                wdtT_bf[m, :, :], "wl2")
            pj = (pMM, pMM2, pMM3)[m % 3]
            for c in range(NDT):
                top(lambda e, c=c, pj=pj, ws=wslab: e.matmul(
                    pj[:, 0:tn], ws[:, c * 128:(c + 1) * 128],
                    xnxc_bf[:, c, 0:tn], start=(c == 0), stop=(c == NDT - 1)),
                    [wslab, xnxc_bf[:, c, 0:tn]], [pj[:, 0:tn]], "dt_mm")
            sop(lambda e, m=m, pj=pj: e.activation(
                dt_s[:, m, 0:tn], pj[:, 0:tn], AF.Exp,
                bias=bdt_c[:, m:m + 1]),
                [pj[:, 0:tn], bdt_c[:]], [dt_s[:, m, 0:tn]], "sp_e")
            sop(lambda e, m=m: e.activation(dt_s[:, m, 0:tn],
                                            dt_s[:, m, 0:tn], AF.Ln,
                                            bias=1.0),
                [dt_s[:, m, 0:tn]], [dt_s[:, m, 0:tn]], "sp_l")
        for m in range(NDT):
            sop(lambda e, m=m: e.activation(dt_s[:, m, 0:tn],
                                            dt_s[:, m, 0:tn],
                                            AF.Tanh, scale=1.0 / 0.099,
                                            bias=-0.0505 / 0.099),
                [dt_s[:, m, 0:tn]], [dt_s[:, m, 0:tn]], "sp_t")
            vop(lambda e, m=m: e.tensor_scalar(
                dt_s[:, m, 0:tn], dt_s[:, m, 0:tn], 0.099, 0.0505,
                ALU.mult, ALU.add),
                [dt_s[:, m, 0:tn]], [dt_s[:, m, 0:tn]], "dt_a")
            vop(lambda e, m=m: e.tensor_mul(dtx_s[:, m, 0:tn],
                                            dt_s[:, m, 0:tn],
                                            xc_s[:, m, 0:tn]),
                [dt_s[:, m, 0:tn], xc_s[:, m, 0:tn]],
                [dtx_s[:, m, 0:tn]], "dtx")
        for d in range(NDT):
            vop(lambda e, d=d: e.tensor_scalar(
                xc_s[:, d, 0:tn], xc_s[:, d, 0:tn], D_c[:, d:d + 1], None,
                ALU.mult),
                [xc_s[:, d, 0:tn], D_c[:]], [xc_s[:, d, 0:tn]], "dxc")

        def a_slice(par):
            return a_buf[:, par * SEGW:(par + 1) * SEGW]

        def a_prep(s0, sl, par):
            toff = s0 - t0
            ab = a_slice(par).rearrange("p (d n s) -> p d n s", n=NST, s=R11)
            dt4 = dt_s[:, :, toff:toff + sl].unsqueeze(2).broadcast_to(
                [128, NDT, NST, sl])
            a164 = A16_c[:].unsqueeze(3).broadcast_to([128, NDT, NST, sl])
            vop(lambda e: e.tensor_tensor(ab[:, :, :, 1:1 + sl], dt4, a164,
                                          ALU.mult),
                [dt_s[:], A16_c[:]], [a_slice(par)], "a_w")
            sop(lambda e: e.activation(ab[:, :, :, 1:1 + sl],
                                       ab[:, :, :, 1:1 + sl], AF.Tanh,
                                       scale=1.0 / 16.0),
                [a_slice(par)], [a_slice(par)], "a_t")
            sop(lambda e: e.activation(ab[:, :, :, 1:1 + sl],
                                       ab[:, :, :, 1:1 + sl], AF.Exp,
                                       scale=16.0),
                [a_slice(par)], [a_slice(par)], "a_e")

        csegs = []
        while seg_ptr[0] < len(cfg.segs):
            s0, sl, rs = cfg.segs[seg_ptr[0]]
            if s0 >= t0 + tn:
                break
            assert s0 >= t0 and s0 + sl <= t0 + tn
            csegs.append((s0, sl, rs))
            seg_ptr[0] += 1

        def y_finish(par, toff, sl):
            DT10 = NDT * 10
            for lev, half in ((0, 8), (1, 4), (2, 2), (3, 1)):
                w = half * DT10
                vop(lambda e, w=w: e.tensor_add(
                    ysc_s[:, 0:w], ysc_s[:, 0:w], ysc_s[:, w:2 * w]),
                    [ysc_s[:, 0:2 * w]], [ysc_s[:, 0:w]], "ytr")
            ydt = bscr2[:, 0:NDT * sl].rearrange("p (d t) -> p d t", t=sl)
            ysum = ysc_s[:, 0:DT10].rearrange(
                "p (d t) -> p d t", t=10)[:, :, 0:sl]
            dxc4 = xc_s[:, :, toff:toff + sl]
            vop(lambda e: e.tensor_tensor(ydt, ysum, dxc4, ALU.add),
                [ysc_s[:, 0:DT10], xc_s[:]], [bscr2[:, 0:NDT * sl]], "yD")
            g4 = gsilu_bf[:, :, toff:toff + sl]
            yb4 = ybf_s[:, :, toff:toff + sl]
            vop(lambda e: e.tensor_tensor(yb4, ydt, g4, ALU.mult),
                [bscr2[:, 0:NDT * sl], gsilu_bf[:]], [ybf_s[:]], "yg")

        pending_y = None
        for si, (s0, sl, rs) in enumerate(csegs):
            par = seg_par[0]
            seg_par[0] = 1 - par
            uh = u_buf if par == 0 else h_buf
            uhn = h_buf if par == 0 else u_buf
            if si == 0:
                a_prep(s0, sl, par)
            toff = s0 - t0
            ub = uh[:].rearrange("p (d n s) -> p d n s", n=NST, s=R11)
            ubn = uhn[:].rearrange("p (d n s) -> p d n s", n=NST, s=R11)
            dtx4 = dtx_s[:, :, toff:toff + sl].unsqueeze(2).broadcast_to(
                [128, NDT, NST, sl])
            bb4 = bbc_s[:, :, toff:toff + sl].unsqueeze(1).broadcast_to(
                [128, NDT, NST, sl])
            vop(lambda e, dtx4=dtx4, bb4=bb4, ub=ub, sl=sl: e.tensor_tensor(
                ub[:, :, :, 1:1 + sl], dtx4, bb4, ALU.mult),
                [dtx_s[:], bbc_s[:]], [uh[:]], "u_w")
            if si + 1 < len(csegs):
                n0, nl, _ = csegs[si + 1]
                a_prep(n0, nl, 1 - par)
            vop(lambda e, par=par, uh=uh: e.tensor_tensor_scan(
                uh[:], a_slice(par), uh[:], 0.0, ALU.mult, ALU.add),
                [a_slice(par), uh[:]], [uh[:]], "scan")
            if pending_y is not None:
                y_finish(*pending_y)
                pending_y = None
            hcols = ub[:, :, :, sl:sl + 1]
            if rs:
                vop(lambda e, hcols=hcols: e.scalar_tensor_tensor(
                    sq_sc[:, 0:NDT * NST].rearrange(
                        "p (d n) -> p d n", n=NST).unsqueeze(3),
                    hcols, 1.0, hcols, ALU.bypass, ALU.mult,
                    accum_out=sqcol[:]),
                    [uh[:]], [sq_sc[:, 0:NDT * NST], sqcol[:]], "r_sq")
                top(lambda e: e.matmul(pBC2[:, 0:1], ones_f[:], sqcol[:],
                                       start=True, stop=True),
                    [ones_f[:], sqcol[:]], [pBC2[:, 0:1]], "r_nm")
                vop(lambda e: e.tensor_scalar(fac_s[:], pBC2[:, 0:1],
                                              1.0 / 400.0, None, ALU.mult),
                    [pBC2[:, 0:1]], [fac_s[:]], "r_cp")
                newton_rsqrt(fac_s[:, 0:1], nsc_s[:, 0:1], 128, 1, "r_n")
                vop(lambda e: e.tensor_scalar_min(fac_s[:], fac_s[:], 1.0),
                    [fac_s[:]], [fac_s[:]], "r_mn")
                vop(lambda e, hcols=hcols: e.tensor_scalar(
                    hcols, hcols, fac_s[:, 0:1], None, ALU.mult),
                    [uh[:], fac_s[:]], [uh[:]], "r_ap")
            vop(lambda e, hcols=hcols, ubn=ubn: e.tensor_copy(
                ubn[:, :, :, 0:1], hcols),
                [uh[:]], [uhn[:]], "carry")
            h_ndt = ub[:, :, :, 1:1 + sl].transpose([0, 2, 1, 3])
            c_ndt = cbc_s[:, :, toff:toff + sl].unsqueeze(1).broadcast_to(
                [128, NDT, NST, sl]).transpose([0, 2, 1, 3])
            yh = ysc_s[:].rearrange("p (n d t) -> p n d t", d=NDT, t=10)
            vop(lambda e, h_ndt=h_ndt, c_ndt=c_ndt, yh=yh, sl=sl:
                e.tensor_tensor(yh[:, :, :, 0:sl], h_ndt, c_ndt, ALU.mult),
                [uh[:], cbc_s[:]], [ysc_s[:]], "yhat")
            pending_y = (par, toff, sl)
        if pending_y is not None:
            y_finish(*pending_y)
            pending_y = None

        for m in range(NDM):
            par = m % 3
            wslab = wm_st[:, par, 0:NDT * 128]
            dma("sync" if m % 2 == 0 else "scalar", wslab,
                woutT_bf[m, :, :], "wl3")
            pj = (pMM, pMM2, pMM3)[m % 3]
            for c in range(NDT):
                top(lambda e, c=c, pj=pj, ws=wslab: e.matmul(
                    pj[:, 0:tn], ws[:, c * 128:(c + 1) * 128],
                    ybf_s[:, c, 0:tn], start=(c == 0), stop=(c == NDT - 1)),
                    [wslab, ybf_s[:, c, 0:tn]], [pj[:, 0:tn]], "wo_mm")
            vop(lambda e, m=m: e.tensor_scalar(
                xb_s[:, m, 0:tn], xb_s[:, m, 0:tn], 2.0, None, ALU.mult),
                [xb_s[:, m, 0:tn]], [xb_s[:, m, 0:tn]], "x2x")
            vop(lambda e, m=m, pj=pj: e.scalar_tensor_tensor(
                xb_s[:, m, 0:tn], pj[:, 0:tn], sscale[:, 0:1],
                xb_s[:, m, 0:tn], ALU.mult, ALU.add),
                [pj[:, 0:tn], sscale[:], xb_s[:, m, 0:tn]],
                [xb_s[:, m, 0:tn]], "resid")
        for d in range(NDM):
            top(lambda e, d=d: e.matmul(pStat[:, 0:tn], ones_f[:, 0:1],
                                        xb_s[:, d, 0:tn], start=(d == 0),
                                        stop=(d == NDM - 1)),
                [ones_f[:], xb_s[:, d, 0:tn]], [pStat[:, 0:tn]], "l1_s")
            sop(lambda e, d=d: e.activation(sq_sc[:, 0:tn], xb_s[:, d, 0:tn],
                                            AF.Square),
                [xb_s[:, d, 0:tn]], [sq_sc[:, 0:tn]], "l1_q")
            top(lambda e, d=d: e.matmul(pStat2[:, 0:tn], ones_f[:, 0:1],
                                        sq_sc[:, 0:tn], start=(d == 0),
                                        stop=(d == NDM - 1)),
                [ones_f[:], sq_sc[:, 0:tn]], [pStat2[:, 0:tn]], "l1_q2")
        sop(lambda e: e.activation(row_sc[:, 0:tn], pStat[0:1, 0:tn], AF.Copy,
                                   scale=1.0 / DM),
            [pStat[:, 0:tn]], [row_sc[:, 0:tn]], "l1_m")
        sop(lambda e: e.activation(row_sc2[:, 0:tn], row_sc[:, 0:tn],
                                   AF.Square),
            [row_sc[:, 0:tn]], [row_sc2[:, 0:tn]], "l1_m2")
        top(lambda e: e.matmul(pBC[:, 0:tn], ones_f[0:1, :], row_sc[:, 0:tn],
                               start=True, stop=True),
            [ones_f[:], row_sc[:, 0:tn]], [pBC[:, 0:tn]], "l1_bm")
        vop(lambda e: e.scalar_tensor_tensor(
            row_sc[:, 0:tn], pStat2[0:1, 0:tn], 1.0 / DM, row_sc2[:, 0:tn],
            ALU.mult, ALU.subtract),
            [pStat2[:, 0:tn], row_sc2[:, 0:tn]], [row_sc[:, 0:tn]], "l1_v")
        vop(lambda e: e.tensor_scalar(row_sc[:, 0:tn], row_sc[:, 0:tn],
                                      1.0, 1e-5, ALU.mult, ALU.add),
            [row_sc[:, 0:tn]], [row_sc[:, 0:tn]], "l1_e")
        newton_rsqrt(row_sc[0:1, 0:tn], row_sc2[0:1, 0:tn], 1, tn, "l1_n")
        top(lambda e: e.matmul(pBC2[:, 0:tn], ones_f[0:1, :],
                               row_sc[:, 0:tn], start=True, stop=True),
            [ones_f[:], row_sc[:, 0:tn]], [pBC2[:, 0:tn]], "l1_br")
        for d in range(NDM):
            vop(lambda e, d=d: e.tensor_sub(xb_s[:, d, 0:tn],
                                            xb_s[:, d, 0:tn], pBC[:, 0:tn]),
                [xb_s[:, d, 0:tn], pBC[:, 0:tn]], [xb_s[:, d, 0:tn]], "xf_c")
            vop(lambda e, d=d: e.tensor_mul(xb_s[:, d, 0:tn],
                                            xb_s[:, d, 0:tn], pBC2[:, 0:tn]),
                [xb_s[:, d, 0:tn], pBC2[:, 0:tn]], [xb_s[:, d, 0:tn]],
                "xf_r")
        for c in range(NDM):
            top(lambda e, c=c: e.matmul(
                pBC[0:E, 0:tn], wrt_s[:, c, :], xb_s[:, c, 0:tn],
                start=(c == 0), stop=(c == NDM - 1)),
                [wrt_s[:], xb_s[:, c, 0:tn]], [pBC[:, 0:tn]], "rt_mm")
        vop(lambda e: e.tensor_copy(lrow_s[:, 0:tn], pBC[0:E, 0:tn]),
            [pBC[:, 0:tn]], [lrow_s[:, 0:tn]], "rt_ev")
        dma("scalar", logit_d[:, t0:t0 + tn], lrow_s[:, 0:tn], "rt_o")
        nsub = (tn + 127) // 128
        for s in range(nsub):
            w = min(128, tn - s * 128)
            for d in range(NDM):
                top(lambda e, s=s, d=d, w=w: e.transpose(
                    pT[0:w, :], xb_s[:, d, s * 128:s * 128 + w], ident_f[:]),
                    [xb_s[:, d, 0:tn], ident_f[:]], [pT[:, :]], "xf_tp")
                vop(lambda e, d=d, w=w: e.tensor_copy(
                    xfrow_st[0:w, :], pT[0:w, :]),
                    [pT[:, :]], [xfrow_st[:, :]], "xf_ev")
                dma("scalar",
                    xf_d[t0 + s * 128:t0 + s * 128 + w,
                         d * 128:(d + 1) * 128],
                    xfrow_st[0:w, :], "xf_o")

    moe_cast_done = [0]

    for ci, (t0, tn) in enumerate(cfg.chunks):
        emit_chunk(ci, t0, tn)
        if ci == 1:
            cast_weight(w1_my, w1_bf, NDM, NH, "scalar", vop, "w1")
            moe_cast_done[0] |= 1
        elif ci == 3:
            cast_weight(w2_my, w2_bf, NH, NDM, "scalar", vop, "w2")
            moe_cast_done[0] |= 2
    if not moe_cast_done[0] & 1:
        cast_weight(w1_my, w1_bf, NDM, NH, "scalar", vop, "w1")
    if not moe_cast_done[0] & 2:
        cast_weight(w2_my, w2_bf, NH, NDM, "scalar", vop, "w2")

    mam.close()
    P.barrier()

    moe = ExitStack()
    SM = lambda name, shape, dtype=F32: moe.enter_context(
        nc.sbuf_tensor(name, shape, dtype))
    rt = ExitStack()
    SR = lambda name, shape, dtype=F32: rt.enter_context(
        nc.sbuf_tensor(name, shape, dtype))

    lgrow = SR("lgrow", [E, L])
    lg_t = SR("lg_t", [128, NTILE, E])
    mx_t = SR("mx_t", [128, NTILE, 8])
    ix_t = SR("ix_t", [128, NTILE, 8], U32)
    e_t = SR("e_t", [128, NTILE, 2])
    dma("sync", lgrow[:], logit_d, "lg_i")
    for i in range(NTILE):
        top(lambda e, i=i: e.transpose(pT[:, 0:E],
                                       lgrow[:, i * 128:(i + 1) * 128],
                                       ident_f[0:E, 0:E]),
            [lgrow[:], ident_f[:]], [pT[:, :]], "lg_tp")
        vop(lambda e, i=i: e.tensor_copy(lg_t[:, i, :], pT[:, 0:E]),
            [pT[:, :]], [lg_t[:, i, :]], "lg_ev")
    for i in range(NTILE):
        vop(lambda e, i=i: e.max(out=mx_t[:, i, :], in_=lg_t[:, i, :]),
            [lg_t[:, i, :]], [mx_t[:, i, :]], "top8")
        vop(lambda e, i=i: e.max_index(out=ix_t[:, i, :],
                                       in_max=mx_t[:, i, :],
                                       in_values=lg_t[:, i, :]),
            [mx_t[:, i, :], lg_t[:, i, :]], [ix_t[:, i, :]], "topi")
        vop(lambda e, i=i: e.tensor_copy(e_t[:, i, :], ix_t[:, i, 0:2]),
            [ix_t[:, i, :]], [e_t[:, i, :]], "topc")
        vop(lambda e, i=i: e.tensor_sub(g_t[:, i, 0:1], mx_t[:, i, 0:1],
                                        mx_t[:, i, 1:2]),
            [mx_t[:, i, :]], [g_t[:, i, 0:1]], "gd")
    sop(lambda e: e.activation(
        g_t[:].rearrange("p a b -> p a b")[:, :, 0:1],
        g_t[:].rearrange("p a b -> p a b")[:, :, 0:1], AF.Sigmoid),
        [g_t[:]], [g_t[:]], "gsig")
    vop(lambda e: e.tensor_scalar(
        g_t[:].rearrange("p a b -> p a b")[:, :, 1:2],
        g_t[:].rearrange("p a b -> p a b")[:, :, 0:1], -1.0, 1.0,
        ALU.mult, ALU.add),
        [g_t[:]], [g_t[:]], "g1m")

    dma("gpsimd", slot_d[0, :].rearrange("(a p b) -> p a b", p=128, b=2),
        e_t[:], "ei_o")
    SC = 512
    NSC = (2 * L + SC - 1) // SC
    cs_d = dscr("cs_d", [8, 2 * L])
    oh_s = SR("oh_s", [8, SC])
    cs_s = SR("cs_s", [8, SC])
    zz_s = SR("zz_s", [8, SC])
    eidx_s = SR("eidx_s", [8, SC])
    tmp8 = SR("tmp8", [8, SC])
    carry8 = SR("carry8", [8, 1])
    ecol = SR("ecol", [8, 1], I32)
    ecolf = SR("ecolf", [8, 1])
    gop(lambda e: e.iota(ecol[:], [[1, 1]], channel_multiplier=1),
        [], [ecol[:]], "e_io")
    vop(lambda e: e.tensor_copy(ecolf[:], ecol[:]), [ecol[:]], [ecolf[:]],
        "e_iof")
    gop(lambda e: e.memset(zz_s[:], 0.0), [], [zz_s[:]], "zz")
    vop(lambda e: e.memset(carry8[:], 0.0), [], [carry8[:]], "zc8")

    def slot_chunk_oh_cs(q, wq):
        dma("sync", eidx_s[:, 0:wq],
            slot_d[0:1, q * SC:q * SC + wq].broadcast_to([8, wq]), "ei_i")
        vop(lambda e, wq=wq: e.tensor_scalar(
            oh_s[:, 0:wq], eidx_s[:, 0:wq], ecolf[:, 0:1], None,
            ALU.is_equal),
            [eidx_s[:, 0:wq], ecolf[:]], [oh_s[:, 0:wq]], "oh")
        vop(lambda e, wq=wq: e.tensor_tensor_scan(
            cs_s[:, 0:wq], oh_s[:, 0:wq], zz_s[:, 0:wq], carry8[:, 0:1],
            ALU.add, ALU.add),
            [oh_s[:, 0:wq], zz_s[:, 0:wq], carry8[:]], [cs_s[:, 0:wq]],
            "cscan")
        vop(lambda e, wq=wq: e.tensor_copy(carry8[:], cs_s[:, wq - 1:wq]),
            [cs_s[:, 0:wq]], [carry8[:]], "csc")

    for q in range(NSC):
        wq = min(SC, 2 * L - q * SC)
        slot_chunk_oh_cs(q, wq)
        dma("gpsimd", cs_d[:, q * SC:q * SC + wq], cs_s[:, 0:wq], "cs_o")
    cnt_s = SR("cnt_s", [8, 8])
    cntall = SR("cntall", [8, 8])
    offm = SR("offm", [8, 8])
    offcol = SR("offcol", [8, 1])
    dma("gpsimd", meta_d[0, 0:8].rearrange("(a b) -> a b", a=8),
        carry8[:], "cn_o")
    dma("sync", cnt_s[:], meta_d[0:1, 0:8].broadcast_to([8, 8]), "cn_i")
    vop(lambda e: e.tensor_scalar(cnt_s[:], cnt_s[:], r1hot[:, 0:1], None,
                                  ALU.mult),
        [cnt_s[:], r1hot[:]], [cnt_s[:]], "cn_m")
    dma("gpsimd", cc_src, cnt_s[:], "cc_o")
    P.op("gpsimd", lambda e: e.collective_compute(
        "AllReduce", ALU.add, replica_groups=[list(range(n_cores))],
        ins=[cc_src.opt()], outs=[cc_dst.opt()]),
        [cc_src], [cc_dst], compl=("cc", "gpsimd"), tag="ccar")
    dma("gpsimd", cntall[:], cc_dst, "cc_i")
    vop(lambda e: e.tensor_scalar(offm[:], cntall[:], pmask[:, 0:1], None,
                                  ALU.mult),
        [cntall[:], pmask[:]], [offm[:]], "of_m")
    top(lambda e: e.matmul(pBC[0:8, 0:1], offm[:], ones_f[0:8, 0:1],
                           start=True, stop=True),
        [offm[:], ones_f[:]], [pBC[:, 0:1]], "of_s")
    vop(lambda e: e.tensor_copy(offcol[:], pBC[0:8, 0:1]),
        [pBC[:, 0:1]], [offcol[:]], "of_e")
    selrow = SR("selrow", [1, SC])
    keep_r = SR("keep_r", [1, SC])
    lpos_r = SR("lpos_r", [1, SC])
    gidx_r = SR("gidx_r", [1, SC])
    sidx_r = SR("sidx_r", [1, SC])
    for q in range(NSC):
        wq = min(SC, 2 * L - q * SC)
        dma("sync", eidx_s[:, 0:wq],
            slot_d[0:1, q * SC:q * SC + wq].broadcast_to([8, wq]), "ei_i2")
        vop(lambda e, wq=wq: e.tensor_scalar(
            oh_s[:, 0:wq], eidx_s[:, 0:wq], ecolf[:, 0:1], None,
            ALU.is_equal),
            [eidx_s[:, 0:wq], ecolf[:]], [oh_s[:, 0:wq]], "oh2")
        dma("sync", cs_s[:, 0:wq], cs_d[:, q * SC:q * SC + wq], "cs_i")
        vop(lambda e, wq=wq: e.tensor_scalar(
            tmp8[:, 0:wq], cs_s[:, 0:wq], offcol[:, 0:1], None, ALU.add),
            [cs_s[:, 0:wq], offcol[:]], [tmp8[:, 0:wq]], "sl_a")
        vop(lambda e, wq=wq: e.tensor_tensor(
            tmp8[:, 0:wq], tmp8[:, 0:wq], oh_s[:, 0:wq], ALU.mult),
            [tmp8[:, 0:wq], oh_s[:, 0:wq]], [tmp8[:, 0:wq]], "sl_m")
        for j in range((wq + 511) // 512):
            w = min(512, wq - j * 512)
            top(lambda e, j=j, w=w: e.matmul(
                pStat[0:1, 0:w], ones_f[0:8, 0:1],
                tmp8[:, j * 512:j * 512 + w], start=True, stop=True),
                [ones_f[:], tmp8[:, 0:wq]], [pStat[:, :]], "sl_r")
            vop(lambda e, j=j, w=w: e.tensor_copy(
                selrow[:, j * 512:j * 512 + w], pStat[0:1, 0:w]),
                [pStat[:, :]], [selrow[:]], "sl_e")
        vop(lambda e, wq=wq: e.tensor_scalar(
            keep_r[:, 0:wq], selrow[:, 0:wq], CAP + 0.5, None, ALU.is_lt),
            [selrow[:, 0:wq]], [keep_r[:, 0:wq]], "kp")
        vop(lambda e, wq=wq: e.tensor_scalar(
            lpos_r[:, 0:wq], selrow[:, 0:wq], -1.0, None, ALU.add),
            [selrow[:, 0:wq]], [lpos_r[:, 0:wq]], "lp")
        vop(lambda e, wq=wq: e.tensor_tensor(
            lpos_r[:, 0:wq], lpos_r[:, 0:wq], keep_r[:, 0:wq], ALU.mult),
            [lpos_r[:, 0:wq], keep_r[:, 0:wq]], [lpos_r[:, 0:wq]], "lpk")
        vop(lambda e, wq=wq: e.scalar_tensor_tensor(
            gidx_r[:, 0:wq], eidx_s[0:1, 0:wq], float(CAP),
            lpos_r[:, 0:wq], ALU.mult, ALU.add),
            [eidx_s[:, 0:wq], lpos_r[:, 0:wq]], [gidx_r[:, 0:wq]], "gx")
        vop(lambda e, wq=wq: e.tensor_scalar(
            sidx_r[:, 0:wq], keep_r[:, 0:wq], -1e7, 1e7, ALU.mult, ALU.add),
            [keep_r[:, 0:wq]], [sidx_r[:, 0:wq]], "sx1")
        vop(lambda e, wq=wq: e.tensor_tensor(
            sidx_r[:, 0:wq], sidx_r[:, 0:wq], gidx_r[:, 0:wq], ALU.add),
            [sidx_r[:, 0:wq], gidx_r[:, 0:wq]], [sidx_r[:, 0:wq]], "sx2")
        dma("gpsimd", slot_d[1, q * SC:q * SC + wq], gidx_r[0:1, 0:wq], "gx_o")
        dma("gpsimd", slot_d[2, q * SC:q * SC + wq], sidx_r[0:1, 0:wq], "sx_o")
        dma("gpsimd", slot_d[3, q * SC:q * SC + wq], keep_r[0:1, 0:wq], "kp_o")
    keep_t = SR("keep_t", [128, NTILE, 2])
    dma("sync", keep_t[:],
        slot_d[3, :].rearrange("(a p b) -> p a b", p=128, b=2), "kp_i")
    vop(lambda e: e.tensor_tensor(g_t[:].rearrange("p a b -> p (a b)"),
                                  g_t[:].rearrange("p a b -> p (a b)"),
                                  keep_t[:].rearrange("p a b -> p (a b)"),
                                  ALU.mult),
        [g_t[:], keep_t[:]], [g_t[:]], "gk")
    sidx_tf = SR("sidx_tf", [128, NTILE, 2])
    sidx_t = SR("sidx_t", [128, NTILE, 2], I32)
    dma("sync", sidx_tf[:],
        slot_d[2, :].rearrange("(a p b) -> p a b", p=128, b=2), "sx_i")
    vop(lambda e: e.tensor_copy(sidx_t[:].rearrange("p a b -> p (a b)"),
                                sidx_tf[:].rearrange("p a b -> p (a b)")),
        [sidx_tf[:]], [sidx_t[:]], "sx_c")
    rowb = SR("rowb", [128, DM])
    rowbf = SR("rowbf", [128, DM], BF16)
    bnd_reg = moe.enter_context(nc.gpsimd.register("bnd_reg"))
    gop(lambda e: e.reg_mov(bnd_reg, E * CAP - 1), [], [], "bmov")
    for i in range(NTILE):
        dma("sync", rowb[:], xf_d[i * 128:(i + 1) * 128, :], "xf_i")
        vop(lambda e: e.tensor_copy(rowbf[:], rowb[:]),
            [rowb[:]], [rowbf[:]], "sc_b")
        for k in range(2):
            idma_scatter(slab_d, sidx_t[:, i, k:k + 1], rowbf[:],
                         bnd_reg, "scat")

    P.op("gpsimd", lambda e: e.collective_compute(
        "ReduceScatter", ALU.add, replica_groups=[list(range(n_cores))],
        ins=[slab_d.opt()], outs=[shard_d.opt()]),
        [slab_d], [shard_d], compl=("cc", "rs"), tag="ccrs")
    rt.close()

    ffn = ExitStack()
    SF = lambda name, shape, dtype=F32: ffn.enter_context(
        nc.sbuf_tensor(name, shape, dtype))
    xfT_e = SF("xfT_e", [128, NDM, CAP], BF16)
    hrc_s = SF("hrc_s", [128, NH, 512], BF16)
    ob_sc = SF("ob_sc", [128, NDM, 512])
    rowbB = SF("rowbB", [128, DM], BF16)
    orow_st = SF("orow_st", [128, DM])
    w1r = SF("w1r", [128, NH, NDM * 128], BF16)
    wm2b = SF("wm2b", [128, 2, NH * 128], BF16)
    NRC = (CAP + 511) // 512
    for m in range(NH):
        dma4(w1r[:, m, :], w1_bf[m, :, :], NDM * 128, "w1l",
             xreads=[slab_d])
    for r in range(CAP // 128):
        dma("sync", rowbB[:], shard_d[r * 128:(r + 1) * 128, :], "sb_i")
        for d in range(NDM):
            top(lambda e, d=d: e.transpose(
                pT.bitcast(BF16)[:, 0:128],
                rowbB[:, d * 128:(d + 1) * 128], ident_bf[:]),
                [rowbB[:], ident_bf[:]], [pT[:, :]], "sb_t")
            vop(lambda e, d=d, r=r: e.tensor_copy(
                xfT_e[:, d, r * 128:(r + 1) * 128],
                pT.bitcast(BF16)[:, 0:128]),
                [pT[:, :]], [xfT_e[:]], "sb_e")
    for rc in range(NRC):
        w = min(512, CAP - rc * 512)
        for mg in range(NH // 8):
            for mm in range(8):
                m = mg * 8 + mm
                wslab = w1r[:, m, :]
                pj = (pMM, pMM2, pMM3)[m % 3]
                for c in range(NDM):
                    top(lambda e, c=c, pj=pj, ws=wslab, rc=rc, w=w:
                        e.matmul(
                            pj[:, 0:w], ws[:, c * 128:(c + 1) * 128],
                            xfT_e[:, c, rc * 512:rc * 512 + w],
                            start=(c == 0), stop=(c == NDM - 1)),
                        [wslab, xfT_e[:]], [pj[:, 0:w]], "w1m")
                sop(lambda e, m=m, pj=pj, w=w: e.activation(
                    hrc_s[:, m, 0:w], pj[:, 0:w], AF.Relu,
                    bias=b1c[:, m:m + 1]),
                    [pj[:, 0:w], b1c[:]], [hrc_s[:, m, 0:w]], "rel")
        for m in range(NDM):
            par = m % 2
            wslab = wm2b[:, par, 0:NH * 128]
            dma4(wslab, w2_bf[m, :, :], NH * 128, "w2l",
                 xreads=[slab_d])
            pj = pMM if m % 2 == 0 else pMM2
            for c in range(NH):
                top(lambda e, c=c, pj=pj, ws=wslab, w=w: e.matmul(
                    pj[:, 0:w], ws[:, c * 128:(c + 1) * 128],
                    hrc_s[:, c, 0:w], start=(c == 0), stop=(c == NH - 1)),
                    [wslab, hrc_s[:]], [pj[:, 0:w]], "w2m")
            sop(lambda e, m=m, pj=pj, w=w: e.activation(
                ob_sc[:, m, 0:w], pj[:, 0:w], AF.Identity,
                bias=b2c[:, m:m + 1]),
                [pj[:, 0:w], b2c[:]], [ob_sc[:, m, 0:w]], "b2a")
        nsub = (w + 127) // 128
        for s2 in range(nsub):
            ww = min(128, w - s2 * 128)
            for d in range(NDM):
                top(lambda e, d=d, s2=s2, ww=ww: e.transpose(
                    pT[0:ww, :], ob_sc[:, d, s2 * 128:s2 * 128 + ww],
                    ident_f[:]),
                    [ob_sc[:], ident_f[:]], [pT[:, :]], "ob_t")
                vop(lambda e, d=d, ww=ww: e.tensor_copy(
                    orow_st[0:ww, d * 128:(d + 1) * 128], pT[0:ww, :]),
                    [pT[:, :]], [orow_st[:, :]], "ob_e")
            r0 = rc * 512 + s2 * 128
            dma("gpsimd", eout_my[r0:r0 + ww, :], orow_st[0:ww, :], "ob_o")

    P.op("gpsimd", lambda e: e.collective_compute(
        "AllGather", ALU.bypass, replica_groups=[list(range(n_cores))],
        ins=[eout_my.opt()], outs=[eout_d.opt()]),
        [eout_my], [eout_d], compl=("cc", "ag"), tag="ccag")
    ffn.close()

    gidx_tf = SM("gidx_tf", [128, NTILE, 2])
    gidx_t = SM("gidx_t", [128, NTILE, 2], I32)
    dma("sync", gidx_tf[:],
        slot_d[1, :].rearrange("(a p b) -> p a b", p=128, b=2), "gx_i")
    vop(lambda e: e.tensor_copy(gidx_t[:].rearrange("p a b -> p (a b)"),
                                gidx_tf[:].rearrange("p a b -> p (a b)")),
        [gidx_tf[:]], [gidx_t[:]], "gx_c")
    r0b = SM("r0b", [128, DM])
    r1b = SM("r1b", [128, DM])
    xfb = SM("xfb", [128, DM])
    ycb = SM("ycb", [128, DM])
    osq = SM("osq", [128, DM])
    msum = SM("msum", [128, 1])
    vsum = SM("vsum", [128, 1])
    m2c = SM("m2c", [128, 1])
    for i in range(NTILE):
        idma_gather(r0b[:], eout_d, gidx_t[:, i, 0:1], "ga0")
        idma_gather(r1b[:], eout_d, gidx_t[:, i, 1:2], "ga1")
        dma("sync", xfb[:], xf_d[i * 128:(i + 1) * 128, :], "xf_i2")
        vop(lambda e, i=i: e.tensor_scalar(
            ycb[:], r0b[:], g_t[:].rearrange("p a b -> p a b")[:, i, 0:1],
            None, ALU.mult),
            [r0b[:], g_t[:]], [ycb[:]], "cb0")
        vop(lambda e, i=i: e.scalar_tensor_tensor(
            ycb[:], r1b[:], g_t[:].rearrange("p a b -> p a b")[:, i, 1:2],
            ycb[:], ALU.mult, ALU.add),
            [r1b[:], g_t[:], ycb[:]], [ycb[:]], "cb1")
        vop(lambda e: e.tensor_add(ycb[:], ycb[:], xfb[:]),
            [ycb[:], xfb[:]], [ycb[:]], "cb2")
        vop(lambda e: e.tensor_reduce(msum[:], ycb[:], AX.X, ALU.add),
            [ycb[:]], [msum[:]], "l2s")
        vop(lambda e: e.scalar_tensor_tensor(osq[:], ycb[:], 1.0, ycb[:],
                                             ALU.bypass, ALU.mult,
                                             accum_out=vsum[:]),
            [ycb[:]], [osq[:], vsum[:]], "l2q")
        vop(lambda e: e.tensor_scalar(msum[:], msum[:], 1.0 / DM, None,
                                      ALU.mult),
            [msum[:]], [msum[:]], "l2m")
        vop(lambda e: e.scalar_tensor_tensor(m2c[:], msum[:], 1.0, msum[:],
                                             ALU.bypass, ALU.mult),
            [msum[:]], [m2c[:]], "l2m2")
        vop(lambda e: e.scalar_tensor_tensor(vsum[:], vsum[:], 1.0 / DM,
                                             m2c[:], ALU.mult, ALU.subtract),
            [vsum[:], m2c[:]], [vsum[:]], "l2v")
        sop(lambda e: e.activation(vsum[:], vsum[:], AF.Ln, bias=1e-5),
            [vsum[:]], [vsum[:]], "l2l")
        sop(lambda e: e.activation(vsum[:], vsum[:], AF.Exp, scale=-0.5),
            [vsum[:]], [vsum[:]], "l2r")
        vop(lambda e: e.tensor_scalar(ycb[:], ycb[:], msum[:, 0:1], None,
                                      ALU.subtract),
            [ycb[:], msum[:]], [ycb[:]], "l2c")
        vop(lambda e: e.tensor_scalar(ycb[:], ycb[:], vsum[:, 0:1], None,
                                      ALU.mult),
            [ycb[:], vsum[:]], [ycb[:]], "l2f")
        dma("gpsimd", out[i * 128:(i + 1) * 128, :], ycb[:], "f_o")

    P.emit()
    moe.close()
    persist.close()
    return nc



def make_in_map(cfg, core, xb, w):
    E = cfg.E
    r1 = np.zeros((8, 1), np.float32)
    r1[core, 0] = 1.0
    pm = np.zeros((8, 1), np.float32)
    pm[:core, 0] = 1.0
    DI, NST, NDT, NDM, NH = cfg.DI, cfg.NST, cfg.NDT, cfg.NDM, cfg.NH
    return {
        "x_t": np.ascontiguousarray(xb.T),
        "norm_w_t": np.ascontiguousarray(
            w["norm_w"].reshape(NDM, 128).T),
        "w_inT": np.ascontiguousarray(w["W_in"].T),
        "conv_w_t": np.ascontiguousarray(
            w["conv_w"].reshape(NDT, 128, 4).transpose(1, 0, 2)),
        "conv_b_t": np.ascontiguousarray(w["conv_b"].reshape(NDT, 128).T),
        "w_dtT": np.ascontiguousarray(w["W_dt"].T),
        "b_dt_t": np.ascontiguousarray(w["b_dt"].reshape(NDT, 128).T),
        "w_BT": np.ascontiguousarray(w["W_B"].T),
        "w_CT": np.ascontiguousarray(w["W_C"].T),
        "A_log_t": np.ascontiguousarray(w["A_log"].reshape(NDT, 128, NST)),
        "D_t": np.ascontiguousarray(w["D_param"].reshape(NDT, 128).T),
        "w_outT": np.ascontiguousarray(w["W_out"].T),
        "res_scale": np.ascontiguousarray(
            w["residual_scale"].reshape(1, 1)),
        "w_router": np.ascontiguousarray(w["w_router"]),
        "w1": np.ascontiguousarray(w["w1"][core]),
        "b1_t": np.ascontiguousarray(w["b1"][core].reshape(NH, 128).T),
        "w2": np.ascontiguousarray(w["w2"][core]),
        "b2_t": np.ascontiguousarray(w["b2"][core].reshape(NDM, 128).T),
        "rank1hot": r1,
        "prefmask": pm,
    }


def kernel(**inputs):
    cfg = CFG()
    w = {
        "norm_w": inputs["L0_attn_norm_w"], "W_in": inputs["L0_attn_W_in"],
        "conv_w": inputs["L0_attn_conv_w"][:, 0, :],
        "conv_b": inputs["L0_attn_conv_b"], "W_dt": inputs["L0_attn_W_dt"],
        "b_dt": inputs["L0_attn_b_dt"], "W_B": inputs["L0_attn_W_B"],
        "W_C": inputs["L0_attn_W_C"], "A_log": inputs["L0_attn_A_log"],
        "D_param": inputs["L0_attn_D_param"], "W_out": inputs["L0_attn_W_out"],
        "residual_scale": np.asarray(inputs["L0_attn_residual_scale"]),
        "w_router": inputs["L0_ffn_w_router"], "w1": inputs["L0_ffn_w1"],
        "b1": inputs["L0_ffn_b1"], "w2": inputs["L0_ffn_w2"],
        "b2": inputs["L0_ffn_b2"],
    }
    x = inputs["L0_attn_x"]
    n_cores = x.shape[0]
    nc = build_program(cfg, n_cores=n_cores)
    in_maps = [make_in_map(cfg, c, x[c], w) for c in range(n_cores)]
    res = run_bass_kernel_spmd(nc, in_maps, core_ids=list(range(n_cores)))
    outs = [res.results[c]["out"] for c in range(n_cores)]
    return np.stack(outs, axis=0)


# version 24
# speedup vs baseline: 1.0848x; 1.0233x over previous
import numpy as np
import concourse.bass as bass
import concourse.mybir as mybir
from concourse.bass_utils import run_bass_kernel_spmd
from concourse.bass_core_rust import IndirectOffsetOnAxis
from contextlib import ExitStack

F32 = mybir.dt.float32
BF16 = mybir.dt.bfloat16
I32 = mybir.dt.int32
U32 = mybir.dt.uint32
AF = mybir.ActivationFunctionType
ALU = mybir.AluOpType
AX = mybir.AxisListType

ENGINES = ("tensor", "vector", "scalar", "gpsimd", "sync")
LN20 = float(np.log(20.0))



def _ap_rect(ap):
    dims = [(int(s), int(c)) for s, c in ap.ap]
    off = int(ap.offset)
    esz = np.dtype(mybir.dt.np(ap.dtype)).itemsize
    space = str(ap.space)
    if "DRAM" in space:
        ext = sum(abs(s) * (c - 1) for s, c in dims)
        return (0, 128, off * esz, (off + ext + 1) * esz)
    p0 = ap.base_partition()
    pn = dims[0][1] if dims else 1
    pstep = dims[0][0] if dims else 0
    foff = off - p0 * pstep
    ext = sum(abs(s) * (c - 1) for s, c in dims[1:])
    return (p0, p0 + pn, foff * esz, (foff + ext + 1) * esz)


def _overlap(r1, r2):
    return (r1[0] < r2[1] and r2[0] < r1[1] and
            r1[2] < r2[3] and r2[2] < r1[3])


def _covers(rn, ro):
    return (rn[0] <= ro[0] and rn[1] >= ro[1] and
            rn[2] <= ro[2] and rn[3] >= ro[3])


class Op:
    __slots__ = ("idx", "eng", "emit", "compl", "marked", "sem_val", "tag",
                 "dom", "prewait")

    def __init__(self, idx, eng, emit, compl, tag):
        self.idx, self.eng, self.emit = idx, eng, emit
        self.compl = compl
        self.marked = False
        self.sem_val = None
        self.tag = tag
        self.dom = None
        self.prewait = None


class Prog:
    NLANE = 12

    def __init__(self, nc):
        self.nc = nc
        self.ops = []
        self.hstate = {}
        self.edges = []
        self.lane_ctr = {}
        self.lane_use = {}

    def op(self, eng, emit, reads=(), writes=(), compl=None, tag=""):
        idx = len(self.ops)
        o = Op(idx, eng, emit, compl or ("eng", eng), tag)
        if o.compl[0] == "dma":
            q = o.compl[1]
            k = self.lane_ctr.get(q, 0)
            self.lane_ctr[q] = k + 1
            dom = ("dma", q, k % self.NLANE)
            o.dom = dom
            u = self.lane_use.get(dom, 0) + 1
            self.lane_use[dom] = u
            o.sem_val = 16 * u
            if u > 1:
                o.prewait = (dom, 16 * (u - 1))
        self.ops.append(o)
        deps = set()
        rrects = [(a.tensor.name, _ap_rect(a)) for a in reads]
        wrects = [(a.tensor.name, _ap_rect(a)) for a in writes]
        for key, rect in rrects:
            stt = self.hstate.get(key)
            if stt:
                for widx, wrect in stt["w"]:
                    if _overlap(rect, wrect):
                        deps.add(widx)
        for key, rect in wrects:
            stt = self.hstate.get(key)
            if stt:
                for widx, wrect in stt["w"]:
                    if _overlap(rect, wrect):
                        deps.add(widx)
                for ridx, rrect in stt["r"]:
                    if _overlap(rect, rrect):
                        deps.add(ridx)
        deps.discard(idx)
        self.edges.append(deps)
        for key, rect in wrects:
            stt = self.hstate.setdefault(key, {"w": [], "r": []})
            stt["w"] = [(w, r) for (w, r) in stt["w"]
                        if not _covers(rect, r)] + [(idx, rect)]
            stt["r"] = [(w, r) for (w, r) in stt["r"] if not _covers(rect, r)]
        for key, rect in rrects:
            stt = self.hstate.setdefault(key, {"w": [], "r": []})
            stt["r"].append((idx, rect))
            if len(stt["r"]) > 768:
                stt["r"] = stt["r"][-768:]
        return o

    def barrier(self):
        tails = {}
        for o in self.ops:
            tails[self._dom(o)] = o.idx
        tl = set(tails.values())
        for eng in ENGINES:
            o = self.op(eng, lambda e: e.nop(), tag="barrier")
            self.edges[o.idx] = set(tl)
        bidx = [o.idx for o in self.ops[-5:]]
        for key, stt in self.hstate.items():
            stt["w"] = [(bidx[0], (0, 128, 0, 1 << 40))]
            stt["r"] = []

    def _dom(self, o):
        if o.dom is not None:
            return o.dom
        return o.compl if o.compl[0] != "eng" else ("eng", o.eng)

    def emit(self):
        nc = self.nc
        for deps in self.edges:
            for d in deps:
                self.ops[d].marked = True
        for o in self.ops:
            if o.compl[0] in ("dma", "cc"):
                o.marked = True
        lane_use = self.lane_use
        dom_counts = {}
        doms = {}
        for o in self.ops:
            dom = self._dom(o)
            doms.setdefault(dom, []).append(o)
            if o.marked and o.sem_val is None:
                inc = 1
                dom_counts[dom] = dom_counts.get(dom, 0) + inc
                o.sem_val = dom_counts[dom]
                if dom[0] == "cc":
                    o.sem_val = 1
        for dom, u in lane_use.items():
            dom_counts[dom] = 16 * u
        self.sems = {}
        self._semctx = ExitStack()
        for dom in doms:
            self.sems[dom] = self._semctx.enter_context(
                nc.semaphore('s_' + '_'.join(str(x) for x in dom)))
        eng_clock = {e: {} for e in ENGINES}
        op_clock = {}
        waits_for = {}
        for o in self.ops:
            cl = dict(eng_clock[o.eng])
            w = []
            for d in sorted(self.edges[o.idx]):
                p = self.ops[d]
                pdom = self._dom(p)
                assert p.marked
                need = p.sem_val
                if cl.get(pdom, -1) >= need:
                    continue
                w.append((pdom, need))
                cl[pdom] = need
                for kd, kv in op_clock.get(d, {}).items():
                    if cl.get(kd, -1) < kv:
                        cl[kd] = kv
            waits_for[o.idx] = w
            op_clock[o.idx] = cl
            eng_clock[o.eng] = cl
        with nc.Block() as block:
            for eng_name in ENGINES:
                stream = [o for o in self.ops if o.eng == eng_name]

                def make_body(stream, eng_name):
                    def body(eng):
                        for o in stream:
                            if o.prewait is not None:
                                eng.wait_ge(self.sems[o.prewait[0]],
                                            o.prewait[1])
                            for dom, val in waits_for[o.idx]:
                                eng.wait_ge(self.sems[dom], val)
                            try:
                                inst = o.emit(eng)
                            except Exception as ex:
                                raise RuntimeError(
                                    f"emit failed op#{o.idx} tag={o.tag} "
                                    f"eng={o.eng}") from ex
                            if o.marked:
                                dom = self._dom(o)
                                inst.then_inc(self.sems[dom],
                                              16 if dom[0] == "dma" else 1)
                        if eng_name == "sync":
                            for dom, cnt in dom_counts.items():
                                if dom[0] == "dma":
                                    eng.wait_ge(self.sems[dom], cnt)
                                elif dom[0] == "cc":
                                    eng.wait_ge(self.sems[dom], 1)
                    return body

                getattr(block, eng_name)(make_body(stream, eng_name))



class CFG:
    def __init__(self, L=4096, E=8, CAP=3072):
        self.L, self.DM, self.DI, self.NST = L, 1024, 2048, 16
        self.E, self.H, self.CAP = E, 4096, CAP
        self.NDT = self.DI // 128
        self.NDM = self.DM // 128
        self.NH = self.H // 128
        R = 10
        base = 26 * R
        chunks = [(0, min(L, base + 1))]
        t = chunks[0][1]
        while t < L:
            n = min(base, L - t)
            chunks.append((t, n))
            t += n
        self.chunks = chunks
        segs = [(0, 1, True)]
        t = 1
        while t < L:
            n = min(R, L - t)
            segs.append((t, n, ((t + n - 1) % R) == 0))
            t += n
        self.segs = segs
        self.NTILE = L // 128



def build_program(cfg, n_cores=8):
    nc = bass.Bass("TRN2", target_bir_lowering=False)
    P = Prog(nc)
    L, DM, DI, NST = cfg.L, cfg.DM, cfg.DI, cfg.NST
    E, H, CAP = cfg.E, cfg.H, cfg.CAP
    NDT, NDM, NH, NTILE = cfg.NDT, cfg.NDM, cfg.NH, cfg.NTILE
    MAXC = max(n for _, n in cfg.chunks)
    R11 = 11

    def din(name, shape, dtype=F32):
        return nc.dram_tensor(name, shape, dtype, kind="ExternalInput").ap()

    x_t = din("x_t", [DM, L])
    norm_w_t = din("norm_w_t", [128, NDM])
    w_inT = din("w_inT", [DM, 2 * DI])
    conv_w_t = din("conv_w_t", [128, NDT, 4])
    conv_b_t = din("conv_b_t", [128, NDT])
    w_dtT = din("w_dtT", [DI, DI])
    b_dt_t = din("b_dt_t", [128, NDT])
    w_BT = din("w_BT", [DI, NST])
    w_CT = din("w_CT", [DI, NST])
    A_log_t = din("A_log_t", [NDT, 128, NST])
    D_t = din("D_t", [128, NDT])
    w_outT = din("w_outT", [DI, DM])
    res_scale = din("res_scale", [1, 1])
    w_router_i = din("w_router", [DM, E])
    w1_my = din("w1", [DM, H])
    b1_t = din("b1_t", [128, NH])
    w2_my = din("w2", [H, DM])
    b2_t = din("b2_t", [128, NDM])
    rank1hot = din("rank1hot", [8, 1])
    prefmask = din("prefmask", [8, 1])

    out = nc.dram_tensor("out", [L, DM], F32, kind="ExternalOutput").ap()

    def dscr(name, shape, dtype=F32):
        return nc.dram_tensor(name, shape, dtype, kind="Internal").ap()

    winT_bf = dscr("winT_bf", [2 * NDT, 128, NDM * 128], BF16)
    wdtT_bf = dscr("wdtT_bf", [NDT, 128, NDT * 128], BF16)
    woutT_bf = dscr("woutT_bf", [NDM, 128, NDT * 128], BF16)
    w1_bf = dscr("w1_bf", [NH, 128, NDM * 128], BF16)
    w2_bf = dscr("w2_bf", [NDM, 128, NH * 128], BF16)
    brow_d = dscr("brow_d", [NST, MAXC])
    crow_d = dscr("crow_d", [NST, MAXC])
    xf_d = dscr("xf_d", [L, DM])
    logit_d = dscr("logit_d", [E, L])
    slot_d = dscr("slot_d", [4, 2 * L])
    slab_d = dscr("slab_d", [E * CAP, DM], BF16)
    shard_d = dscr("shard_d", [CAP, DM], BF16)
    eout_my = dscr("eout_my", [CAP, DM], BF16)
    eout_d = dscr("eout_d", [E * CAP, DM], BF16)
    cc_src = dscr("cc_src", [8, 8])
    cc_dst = dscr("cc_dst", [8, 8])
    meta_d = dscr("meta_d", [1, 64])

    def vop(emit, reads, writes, tag=""):
        return P.op("vector", emit, reads, writes, tag=tag)

    def sop(emit, reads, writes, tag=""):
        return P.op("scalar", emit, list(reads) + [const_pool[:]], writes,
                    tag=tag)

    def top(emit, reads, writes, tag=""):
        return P.op("tensor", emit, reads, writes, tag=tag)

    def gop(emit, reads, writes, tag=""):
        return P.op("gpsimd", emit, reads, writes, tag=tag)

    def dma(queue, dst, src, tag=""):
        return P.op(queue, lambda e, d=dst, s=src: e.dma_start(d, s),
                    [src], [dst], compl=("dma", queue), tag=tag)

    def dma4(dst, src, w, tag="", xreads=()):
        q = w // 4
        for i in range(4):
            P.op(("sync", "scalar")[i % 2],
                 lambda e, d=dst[:, i * q:(i + 1) * q],
                 s=src[:, i * q:(i + 1) * q]: e.dma_start(d, s),
                 [src[:, i * q:(i + 1) * q]] + list(xreads),
                 [dst[:, i * q:(i + 1) * q]],
                 compl=("dma", ("sync", "scalar")[i % 2]), tag=tag)

    def idma_scatter(dst, idx_ap, src, bound, tag=""):
        return P.op("gpsimd",
                    lambda e: e.indirect_dma_start(
                        dst, IndirectOffsetOnAxis(idx_ap, 0), src, None,
                        bounds_check=bound, oob_is_err=False),
                    [src, idx_ap], [dst], compl=("dma", "gpsimd"), tag=tag)

    def idma_gather(dst, src, idx_ap, tag=""):
        return P.op("gpsimd",
                    lambda e: e.indirect_dma_start(
                        dst, None, src, IndirectOffsetOnAxis(idx_ap, 0)),
                    [src, idx_ap], [dst], compl=("dma", "gpsimd"), tag=tag)

    persist = ExitStack()
    SBp = lambda name, shape, dtype=F32: persist.enter_context(
        nc.sbuf_tensor(name, shape, dtype))
    PSp = lambda name, shape, dtype=F32: persist.enter_context(
        nc.psum_tensor(name, shape, dtype))

    ones_f = SBp("ones_f", [128, 128])
    ident_bf = SBp("ident_bf", [128, 128], BF16)
    ident_f = SBp("ident_f", [128, 128])
    normw_c = SBp("normw_c", [128, NDM])
    convw_c = SBp("convw_c", [128, NDT, 4])
    convb_c = SBp("convb_c", [128, NDT])
    bdt_c = SBp("bdt_c", [128, NDT])
    D_c = SBp("D_c", [128, NDT])
    A16_c = SBp("A16_c", [128, NDT, NST])
    sscale = SBp("sscale", [128, 1])
    r1hot = SBp("r1hot", [8, 1])
    pmask = SBp("pmask", [8, 1])
    wbt_s = SBp("wbt_s", [128, NDT, NST], BF16)
    wct_s = SBp("wct_s", [128, NDT, NST], BF16)
    wrt_s = SBp("wrt_s", [128, NDM, E])
    b1c = SBp("b1c", [128, NH])
    b2c = SBp("b2c", [128, NDM])
    g_t = SBp("g_t", [128, NTILE, 2])
    io128 = SBp("io128", [128, 128], I32)
    pidx = SBp("pidx", [128, 1], I32)
    pidxf = SBp("pidxf", [128, 1])
    zrow = SBp("zrow", [128, DM], BF16)

    CONST_VALS = [0.0, 1.0, 1e-6, 1e-5, 1e-30, LN20, -0.0505 / 0.099]
    const_pool = SBp("const_pool", [128, len(CONST_VALS)])
    for i, v in enumerate(CONST_VALS):
        nc.const_aps.register(F32, v, const_pool[:, i:i + 1])

    pMM = PSp("pMM", [128, 512])
    pMM2 = PSp("pMM2", [128, 512])
    pMM3 = PSp("pMM3", [128, 512])
    pStat = PSp("pStat", [1, 512])
    pStat2 = PSp("pStat2", [1, 512])
    pBC = PSp("pBC", [128, 512])
    pBC2 = PSp("pBC2", [128, 512])
    pT = PSp("pT", [128, 128])

    for i, v in enumerate(CONST_VALS):
        gop(lambda e, i=i, v=v: e.memset(const_pool[:, i:i + 1], v),
            [], [const_pool[:, i:i + 1]], "cpool")
    gop(lambda e: e.iota(io128[:], [[1, 128]], channel_multiplier=0),
        [], [io128[:]], "iota128")
    gop(lambda e: e.iota(pidx[:], [[1, 1]], channel_multiplier=1),
        [], [pidx[:]], "iotap")
    gop(lambda e: e.memset(ones_f[:], 1.0), [], [ones_f[:]], "ones")
    gop(lambda e: e.memset(zrow[:], 0.0), [], [zrow[:]], "zrow")
    vop(lambda e: e.tensor_copy(ident_f[:], io128[:]), [io128[:]], [ident_f[:]],
        "io_f")
    vop(lambda e: e.tensor_copy(pidxf[:], pidx[:]), [pidx[:]], [pidxf[:]],
        "pi_f")
    vop(lambda e: e.tensor_scalar(ident_f[:], ident_f[:], pidxf[:, 0:1], None,
                                  ALU.is_equal),
        [ident_f[:], pidxf[:]], [ident_f[:]], "ident")
    vop(lambda e: e.tensor_copy(ident_bf[:], ident_f[:]),
        [ident_f[:]], [ident_bf[:]], "identb")

    dma("sync", normw_c[:], norm_w_t)
    dma("sync", convw_c[:], conv_w_t)
    dma("sync", convb_c[:], conv_b_t)
    dma("sync", bdt_c[:], b_dt_t)
    dma("sync", D_c[:], D_t)
    dma("sync", r1hot[:], rank1hot)
    dma("sync", pmask[:], prefmask)
    dma("sync", b1c[:], b1_t)
    dma("sync", b2c[:], b2_t)
    dma("sync", wrt_s[:], w_router_i.rearrange("(c p) e -> p c e", p=128))

    mam = ExitStack()
    SB = lambda name, shape, dtype=F32: mam.enter_context(
        nc.sbuf_tensor(name, shape, dtype))

    atmp = SB("atmp", [128, NDT * NST])
    dma("sync", atmp[:].rearrange("p (a b) -> p a b", b=NST),
        A_log_t.rearrange("a p b -> p a b"))
    sop(lambda e: e.activation(atmp[:], atmp[:], AF.Tanh, scale=0.1),
        [atmp[:]], [atmp[:]], "A_tanh")
    sop(lambda e: e.activation(atmp[:], atmp[:], AF.Exp, scale=10.0),
        [atmp[:]], [atmp[:]], "A_exp")
    vop(lambda e: e.tensor_scalar(A16_c[:].rearrange("p a b -> p (a b)"),
                                  atmp[:], -1.0 / 16.0, None, ALU.mult),
        [atmp[:]], [A16_c[:]], "A_sc")

    stmp = SB("stmp", [1, 1])
    dma("sync", stmp[:], res_scale)
    sop(lambda e: e.activation(stmp[:], stmp[:], AF.Sigmoid),
        [stmp[:]], [stmp[:]], "sres")
    top(lambda e: e.matmul(pBC[:, 0:1], ones_f[0:1, :], stmp[:],
                           start=True, stop=True),
        [ones_f[:], stmp[:]], [pBC[:, 0:1]], "sres_bc")
    vop(lambda e: e.tensor_scalar(sscale[:], pBC[:, 0:1], 0.5, None, ALU.mult),
        [pBC[:, 0:1]], [sscale[:]], "sres_h")

    btmp = SB("btmp", [128, NDT * NST])
    for src, dst_s in ((w_BT, wbt_s), (w_CT, wct_s)):
        dma("sync", btmp[:].rearrange("p (a b) -> p a b", b=NST),
            src.rearrange("(a p) b -> p a b", p=128))
        vop(lambda e, dd=dst_s: e.tensor_copy(
            dd[:].rearrange("p a b -> p (a b)"), btmp[:]),
            [btmp[:]], [dst_s[:]], "wbc_bf")

    cast_st = SB("cast_st", [128, 2, 512])
    cast_ot = SB("cast_ot", [128, 2, 512], BF16)

    def cast_weight(src_rows, dst_slabs, n_c, n_m, queue, ceng, tag):
        srcv = src_rows.rearrange("(c p) (m f) -> m p c f", p=128, f=128)
        CB = 4
        pp = [0]
        for m in range(n_m):
            for c0 in range(0, n_c, CB):
                c1 = min(n_c, c0 + CB)
                ww = (c1 - c0) * 128
                par = pp[0] % 2
                qq = ("sync", "scalar")[par]
                pp[0] += 1
                stg = cast_st[:, par, 0:ww]
                outg = cast_ot[:, par, 0:ww]
                dma(qq, stg, srcv[m:m + 1, :, c0:c1, :], tag)
                ceng(lambda e, s=stg, o=outg: e.tensor_copy(o, s),
                     [stg], [outg], tag + "c")
                dma(qq, dst_slabs[m, :, c0 * 128:c1 * 128], outg,
                    tag + "o")

    cast_weight(w_inT, winT_bf, NDM, 2 * NDT, "sync", vop, "win")
    cast_weight(w_dtT, wdtT_bf, NDT, NDT, "sync", vop, "wdt")
    cast_weight(w_outT, woutT_bf, NDT, NDM, "sync", vop, "wou")

    xb_s = SB("xb_s", [128, NDM, MAXC])
    xnxc_bf = SB("xnxc_bf", [128, NDT, MAXC], BF16)
    xc_s = SB("xc_s", [128, NDT, MAXC])
    dt_s = SB("dt_s", [128, NDT, MAXC])
    dtx_s = SB("dtx_s", [128, NDT, MAXC])
    gsilu_bf = SB("gsilu_bf", [128, NDT, MAXC], BF16)
    bbc_s = SB("bbc_s", [128, NST, MAXC])
    cbc_s = SB("cbc_s", [128, NST, MAXC])
    SEGW = NDT * NST * R11
    a_buf = SB("a_buf", [128, 2 * SEGW])
    u_buf = SB("u_buf", [128, SEGW])
    h_buf = SB("h_buf", [128, SEGW])
    ybf_s = SB("ybf_s", [128, NDT, MAXC], BF16)
    conv_sc = SB("conv_sc", [128, MAXC + 3])
    acc0_sc = SB("acc0_sc", [128, MAXC])
    acc1_sc = SB("acc1_sc", [128, MAXC])
    sq_sc = SB("sq_sc", [128, MAXC])
    ysc_s = SB("ysc_s", [128, NST * NDT * 10])
    xtail_s = SB("xtail_s", [128, NDT, 3])
    wm_st = SB("wm_st", [128, 3, NDT * 128], BF16)
    brow_s = SB("brow_s", [NST, MAXC])
    crow_s = SB("crow_s", [NST, MAXC])
    lrow_s = SB("lrow_s", [E, MAXC])
    row_sc = SB("row_sc", [1, MAXC])
    row_sc2 = SB("row_sc2", [1, MAXC])
    sqcol = SB("sqcol", [128, 1])
    nsc_s = SB("nsc_s", [128, 1])
    bscr2 = SB("bscr2", [128, MAXC])
    fac_s = SB("fac_s", [128, 1])
    xfrow_st = SB("xfrow_st", [128, 128])

    MAGIC = 1597463007

    def newton_rsqrt(buf, scr, np_, w, tag):
        bi = buf.bitcast(I32)
        si = scr.bitcast(I32)
        t2 = bscr2[0:np_, 0:w]
        vop(lambda e: e.tensor_scalar(si, bi, 1, None,
                                      ALU.arith_shift_right),
            [buf], [scr], tag + "_sh")
        vop(lambda e: e.tensor_scalar(si, si, -1, MAGIC, ALU.mult, ALU.add),
            [scr], [scr], tag + "_sd")
        for it in range(2):
            vop(lambda e: e.scalar_tensor_tensor(t2, scr, 1.0,
                                                 scr, ALU.bypass, ALU.mult),
                [scr], [t2], tag + "_y2")
            vop(lambda e: e.tensor_tensor(t2, t2, buf, ALU.mult),
                [t2, buf], [t2], tag + "_sy")
            vop(lambda e: e.tensor_scalar(t2, t2, -0.5, 1.5,
                                          ALU.mult, ALU.add),
                [t2], [t2], tag + "_ht")
            vop(lambda e: e.tensor_tensor(scr, scr, t2, ALU.mult),
                [scr, t2], [scr], tag + "_yn")
        vop(lambda e: e.tensor_copy(buf, scr), [scr], [buf], tag + "_cp")

    vop(lambda e: e.memset(u_buf[:], 0.0), [], [u_buf[:]], "zc")
    vop(lambda e: e.memset(ysc_s[:], 0.0), [], [ysc_s[:]], "zy")
    vop(lambda e: e.memset(h_buf[:], 0.0), [], [h_buf[:]], "zc2")
    vop(lambda e: e.memset(a_buf[:], 0.0), [], [a_buf[:]], "za")

    for r0 in range(0, E * CAP, 128):
        dma("gpsimd", slab_d[r0:r0 + 128, :], zrow[:], "zslab")

    seg_ptr = [0]
    seg_par = [0]

    def emit_chunk(ci, t0, tn):
        for d in range(NDM):
            dma("sync", xb_s[:, d, 0:tn],
                x_t[d * 128:(d + 1) * 128, t0:t0 + tn], "x_in")
        for d in range(NDM):
            sop(lambda e, d=d: e.activation(sq_sc[:, 0:tn], xb_s[:, d, 0:tn],
                                            AF.Square),
                [xb_s[:, d, 0:tn]], [sq_sc[:, 0:tn]], "rms_sq")
            top(lambda e, d=d: e.matmul(pStat[:, 0:tn], ones_f[:, 0:1],
                                        sq_sc[:, 0:tn], start=(d == 0),
                                        stop=(d == NDM - 1)),
                [ones_f[:], sq_sc[:, 0:tn]], [pStat[:, 0:tn]], "rms_mm")
        vop(lambda e: e.tensor_scalar(row_sc[:, 0:tn], pStat[0:1, 0:tn],
                                      1.0 / DM, 1e-6, ALU.mult, ALU.add),
            [pStat[:, 0:tn]], [row_sc[:, 0:tn]], "rms_v")
        newton_rsqrt(row_sc[0:1, 0:tn], row_sc2[0:1, 0:tn], 1, tn, "rms_n")
        top(lambda e: e.matmul(pBC[:, 0:tn], ones_f[0:1, :], row_sc[:, 0:tn],
                               start=True, stop=True),
            [ones_f[:], row_sc[:, 0:tn]], [pBC[:, 0:tn]], "rms_bc")
        for d in range(NDM):
            vop(lambda e, d=d: e.scalar_tensor_tensor(
                xnxc_bf[:, d, 0:tn], xb_s[:, d, 0:tn], normw_c[:, d:d + 1],
                pBC[:, 0:tn], ALU.mult, ALU.mult),
                [xb_s[:, d, 0:tn], normw_c[:], pBC[:, 0:tn]],
                [xnxc_bf[:, d, 0:tn]], "xn")

        for m in range(2 * NDT):
            par = m % 3
            wslab = wm_st[:, par, 0:NDM * 128]
            dma("sync" if m % 2 == 0 else "scalar", wslab,
                winT_bf[m, :, :], "wl")
            pj = (pMM, pMM2, pMM3)[m % 3]
            for c in range(NDM):
                top(lambda e, c=c, pj=pj, ws=wslab: e.matmul(
                    pj[:, 0:tn], ws[:, c * 128:(c + 1) * 128],
                    xnxc_bf[:, c, 0:tn], start=(c == 0), stop=(c == NDM - 1)),
                    [wslab, xnxc_bf[:, c, 0:tn]], [pj[:, 0:tn]], "pj_mm")
            if m >= NDT:
                g = m - NDT
                sop(lambda e, g=g, pj=pj: e.activation(
                    gsilu_bf[:, g, 0:tn], pj[:, 0:tn], AF.Silu),
                    [pj[:, 0:tn]], [gsilu_bf[:, g, 0:tn]], "gsilu")
            else:
                d = m
                vop(lambda e, pj=pj: e.tensor_copy(conv_sc[:, 3:3 + tn],
                                                   pj[:, 0:tn]),
                    [pj[:, 0:tn]], [conv_sc[:, 3:3 + tn]], "cv_cp")
                if ci == 0:
                    vop(lambda e: e.memset(conv_sc[:, 0:3], 0.0),
                        [], [conv_sc[:, 0:3]], "cv_z")
                else:
                    vop(lambda e, d=d: e.tensor_copy(conv_sc[:, 0:3],
                                                     xtail_s[:, d, :]),
                        [xtail_s[:, d, :]], [conv_sc[:, 0:3]], "cv_t")
                vop(lambda e, d=d: e.tensor_copy(xtail_s[:, d, :],
                                                 conv_sc[:, tn:tn + 3]),
                    [conv_sc[:, 0:tn + 3]], [xtail_s[:, d, :]], "cv_st")
                vop(lambda e, d=d: e.tensor_scalar(
                    acc0_sc[:, 0:tn], conv_sc[:, 0:tn], convw_c[:, d, 0:1],
                    convb_c[:, d:d + 1], ALU.mult, ALU.add),
                    [conv_sc[:, 0:tn + 3], convw_c[:], convb_c[:]],
                    [acc0_sc[:, 0:tn]], "cv0")
                lastacc = acc0_sc
                for k in range(1, 4):
                    src, dst = ((acc0_sc, acc1_sc) if k % 2 else
                                (acc1_sc, acc0_sc))
                    vop(lambda e, d=d, k=k, s=src, ds=dst:
                        e.scalar_tensor_tensor(
                            ds[:, 0:tn], conv_sc[:, k:k + tn],
                            convw_c[:, d, k:k + 1], s[:, 0:tn],
                            ALU.mult, ALU.add),
                        [conv_sc[:, 0:tn + 3], convw_c[:], src[:, 0:tn]],
                        [dst[:, 0:tn]], "cvk")
                    lastacc = dst
                sop(lambda e, d=d, la=lastacc: e.activation(
                    xc_s[:, d, 0:tn], la[:, 0:tn], AF.Silu),
                    [lastacc[:, 0:tn]], [xc_s[:, d, 0:tn]], "xc_silu")

        for d in range(NDT):
            vop(lambda e, d=d: e.tensor_copy(xnxc_bf[:, d, 0:tn],
                                             xc_s[:, d, 0:tn]),
                [xc_s[:, d, 0:tn]], [xnxc_bf[:, d, 0:tn]], "xcbf")

        for wt, pp, rows, rowd, bc in ((wbt_s, pBC, brow_s, brow_d, bbc_s),
                                       (wct_s, pBC2, crow_s, crow_d, cbc_s)):
            for c in range(NDT):
                top(lambda e, c=c, wt=wt, pp=pp: e.matmul(
                    pp[0:NST, 0:tn], wt[:, c, :], xnxc_bf[:, c, 0:tn],
                    start=(c == 0), stop=(c == NDT - 1)),
                    [wt[:], xnxc_bf[:, c, 0:tn]], [pp[:, 0:tn]], "bc_mm")
            vop(lambda e, pp=pp, rows=rows: e.tensor_copy(
                rows[:, 0:tn], pp[0:NST, 0:tn]),
                [pp[:, 0:tn]], [rows[:, 0:tn]], "bc_ev")
            dma("scalar", rowd[:, 0:tn], rows[:, 0:tn], "bc_o")
            dma("scalar", bc[:, :, 0:tn],
                rowd[:, 0:tn].unsqueeze(0).broadcast_to([128, NST, tn]),
                "bc_r")

        for m in range(NDT):
            par = m % 3
            wslab = wm_st[:, par, 0:NDT * 128]
            dma("sync" if m % 2 == 0 else "scalar", wslab,
                wdtT_bf[m, :, :], "wl2")
            pj = (pMM, pMM2, pMM3)[m % 3]
            for c in range(NDT):
                top(lambda e, c=c, pj=pj, ws=wslab: e.matmul(
                    pj[:, 0:tn], ws[:, c * 128:(c + 1) * 128],
                    xnxc_bf[:, c, 0:tn], start=(c == 0), stop=(c == NDT - 1)),
                    [wslab, xnxc_bf[:, c, 0:tn]], [pj[:, 0:tn]], "dt_mm")
            sop(lambda e, m=m, pj=pj: e.activation(
                dt_s[:, m, 0:tn], pj[:, 0:tn], AF.Exp,
                bias=bdt_c[:, m:m + 1]),
                [pj[:, 0:tn], bdt_c[:]], [dt_s[:, m, 0:tn]], "sp_e")
            sop(lambda e, m=m: e.activation(dt_s[:, m, 0:tn],
                                            dt_s[:, m, 0:tn], AF.Ln,
                                            bias=1.0),
                [dt_s[:, m, 0:tn]], [dt_s[:, m, 0:tn]], "sp_l")
        for m in range(NDT):
            sop(lambda e, m=m: e.activation(dt_s[:, m, 0:tn],
                                            dt_s[:, m, 0:tn],
                                            AF.Tanh, scale=1.0 / 0.099,
                                            bias=-0.0505 / 0.099),
                [dt_s[:, m, 0:tn]], [dt_s[:, m, 0:tn]], "sp_t")
            vop(lambda e, m=m: e.tensor_scalar(
                dt_s[:, m, 0:tn], dt_s[:, m, 0:tn], 0.099, 0.0505,
                ALU.mult, ALU.add),
                [dt_s[:, m, 0:tn]], [dt_s[:, m, 0:tn]], "dt_a")
            vop(lambda e, m=m: e.tensor_mul(dtx_s[:, m, 0:tn],
                                            dt_s[:, m, 0:tn],
                                            xc_s[:, m, 0:tn]),
                [dt_s[:, m, 0:tn], xc_s[:, m, 0:tn]],
                [dtx_s[:, m, 0:tn]], "dtx")
        for d in range(NDT):
            vop(lambda e, d=d: e.tensor_scalar(
                xc_s[:, d, 0:tn], xc_s[:, d, 0:tn], D_c[:, d:d + 1], None,
                ALU.mult),
                [xc_s[:, d, 0:tn], D_c[:]], [xc_s[:, d, 0:tn]], "dxc")

        def a_slice(par):
            return a_buf[:, par * SEGW:(par + 1) * SEGW]

        def a_prep(s0, sl, par):
            toff = s0 - t0
            ab = a_slice(par).rearrange("p (d n s) -> p d n s", n=NST, s=R11)
            dt4 = dt_s[:, :, toff:toff + sl].unsqueeze(2).broadcast_to(
                [128, NDT, NST, sl])
            a164 = A16_c[:].unsqueeze(3).broadcast_to([128, NDT, NST, sl])
            vop(lambda e: e.tensor_tensor(ab[:, :, :, 1:1 + sl], dt4, a164,
                                          ALU.mult),
                [dt_s[:], A16_c[:]], [a_slice(par)], "a_w")
            sop(lambda e: e.activation(ab[:, :, :, 1:1 + sl],
                                       ab[:, :, :, 1:1 + sl], AF.Tanh,
                                       scale=1.0 / 16.0),
                [a_slice(par)], [a_slice(par)], "a_t")
            sop(lambda e: e.activation(ab[:, :, :, 1:1 + sl],
                                       ab[:, :, :, 1:1 + sl], AF.Exp,
                                       scale=16.0),
                [a_slice(par)], [a_slice(par)], "a_e")

        csegs = []
        while seg_ptr[0] < len(cfg.segs):
            s0, sl, rs = cfg.segs[seg_ptr[0]]
            if s0 >= t0 + tn:
                break
            assert s0 >= t0 and s0 + sl <= t0 + tn
            csegs.append((s0, sl, rs))
            seg_ptr[0] += 1

        def y_finish(par, toff, sl):
            DT10 = NDT * 10
            for lev, half in ((0, 8), (1, 4), (2, 2), (3, 1)):
                w = half * DT10
                vop(lambda e, w=w: e.tensor_add(
                    ysc_s[:, 0:w], ysc_s[:, 0:w], ysc_s[:, w:2 * w]),
                    [ysc_s[:, 0:2 * w]], [ysc_s[:, 0:w]], "ytr")
            ydt = bscr2[:, 0:NDT * sl].rearrange("p (d t) -> p d t", t=sl)
            ysum = ysc_s[:, 0:DT10].rearrange(
                "p (d t) -> p d t", t=10)[:, :, 0:sl]
            dxc4 = xc_s[:, :, toff:toff + sl]
            vop(lambda e: e.tensor_tensor(ydt, ysum, dxc4, ALU.add),
                [ysc_s[:, 0:DT10], xc_s[:]], [bscr2[:, 0:NDT * sl]], "yD")
            g4 = gsilu_bf[:, :, toff:toff + sl]
            yb4 = ybf_s[:, :, toff:toff + sl]
            vop(lambda e: e.tensor_tensor(yb4, ydt, g4, ALU.mult),
                [bscr2[:, 0:NDT * sl], gsilu_bf[:]], [ybf_s[:]], "yg")

        pending_y = None
        for si, (s0, sl, rs) in enumerate(csegs):
            par = seg_par[0]
            seg_par[0] = 1 - par
            uh = u_buf if par == 0 else h_buf
            uhn = h_buf if par == 0 else u_buf
            if si == 0:
                a_prep(s0, sl, par)
            toff = s0 - t0
            ub = uh[:].rearrange("p (d n s) -> p d n s", n=NST, s=R11)
            ubn = uhn[:].rearrange("p (d n s) -> p d n s", n=NST, s=R11)
            dtx4 = dtx_s[:, :, toff:toff + sl].unsqueeze(2).broadcast_to(
                [128, NDT, NST, sl])
            bb4 = bbc_s[:, :, toff:toff + sl].unsqueeze(1).broadcast_to(
                [128, NDT, NST, sl])
            vop(lambda e, dtx4=dtx4, bb4=bb4, ub=ub, sl=sl: e.tensor_tensor(
                ub[:, :, :, 1:1 + sl], dtx4, bb4, ALU.mult),
                [dtx_s[:], bbc_s[:]], [uh[:]], "u_w")
            if si + 1 < len(csegs):
                n0, nl, _ = csegs[si + 1]
                a_prep(n0, nl, 1 - par)
            vop(lambda e, par=par, uh=uh: e.tensor_tensor_scan(
                uh[:], a_slice(par), uh[:], 0.0, ALU.mult, ALU.add),
                [a_slice(par), uh[:]], [uh[:]], "scan")
            if pending_y is not None:
                y_finish(*pending_y)
                pending_y = None
            hcols = ub[:, :, :, sl:sl + 1]
            if rs:
                vop(lambda e, hcols=hcols: e.scalar_tensor_tensor(
                    sq_sc[:, 0:NDT * NST].rearrange(
                        "p (d n) -> p d n", n=NST).unsqueeze(3),
                    hcols, 1.0, hcols, ALU.bypass, ALU.mult,
                    accum_out=sqcol[:]),
                    [uh[:]], [sq_sc[:, 0:NDT * NST], sqcol[:]], "r_sq")
                top(lambda e: e.matmul(pBC2[:, 0:1], ones_f[:], sqcol[:],
                                       start=True, stop=True),
                    [ones_f[:], sqcol[:]], [pBC2[:, 0:1]], "r_nm")
                vop(lambda e: e.tensor_scalar(fac_s[:], pBC2[:, 0:1],
                                              1.0 / 400.0, None, ALU.mult),
                    [pBC2[:, 0:1]], [fac_s[:]], "r_cp")
                newton_rsqrt(fac_s[:, 0:1], nsc_s[:, 0:1], 128, 1, "r_n")
                vop(lambda e: e.tensor_scalar_min(fac_s[:], fac_s[:], 1.0),
                    [fac_s[:]], [fac_s[:]], "r_mn")
                vop(lambda e, hcols=hcols: e.tensor_scalar(
                    hcols, hcols, fac_s[:, 0:1], None, ALU.mult),
                    [uh[:], fac_s[:]], [uh[:]], "r_ap")
            vop(lambda e, hcols=hcols, ubn=ubn: e.tensor_copy(
                ubn[:, :, :, 0:1], hcols),
                [uh[:]], [uhn[:]], "carry")
            h_ndt = ub[:, :, :, 1:1 + sl].transpose([0, 2, 1, 3])
            c_ndt = cbc_s[:, :, toff:toff + sl].unsqueeze(1).broadcast_to(
                [128, NDT, NST, sl]).transpose([0, 2, 1, 3])
            yh = ysc_s[:].rearrange("p (n d t) -> p n d t", d=NDT, t=10)
            vop(lambda e, h_ndt=h_ndt, c_ndt=c_ndt, yh=yh, sl=sl:
                e.tensor_tensor(yh[:, :, :, 0:sl], h_ndt, c_ndt, ALU.mult),
                [uh[:], cbc_s[:]], [ysc_s[:]], "yhat")
            pending_y = (par, toff, sl)
        if pending_y is not None:
            y_finish(*pending_y)
            pending_y = None

        for m in range(NDM):
            par = m % 3
            wslab = wm_st[:, par, 0:NDT * 128]
            dma("sync" if m % 2 == 0 else "scalar", wslab,
                woutT_bf[m, :, :], "wl3")
            pj = (pMM, pMM2, pMM3)[m % 3]
            for c in range(NDT):
                top(lambda e, c=c, pj=pj, ws=wslab: e.matmul(
                    pj[:, 0:tn], ws[:, c * 128:(c + 1) * 128],
                    ybf_s[:, c, 0:tn], start=(c == 0), stop=(c == NDT - 1)),
                    [wslab, ybf_s[:, c, 0:tn]], [pj[:, 0:tn]], "wo_mm")
            vop(lambda e, m=m: e.tensor_scalar(
                xb_s[:, m, 0:tn], xb_s[:, m, 0:tn], 2.0, None, ALU.mult),
                [xb_s[:, m, 0:tn]], [xb_s[:, m, 0:tn]], "x2x")
            vop(lambda e, m=m, pj=pj: e.scalar_tensor_tensor(
                xb_s[:, m, 0:tn], pj[:, 0:tn], sscale[:, 0:1],
                xb_s[:, m, 0:tn], ALU.mult, ALU.add),
                [pj[:, 0:tn], sscale[:], xb_s[:, m, 0:tn]],
                [xb_s[:, m, 0:tn]], "resid")
        for d in range(NDM):
            top(lambda e, d=d: e.matmul(pStat[:, 0:tn], ones_f[:, 0:1],
                                        xb_s[:, d, 0:tn], start=(d == 0),
                                        stop=(d == NDM - 1)),
                [ones_f[:], xb_s[:, d, 0:tn]], [pStat[:, 0:tn]], "l1_s")
            sop(lambda e, d=d: e.activation(sq_sc[:, 0:tn], xb_s[:, d, 0:tn],
                                            AF.Square),
                [xb_s[:, d, 0:tn]], [sq_sc[:, 0:tn]], "l1_q")
            top(lambda e, d=d: e.matmul(pStat2[:, 0:tn], ones_f[:, 0:1],
                                        sq_sc[:, 0:tn], start=(d == 0),
                                        stop=(d == NDM - 1)),
                [ones_f[:], sq_sc[:, 0:tn]], [pStat2[:, 0:tn]], "l1_q2")
        sop(lambda e: e.activation(row_sc[:, 0:tn], pStat[0:1, 0:tn], AF.Copy,
                                   scale=1.0 / DM),
            [pStat[:, 0:tn]], [row_sc[:, 0:tn]], "l1_m")
        sop(lambda e: e.activation(row_sc2[:, 0:tn], row_sc[:, 0:tn],
                                   AF.Square),
            [row_sc[:, 0:tn]], [row_sc2[:, 0:tn]], "l1_m2")
        top(lambda e: e.matmul(pBC[:, 0:tn], ones_f[0:1, :], row_sc[:, 0:tn],
                               start=True, stop=True),
            [ones_f[:], row_sc[:, 0:tn]], [pBC[:, 0:tn]], "l1_bm")
        vop(lambda e: e.scalar_tensor_tensor(
            row_sc[:, 0:tn], pStat2[0:1, 0:tn], 1.0 / DM, row_sc2[:, 0:tn],
            ALU.mult, ALU.subtract),
            [pStat2[:, 0:tn], row_sc2[:, 0:tn]], [row_sc[:, 0:tn]], "l1_v")
        vop(lambda e: e.tensor_scalar(row_sc[:, 0:tn], row_sc[:, 0:tn],
                                      1.0, 1e-5, ALU.mult, ALU.add),
            [row_sc[:, 0:tn]], [row_sc[:, 0:tn]], "l1_e")
        newton_rsqrt(row_sc[0:1, 0:tn], row_sc2[0:1, 0:tn], 1, tn, "l1_n")
        top(lambda e: e.matmul(pBC2[:, 0:tn], ones_f[0:1, :],
                               row_sc[:, 0:tn], start=True, stop=True),
            [ones_f[:], row_sc[:, 0:tn]], [pBC2[:, 0:tn]], "l1_br")
        for d in range(NDM):
            vop(lambda e, d=d: e.tensor_sub(xb_s[:, d, 0:tn],
                                            xb_s[:, d, 0:tn], pBC[:, 0:tn]),
                [xb_s[:, d, 0:tn], pBC[:, 0:tn]], [xb_s[:, d, 0:tn]], "xf_c")
            vop(lambda e, d=d: e.tensor_mul(xb_s[:, d, 0:tn],
                                            xb_s[:, d, 0:tn], pBC2[:, 0:tn]),
                [xb_s[:, d, 0:tn], pBC2[:, 0:tn]], [xb_s[:, d, 0:tn]],
                "xf_r")
        for c in range(NDM):
            top(lambda e, c=c: e.matmul(
                pBC[0:E, 0:tn], wrt_s[:, c, :], xb_s[:, c, 0:tn],
                start=(c == 0), stop=(c == NDM - 1)),
                [wrt_s[:], xb_s[:, c, 0:tn]], [pBC[:, 0:tn]], "rt_mm")
        vop(lambda e: e.tensor_copy(lrow_s[:, 0:tn], pBC[0:E, 0:tn]),
            [pBC[:, 0:tn]], [lrow_s[:, 0:tn]], "rt_ev")
        dma("scalar", logit_d[:, t0:t0 + tn], lrow_s[:, 0:tn], "rt_o")
        nsub = (tn + 127) // 128
        for s in range(nsub):
            w = min(128, tn - s * 128)
            for d in range(NDM):
                top(lambda e, s=s, d=d, w=w: e.transpose(
                    pT[0:w, :], xb_s[:, d, s * 128:s * 128 + w], ident_f[:]),
                    [xb_s[:, d, 0:tn], ident_f[:]], [pT[:, :]], "xf_tp")
                vop(lambda e, d=d, w=w: e.tensor_copy(
                    xfrow_st[0:w, :], pT[0:w, :]),
                    [pT[:, :]], [xfrow_st[:, :]], "xf_ev")
                dma("scalar",
                    xf_d[t0 + s * 128:t0 + s * 128 + w,
                         d * 128:(d + 1) * 128],
                    xfrow_st[0:w, :], "xf_o")

    moe_cast_done = [0]

    for ci, (t0, tn) in enumerate(cfg.chunks):
        emit_chunk(ci, t0, tn)
        if ci == 1:
            cast_weight(w1_my, w1_bf, NDM, NH, "scalar", vop, "w1")
            moe_cast_done[0] |= 1
        elif ci == 3:
            cast_weight(w2_my, w2_bf, NH, NDM, "scalar", vop, "w2")
            moe_cast_done[0] |= 2
    if not moe_cast_done[0] & 1:
        cast_weight(w1_my, w1_bf, NDM, NH, "scalar", vop, "w1")
    if not moe_cast_done[0] & 2:
        cast_weight(w2_my, w2_bf, NH, NDM, "scalar", vop, "w2")

    mam.close()
    P.barrier()

    moe = ExitStack()
    SM = lambda name, shape, dtype=F32: moe.enter_context(
        nc.sbuf_tensor(name, shape, dtype))
    rt = ExitStack()
    SR = lambda name, shape, dtype=F32: rt.enter_context(
        nc.sbuf_tensor(name, shape, dtype))

    lgrow = SR("lgrow", [E, L])
    lg_t = SR("lg_t", [128, NTILE, E])
    mx_t = SR("mx_t", [128, NTILE, 8])
    ix_t = SR("ix_t", [128, NTILE, 8], U32)
    e_t = SR("e_t", [128, NTILE, 2])
    dma("sync", lgrow[:], logit_d, "lg_i")
    for i in range(NTILE):
        top(lambda e, i=i: e.transpose(pT[:, 0:E],
                                       lgrow[:, i * 128:(i + 1) * 128],
                                       ident_f[0:E, 0:E]),
            [lgrow[:], ident_f[:]], [pT[:, :]], "lg_tp")
        vop(lambda e, i=i: e.tensor_copy(lg_t[:, i, :], pT[:, 0:E]),
            [pT[:, :]], [lg_t[:, i, :]], "lg_ev")
    for i in range(NTILE):
        vop(lambda e, i=i: e.max(out=mx_t[:, i, :], in_=lg_t[:, i, :]),
            [lg_t[:, i, :]], [mx_t[:, i, :]], "top8")
        vop(lambda e, i=i: e.max_index(out=ix_t[:, i, :],
                                       in_max=mx_t[:, i, :],
                                       in_values=lg_t[:, i, :]),
            [mx_t[:, i, :], lg_t[:, i, :]], [ix_t[:, i, :]], "topi")
        vop(lambda e, i=i: e.tensor_copy(e_t[:, i, :], ix_t[:, i, 0:2]),
            [ix_t[:, i, :]], [e_t[:, i, :]], "topc")
        vop(lambda e, i=i: e.tensor_sub(g_t[:, i, 0:1], mx_t[:, i, 0:1],
                                        mx_t[:, i, 1:2]),
            [mx_t[:, i, :]], [g_t[:, i, 0:1]], "gd")
    sop(lambda e: e.activation(
        g_t[:].rearrange("p a b -> p a b")[:, :, 0:1],
        g_t[:].rearrange("p a b -> p a b")[:, :, 0:1], AF.Sigmoid),
        [g_t[:]], [g_t[:]], "gsig")
    vop(lambda e: e.tensor_scalar(
        g_t[:].rearrange("p a b -> p a b")[:, :, 1:2],
        g_t[:].rearrange("p a b -> p a b")[:, :, 0:1], -1.0, 1.0,
        ALU.mult, ALU.add),
        [g_t[:]], [g_t[:]], "g1m")

    dma("gpsimd", slot_d[0, :].rearrange("(a p b) -> p a b", p=128, b=2),
        e_t[:], "ei_o")
    SC = 512
    NSC = (2 * L + SC - 1) // SC
    cs_d = dscr("cs_d", [8, 2 * L])
    oh_s = SR("oh_s", [8, SC])
    cs_s = SR("cs_s", [8, SC])
    zz_s = SR("zz_s", [8, SC])
    eidx_s = SR("eidx_s", [8, SC])
    tmp8 = SR("tmp8", [8, SC])
    carry8 = SR("carry8", [8, 1])
    ecol = SR("ecol", [8, 1], I32)
    ecolf = SR("ecolf", [8, 1])
    gop(lambda e: e.iota(ecol[:], [[1, 1]], channel_multiplier=1),
        [], [ecol[:]], "e_io")
    vop(lambda e: e.tensor_copy(ecolf[:], ecol[:]), [ecol[:]], [ecolf[:]],
        "e_iof")
    gop(lambda e: e.memset(zz_s[:], 0.0), [], [zz_s[:]], "zz")
    vop(lambda e: e.memset(carry8[:], 0.0), [], [carry8[:]], "zc8")

    def slot_chunk_oh_cs(q, wq):
        dma("sync", eidx_s[:, 0:wq],
            slot_d[0:1, q * SC:q * SC + wq].broadcast_to([8, wq]), "ei_i")
        vop(lambda e, wq=wq: e.tensor_scalar(
            oh_s[:, 0:wq], eidx_s[:, 0:wq], ecolf[:, 0:1], None,
            ALU.is_equal),
            [eidx_s[:, 0:wq], ecolf[:]], [oh_s[:, 0:wq]], "oh")
        vop(lambda e, wq=wq: e.tensor_tensor_scan(
            cs_s[:, 0:wq], oh_s[:, 0:wq], zz_s[:, 0:wq], carry8[:, 0:1],
            ALU.add, ALU.add),
            [oh_s[:, 0:wq], zz_s[:, 0:wq], carry8[:]], [cs_s[:, 0:wq]],
            "cscan")
        vop(lambda e, wq=wq: e.tensor_copy(carry8[:], cs_s[:, wq - 1:wq]),
            [cs_s[:, 0:wq]], [carry8[:]], "csc")

    for q in range(NSC):
        wq = min(SC, 2 * L - q * SC)
        slot_chunk_oh_cs(q, wq)
        dma("gpsimd", cs_d[:, q * SC:q * SC + wq], cs_s[:, 0:wq], "cs_o")
    cnt_s = SR("cnt_s", [8, 8])
    cntall = SR("cntall", [8, 8])
    offm = SR("offm", [8, 8])
    offcol = SR("offcol", [8, 1])
    dma("gpsimd", meta_d[0, 0:8].rearrange("(a b) -> a b", a=8),
        carry8[:], "cn_o")
    dma("sync", cnt_s[:], meta_d[0:1, 0:8].broadcast_to([8, 8]), "cn_i")
    vop(lambda e: e.tensor_scalar(cnt_s[:], cnt_s[:], r1hot[:, 0:1], None,
                                  ALU.mult),
        [cnt_s[:], r1hot[:]], [cnt_s[:]], "cn_m")
    dma("gpsimd", cc_src, cnt_s[:], "cc_o")
    P.op("gpsimd", lambda e: e.collective_compute(
        "AllReduce", ALU.add, replica_groups=[list(range(n_cores))],
        ins=[cc_src.opt()], outs=[cc_dst.opt()]),
        [cc_src], [cc_dst], compl=("cc", "gpsimd"), tag="ccar")
    dma("gpsimd", cntall[:], cc_dst, "cc_i")
    vop(lambda e: e.tensor_scalar(offm[:], cntall[:], pmask[:, 0:1], None,
                                  ALU.mult),
        [cntall[:], pmask[:]], [offm[:]], "of_m")
    top(lambda e: e.matmul(pBC[0:8, 0:1], offm[:], ones_f[0:8, 0:1],
                           start=True, stop=True),
        [offm[:], ones_f[:]], [pBC[:, 0:1]], "of_s")
    vop(lambda e: e.tensor_copy(offcol[:], pBC[0:8, 0:1]),
        [pBC[:, 0:1]], [offcol[:]], "of_e")
    selrow = SR("selrow", [1, SC])
    keep_r = SR("keep_r", [1, SC])
    lpos_r = SR("lpos_r", [1, SC])
    gidx_r = SR("gidx_r", [1, SC])
    sidx_r = SR("sidx_r", [1, SC])
    for q in range(NSC):
        wq = min(SC, 2 * L - q * SC)
        dma("sync", eidx_s[:, 0:wq],
            slot_d[0:1, q * SC:q * SC + wq].broadcast_to([8, wq]), "ei_i2")
        vop(lambda e, wq=wq: e.tensor_scalar(
            oh_s[:, 0:wq], eidx_s[:, 0:wq], ecolf[:, 0:1], None,
            ALU.is_equal),
            [eidx_s[:, 0:wq], ecolf[:]], [oh_s[:, 0:wq]], "oh2")
        dma("sync", cs_s[:, 0:wq], cs_d[:, q * SC:q * SC + wq], "cs_i")
        vop(lambda e, wq=wq: e.tensor_scalar(
            tmp8[:, 0:wq], cs_s[:, 0:wq], offcol[:, 0:1], None, ALU.add),
            [cs_s[:, 0:wq], offcol[:]], [tmp8[:, 0:wq]], "sl_a")
        vop(lambda e, wq=wq: e.tensor_tensor(
            tmp8[:, 0:wq], tmp8[:, 0:wq], oh_s[:, 0:wq], ALU.mult),
            [tmp8[:, 0:wq], oh_s[:, 0:wq]], [tmp8[:, 0:wq]], "sl_m")
        for j in range((wq + 511) // 512):
            w = min(512, wq - j * 512)
            top(lambda e, j=j, w=w: e.matmul(
                pStat[0:1, 0:w], ones_f[0:8, 0:1],
                tmp8[:, j * 512:j * 512 + w], start=True, stop=True),
                [ones_f[:], tmp8[:, 0:wq]], [pStat[:, :]], "sl_r")
            vop(lambda e, j=j, w=w: e.tensor_copy(
                selrow[:, j * 512:j * 512 + w], pStat[0:1, 0:w]),
                [pStat[:, :]], [selrow[:]], "sl_e")
        vop(lambda e, wq=wq: e.tensor_scalar(
            keep_r[:, 0:wq], selrow[:, 0:wq], CAP + 0.5, None, ALU.is_lt),
            [selrow[:, 0:wq]], [keep_r[:, 0:wq]], "kp")
        vop(lambda e, wq=wq: e.tensor_scalar(
            lpos_r[:, 0:wq], selrow[:, 0:wq], -1.0, None, ALU.add),
            [selrow[:, 0:wq]], [lpos_r[:, 0:wq]], "lp")
        vop(lambda e, wq=wq: e.tensor_tensor(
            lpos_r[:, 0:wq], lpos_r[:, 0:wq], keep_r[:, 0:wq], ALU.mult),
            [lpos_r[:, 0:wq], keep_r[:, 0:wq]], [lpos_r[:, 0:wq]], "lpk")
        vop(lambda e, wq=wq: e.scalar_tensor_tensor(
            gidx_r[:, 0:wq], eidx_s[0:1, 0:wq], float(CAP),
            lpos_r[:, 0:wq], ALU.mult, ALU.add),
            [eidx_s[:, 0:wq], lpos_r[:, 0:wq]], [gidx_r[:, 0:wq]], "gx")
        vop(lambda e, wq=wq: e.tensor_scalar(
            sidx_r[:, 0:wq], keep_r[:, 0:wq], -1e7, 1e7, ALU.mult, ALU.add),
            [keep_r[:, 0:wq]], [sidx_r[:, 0:wq]], "sx1")
        vop(lambda e, wq=wq: e.tensor_tensor(
            sidx_r[:, 0:wq], sidx_r[:, 0:wq], gidx_r[:, 0:wq], ALU.add),
            [sidx_r[:, 0:wq], gidx_r[:, 0:wq]], [sidx_r[:, 0:wq]], "sx2")
        dma("gpsimd", slot_d[1, q * SC:q * SC + wq], gidx_r[0:1, 0:wq], "gx_o")
        dma("gpsimd", slot_d[2, q * SC:q * SC + wq], sidx_r[0:1, 0:wq], "sx_o")
        dma("gpsimd", slot_d[3, q * SC:q * SC + wq], keep_r[0:1, 0:wq], "kp_o")
    keep_t = SR("keep_t", [128, NTILE, 2])
    dma("sync", keep_t[:],
        slot_d[3, :].rearrange("(a p b) -> p a b", p=128, b=2), "kp_i")
    vop(lambda e: e.tensor_tensor(g_t[:].rearrange("p a b -> p (a b)"),
                                  g_t[:].rearrange("p a b -> p (a b)"),
                                  keep_t[:].rearrange("p a b -> p (a b)"),
                                  ALU.mult),
        [g_t[:], keep_t[:]], [g_t[:]], "gk")
    sidx_tf = SR("sidx_tf", [128, NTILE, 2])
    sidx_t = SR("sidx_t", [128, NTILE, 2], I32)
    dma("sync", sidx_tf[:],
        slot_d[2, :].rearrange("(a p b) -> p a b", p=128, b=2), "sx_i")
    vop(lambda e: e.tensor_copy(sidx_t[:].rearrange("p a b -> p (a b)"),
                                sidx_tf[:].rearrange("p a b -> p (a b)")),
        [sidx_tf[:]], [sidx_t[:]], "sx_c")
    rowb = SR("rowb", [128, DM])
    rowbf = SR("rowbf", [128, DM], BF16)
    bnd_reg = moe.enter_context(nc.gpsimd.register("bnd_reg"))
    gop(lambda e: e.reg_mov(bnd_reg, E * CAP - 1), [], [], "bmov")
    for i in range(NTILE):
        dma("sync", rowb[:], xf_d[i * 128:(i + 1) * 128, :], "xf_i")
        vop(lambda e: e.tensor_copy(rowbf[:], rowb[:]),
            [rowb[:]], [rowbf[:]], "sc_b")
        for k in range(2):
            idma_scatter(slab_d, sidx_t[:, i, k:k + 1], rowbf[:],
                         bnd_reg, "scat")

    P.op("gpsimd", lambda e: e.collective_compute(
        "ReduceScatter", ALU.add, replica_groups=[list(range(n_cores))],
        ins=[slab_d.opt()], outs=[shard_d.opt()]),
        [slab_d], [shard_d], compl=("cc", "rs"), tag="ccrs")
    rt.close()

    ffn = ExitStack()
    SF = lambda name, shape, dtype=F32: ffn.enter_context(
        nc.sbuf_tensor(name, shape, dtype))
    xfT_e = SF("xfT_e", [128, NDM, CAP], BF16)
    hrc_s = SF("hrc_s", [128, NH, 512], BF16)
    ob_sc = SF("ob_sc", [128, NDM, 512])
    rowbB = SF("rowbB", [128, DM], BF16)
    orow_st = SF("orow_st", [128, DM], BF16)
    w1r = SF("w1r", [128, NH, NDM * 128], BF16)
    wm2b = SF("wm2b", [128, 2, NH * 128], BF16)
    NRC = (CAP + 511) // 512
    for m in range(NH):
        dma4(w1r[:, m, :], w1_bf[m, :, :], NDM * 128, "w1l",
             xreads=[slab_d])
    for r in range(CAP // 128):
        dma("sync", rowbB[:], shard_d[r * 128:(r + 1) * 128, :], "sb_i")
        for d in range(NDM):
            top(lambda e, d=d: e.transpose(
                pT.bitcast(BF16)[:, 0:128],
                rowbB[:, d * 128:(d + 1) * 128], ident_bf[:]),
                [rowbB[:], ident_bf[:]], [pT[:, :]], "sb_t")
            vop(lambda e, d=d, r=r: e.tensor_copy(
                xfT_e[:, d, r * 128:(r + 1) * 128],
                pT.bitcast(BF16)[:, 0:128]),
                [pT[:, :]], [xfT_e[:]], "sb_e")
    for rc in range(NRC):
        w = min(512, CAP - rc * 512)
        for mg in range(NH // 8):
            for mm in range(8):
                m = mg * 8 + mm
                wslab = w1r[:, m, :]
                pj = (pMM, pMM2, pMM3)[m % 3]
                for c in range(NDM):
                    top(lambda e, c=c, pj=pj, ws=wslab, rc=rc, w=w:
                        e.matmul(
                            pj[:, 0:w], ws[:, c * 128:(c + 1) * 128],
                            xfT_e[:, c, rc * 512:rc * 512 + w],
                            start=(c == 0), stop=(c == NDM - 1)),
                        [wslab, xfT_e[:]], [pj[:, 0:w]], "w1m")
                sop(lambda e, m=m, pj=pj, w=w: e.activation(
                    hrc_s[:, m, 0:w], pj[:, 0:w], AF.Relu,
                    bias=b1c[:, m:m + 1]),
                    [pj[:, 0:w], b1c[:]], [hrc_s[:, m, 0:w]], "rel")
        for m in range(NDM):
            par = m % 2
            wslab = wm2b[:, par, 0:NH * 128]
            dma4(wslab, w2_bf[m, :, :], NH * 128, "w2l",
                 xreads=[slab_d])
            pj = pMM if m % 2 == 0 else pMM2
            for c in range(NH):
                top(lambda e, c=c, pj=pj, ws=wslab, w=w: e.matmul(
                    pj[:, 0:w], ws[:, c * 128:(c + 1) * 128],
                    hrc_s[:, c, 0:w], start=(c == 0), stop=(c == NH - 1)),
                    [wslab, hrc_s[:]], [pj[:, 0:w]], "w2m")
            sop(lambda e, m=m, pj=pj, w=w: e.activation(
                ob_sc[:, m, 0:w], pj[:, 0:w], AF.Identity,
                bias=b2c[:, m:m + 1]),
                [pj[:, 0:w], b2c[:]], [ob_sc[:, m, 0:w]], "b2a")
        nsub = (w + 127) // 128
        for s2 in range(nsub):
            ww = min(128, w - s2 * 128)
            for d in range(NDM):
                top(lambda e, d=d, s2=s2, ww=ww: e.transpose(
                    pT[0:ww, :], ob_sc[:, d, s2 * 128:s2 * 128 + ww],
                    ident_f[:]),
                    [ob_sc[:], ident_f[:]], [pT[:, :]], "ob_t")
                vop(lambda e, d=d, ww=ww: e.tensor_copy(
                    orow_st[0:ww, d * 128:(d + 1) * 128], pT[0:ww, :]),
                    [pT[:, :]], [orow_st[:, :]], "ob_e")
            r0 = rc * 512 + s2 * 128
            dma("gpsimd", eout_my[r0:r0 + ww, :], orow_st[0:ww, :], "ob_o")

    P.op("gpsimd", lambda e: e.collective_compute(
        "AllGather", ALU.bypass, replica_groups=[list(range(n_cores))],
        ins=[eout_my.opt()], outs=[eout_d.opt()]),
        [eout_my], [eout_d], compl=("cc", "ag"), tag="ccag")
    ffn.close()

    gidx_tf = SM("gidx_tf", [128, NTILE, 2])
    gidx_t = SM("gidx_t", [128, NTILE, 2], I32)
    dma("sync", gidx_tf[:],
        slot_d[1, :].rearrange("(a p b) -> p a b", p=128, b=2), "gx_i")
    vop(lambda e: e.tensor_copy(gidx_t[:].rearrange("p a b -> p (a b)"),
                                gidx_tf[:].rearrange("p a b -> p (a b)")),
        [gidx_tf[:]], [gidx_t[:]], "gx_c")
    r0b = SM("r0b", [128, DM], BF16)
    r1b = SM("r1b", [128, DM], BF16)
    xfb = SM("xfb", [128, DM])
    ycb = SM("ycb", [128, DM])
    osq = SM("osq", [128, DM])
    msum = SM("msum", [128, 1])
    vsum = SM("vsum", [128, 1])
    m2c = SM("m2c", [128, 1])
    for i in range(NTILE):
        idma_gather(r0b[:], eout_d, gidx_t[:, i, 0:1], "ga0")
        idma_gather(r1b[:], eout_d, gidx_t[:, i, 1:2], "ga1")
        dma("sync", xfb[:], xf_d[i * 128:(i + 1) * 128, :], "xf_i2")
        vop(lambda e, i=i: e.tensor_scalar(
            ycb[:], r0b[:], g_t[:].rearrange("p a b -> p a b")[:, i, 0:1],
            None, ALU.mult),
            [r0b[:], g_t[:]], [ycb[:]], "cb0")
        vop(lambda e, i=i: e.scalar_tensor_tensor(
            ycb[:], r1b[:], g_t[:].rearrange("p a b -> p a b")[:, i, 1:2],
            ycb[:], ALU.mult, ALU.add),
            [r1b[:], g_t[:], ycb[:]], [ycb[:]], "cb1")
        vop(lambda e: e.tensor_add(ycb[:], ycb[:], xfb[:]),
            [ycb[:], xfb[:]], [ycb[:]], "cb2")
        vop(lambda e: e.tensor_reduce(msum[:], ycb[:], AX.X, ALU.add),
            [ycb[:]], [msum[:]], "l2s")
        vop(lambda e: e.scalar_tensor_tensor(osq[:], ycb[:], 1.0, ycb[:],
                                             ALU.bypass, ALU.mult,
                                             accum_out=vsum[:]),
            [ycb[:]], [osq[:], vsum[:]], "l2q")
        vop(lambda e: e.tensor_scalar(msum[:], msum[:], 1.0 / DM, None,
                                      ALU.mult),
            [msum[:]], [msum[:]], "l2m")
        vop(lambda e: e.scalar_tensor_tensor(m2c[:], msum[:], 1.0, msum[:],
                                             ALU.bypass, ALU.mult),
            [msum[:]], [m2c[:]], "l2m2")
        vop(lambda e: e.scalar_tensor_tensor(vsum[:], vsum[:], 1.0 / DM,
                                             m2c[:], ALU.mult, ALU.subtract),
            [vsum[:], m2c[:]], [vsum[:]], "l2v")
        sop(lambda e: e.activation(vsum[:], vsum[:], AF.Ln, bias=1e-5),
            [vsum[:]], [vsum[:]], "l2l")
        sop(lambda e: e.activation(vsum[:], vsum[:], AF.Exp, scale=-0.5),
            [vsum[:]], [vsum[:]], "l2r")
        vop(lambda e: e.tensor_scalar(ycb[:], ycb[:], msum[:, 0:1], None,
                                      ALU.subtract),
            [ycb[:], msum[:]], [ycb[:]], "l2c")
        vop(lambda e: e.tensor_scalar(ycb[:], ycb[:], vsum[:, 0:1], None,
                                      ALU.mult),
            [ycb[:], vsum[:]], [ycb[:]], "l2f")
        dma("gpsimd", out[i * 128:(i + 1) * 128, :], ycb[:], "f_o")

    P.emit()
    moe.close()
    persist.close()
    return nc



def make_in_map(cfg, core, xb, w):
    E = cfg.E
    r1 = np.zeros((8, 1), np.float32)
    r1[core, 0] = 1.0
    pm = np.zeros((8, 1), np.float32)
    pm[:core, 0] = 1.0
    DI, NST, NDT, NDM, NH = cfg.DI, cfg.NST, cfg.NDT, cfg.NDM, cfg.NH
    return {
        "x_t": np.ascontiguousarray(xb.T),
        "norm_w_t": np.ascontiguousarray(
            w["norm_w"].reshape(NDM, 128).T),
        "w_inT": np.ascontiguousarray(w["W_in"].T),
        "conv_w_t": np.ascontiguousarray(
            w["conv_w"].reshape(NDT, 128, 4).transpose(1, 0, 2)),
        "conv_b_t": np.ascontiguousarray(w["conv_b"].reshape(NDT, 128).T),
        "w_dtT": np.ascontiguousarray(w["W_dt"].T),
        "b_dt_t": np.ascontiguousarray(w["b_dt"].reshape(NDT, 128).T),
        "w_BT": np.ascontiguousarray(w["W_B"].T),
        "w_CT": np.ascontiguousarray(w["W_C"].T),
        "A_log_t": np.ascontiguousarray(w["A_log"].reshape(NDT, 128, NST)),
        "D_t": np.ascontiguousarray(w["D_param"].reshape(NDT, 128).T),
        "w_outT": np.ascontiguousarray(w["W_out"].T),
        "res_scale": np.ascontiguousarray(
            w["residual_scale"].reshape(1, 1)),
        "w_router": np.ascontiguousarray(w["w_router"]),
        "w1": np.ascontiguousarray(w["w1"][core]),
        "b1_t": np.ascontiguousarray(w["b1"][core].reshape(NH, 128).T),
        "w2": np.ascontiguousarray(w["w2"][core]),
        "b2_t": np.ascontiguousarray(w["b2"][core].reshape(NDM, 128).T),
        "rank1hot": r1,
        "prefmask": pm,
    }


def kernel(**inputs):
    cfg = CFG()
    w = {
        "norm_w": inputs["L0_attn_norm_w"], "W_in": inputs["L0_attn_W_in"],
        "conv_w": inputs["L0_attn_conv_w"][:, 0, :],
        "conv_b": inputs["L0_attn_conv_b"], "W_dt": inputs["L0_attn_W_dt"],
        "b_dt": inputs["L0_attn_b_dt"], "W_B": inputs["L0_attn_W_B"],
        "W_C": inputs["L0_attn_W_C"], "A_log": inputs["L0_attn_A_log"],
        "D_param": inputs["L0_attn_D_param"], "W_out": inputs["L0_attn_W_out"],
        "residual_scale": np.asarray(inputs["L0_attn_residual_scale"]),
        "w_router": inputs["L0_ffn_w_router"], "w1": inputs["L0_ffn_w1"],
        "b1": inputs["L0_ffn_b1"], "w2": inputs["L0_ffn_w2"],
        "b2": inputs["L0_ffn_b2"],
    }
    x = inputs["L0_attn_x"]
    n_cores = x.shape[0]
    nc = build_program(cfg, n_cores=n_cores)
    in_maps = [make_in_map(cfg, c, x[c], w) for c in range(n_cores)]
    res = run_bass_kernel_spmd(nc, in_maps, core_ids=list(range(n_cores)))
    outs = [res.results[c]["out"] for c in range(n_cores)]
    return np.stack(outs, axis=0)
